# version 29
# baseline (speedup 1.0000x reference)
import numpy as np
from contextlib import ExitStack

import concourse.bass as bass
import concourse.mybir as mybir
from concourse.bass_utils import run_bass_kernel_spmd

F32 = mybir.dt.float32
BF16 = mybir.dt.bfloat16
I32 = mybir.dt.int32
AF = mybir.ActivationFunctionType
ALU = mybir.AluOpType
AX = mybir.AxisListType.X

D = 2048
T = 1024
TO = 1024
HC = D // 128
NM = 16384
NCH = NM // 128
DH = 128
NG = 4
HPG = 4
DKV = 512
NHEAD = 16
LN_EPS = 1e-5
RMS_EPS = 1e-6
GELU_C = 0.7978845608028654
GELU_A = 0.044715
NEG = -1e30



def _pair_perm(n):
    out = np.empty(n, np.int64)
    for bb in range(n // 256):
        out[bb * 256: bb * 256 + 128] = bb * 256 + 2 * np.arange(128)
        out[bb * 256 + 128: bb * 256 + 256] = bb * 256 + 2 * np.arange(128) + 1
    return out


PI = _pair_perm(D)
SIGMA = _pair_perm(NM)


def _slot_tokens(par):
    return np.concatenate([np.arange(s * 128, (s + 1) * 128)
                           for s in range(par, 16, 2)])


def _rope_tables(tok):
    pos = tok.astype(np.float64)[None, :]
    j = np.arange(64, dtype=np.float64)[:, None]
    ang = np.power(10000.0, -j / 64.0)
    rad = (pos * ang).astype(np.float32)
    return np.cos(rad).astype(np.float32), np.sin(rad).astype(np.float32)


def _host_inputs(inp):
    x = np.asarray(inp["L0_attn_x"], np.float32)
    wq = np.asarray(inp["L0_attn_wq"], np.float32)
    wk = np.asarray(inp["L0_attn_wk"], np.float32)
    wv = np.asarray(inp["L0_attn_wv"], np.float32)
    wo = np.asarray(inp["L0_attn_wo"], np.float32)
    gq = np.asarray(inp["L0_attn_gq"], np.float32)
    gk = np.asarray(inp["L0_attn_gk"], np.float32)
    gp = np.asarray(inp["L0_ffn_gate_W_packed"], np.int32)
    gs = np.asarray(inp["L0_ffn_gate_scales"], np.float32)
    up = np.asarray(inp["L0_ffn_up_W_packed"], np.int32)
    us = np.asarray(inp["L0_ffn_up_scales"], np.float32)
    dp = np.asarray(inp["L0_ffn_down_W_packed"], np.int32)
    ds = np.asarray(inp["L0_ffn_down_scales"], np.float32)

    C = np.ascontiguousarray
    import ml_dtypes
    gpT = gp[SIGMA, :].T
    gsT = gs[SIGMA, :].T
    upT = up[SIGMA, :].T
    usT = us[SIGMA, :].T
    dpT = dp[PI, :].T
    dsT = ds[PI, :].T
    shared = {
        "wq": C(wq[PI, :]),
        "wk": C(wk[PI, :]),
        "wv": C(wv[PI, :]),
        "wo": C(wo[:, PI]),
        "gqt": C(np.transpose(gq, (2, 0, 1)).reshape(128, 16)),
        "gkt": C(gk.T),
        "ident": np.eye(128, dtype=ml_dtypes.bfloat16),
        "ones_col": np.ones((128, 1), np.float32),
        "ones_row": np.ones((1, 128), np.float32),
        "ones_row_bf": np.ones((1, 128), dtype=ml_dtypes.bfloat16),
    }
    em = np.zeros((8, 128), np.float32)
    for p in range(128):
        em[p // 16, p] = 1.0
    shared["emat"] = -em
    shared["emat_h"] = -0.5 * em
    tri = np.where(np.arange(128)[None, :] > np.arange(128)[:, None],
                   np.float32(NEG), np.float32(0.0))
    sw = np.zeros((128, 128), dtype=ml_dtypes.bfloat16)
    for i in range(128):
        sw[(i + 64) % 128, i] = 1
    shared["sswap"] = sw

    maps = []
    for c in range(8):
        p, par = c // 2, c % 2
        tok = _slot_tokens(par)
        m = dict(shared)
        if par == 0:
            m["mask_a"] = C(tri)
            m["mask_b"] = np.full((128, 128), NEG, np.float32)
        else:
            m["mask_a"] = np.zeros((128, 128), np.float32)
            m["mask_b"] = C(tri)
        m["x"] = C(x[p].T[PI][:, tok])
        m["gpT"] = C(gpT[:, c * 2048:(c + 1) * 2048])
        m["gsT"] = C(gsT[:, c * 2048:(c + 1) * 2048])
        m["upT"] = C(upT[:, c * 2048:(c + 1) * 2048])
        m["usT"] = C(usT[:, c * 2048:(c + 1) * 2048])
        m["dpT"] = C(dpT[c * 1024:(c + 1) * 1024, :])
        m["dsT"] = C(dsT[c * 64:(c + 1) * 64, :])
        cos, sin = _rope_tables(tok)
        m["rcos"] = np.vstack([cos, cos])
        m["rsin"] = np.vstack([-sin, sin])
        maps.append(m)
    return maps


def _host_output(results):
    out = np.empty((4, 2048, 2048), np.float32)
    inv = np.empty(D, np.int64)
    inv[PI] = np.arange(D)
    for p in range(4):
        full = np.empty((2048, 2048), np.float32)
        for par in range(2):
            y = np.asarray(results[2 * p + par]["y"])
            for j in range(8):
                s = 2 * j + par
                full[:, s * 128:(s + 1) * 128] = y[:, j * 128:(j + 1) * 128]
        out[p] = full[inv].T
    return out



class Build:
    def __init__(self, nc, es, block):
        self.nc = nc
        self.es = es
        self.block = block
        self.cnt = {}
        self.sems = {}
        self.phase = 0
        self.nuid = 0

    def uid(self):
        self.nuid += 1
        return self.nuid

    def sem(self, name):
        if name not in self.sems:
            self.sems[name] = self.es.enter_context(self.nc.semaphore(name))
            self.cnt[name] = 0
        return self.sems[name]

    def inc(self, instr, name, amt):
        instr.then_inc(self.sems[name], amt)
        self.cnt[name] += amt
        return self.cnt[name]

    def dma(self, q, out, in_, name):
        return self.inc(q.dma_start(out=out, in_=in_), name, 16)

    def barrier(self):
        self.sem("PH")
        dma_names = [n for n in ("dX", "dA0", "dA1", "dB0", "dB1", "dG0", "dG1", "dS0", "dS1")
                     if n in self.sems]
        targets = {n: self.cnt[n] for n in dma_names}

        def _sync(eng):
            for n in dma_names:
                eng.wait_ge(self.sems[n], targets[n])
            self.inc(eng.nop(), "PH", 1)
        self.block.sync(_sync)

        for name in ("scalar", "vector", "gpsimd", "tensor"):
            def _mk(eng):
                self_inc = eng.nop().then_inc(self.sems["PH"], 1)
            getattr(self.block, name)(_mk)
            self.cnt["PH"] += 1
        tgt = self.cnt["PH"]

        def _wait(eng):
            eng.wait_ge(self.sems["PH"], tgt)
        for name in ("sync", "scalar", "vector", "gpsimd", "tensor"):
            getattr(self.block, name)(_wait)
        self.phase += 1



def _emit_dequant(b, io):
    nc, block = b.nc, b.block
    es = ExitStack()
    sb = lambda nm, shape, dt=F32: es.enter_context(nc.sbuf_tensor(nm, shape, dt))
    pk = [sb(f"dq_pk{i}", [128, 2048], I32) for i in range(2)]
    sT = [sb(f"dq_sT{i}", [8, 2048]) for i in range(2)]
    scl = [sb(f"dq_scl{i}", [128, 2048]) for i in range(2)]
    loi = [sb(f"dq_loi{i}", [128, 2048], I32) for i in range(2)]
    hii = [sb(f"dq_hii{i}", [128, 2048], I32) for i in range(2)]
    cf = [sb(f"dq_cf{i}", [128, 2048]) for i in range(4)]
    m8 = sb("dq_m8", [128, 2048])
    mt = sb("dq_mt", [128, 2048])
    t4h = sb("dq_t4h", [128, 2048])
    uuh = [sb(f"dq_uh{i}", [128, 2048]) for i in range(2)]
    t6 = [sb(f"dq_t6{i}", [128, 2048]) for i in range(2)]
    sgs = [sb(f"dq_sgs{i}", [128, 2048], BF16) for i in range(2)]
    mg = sb("dq_mg", [128, 2048])
    nb75 = sb("dq_n75", [128, 1])
    val = [sb(f"dq_val{i}", [128, 2048], BF16) for i in range(2)]
    PS = io["PS"]
    dq = io["dq_loc"]

    for s_ in ("dA0", "dA1", "dS0", "dS1", "PE", "EV", "VE", "PL", "CC"):
        b.sem(s_)
    ab = [b.cnt["dA0"], b.cnt["dA1"]]
    ob = [b.cnt["dS0"], b.cnt["dS1"]]
    pe0 = b.cnt["PE"]; ev0 = b.cnt["EV"]; ve0 = b.cnt["VE"]; pl0 = b.cnt["PL"]

    steps = []
    for kind, pT, sTa, ema in (("g", io["gpT"], io["gsT"], io["s_emat"]),
                               ("u", io["upT"], io["usT"], io["s_emat"]),
                               ("d", io["dpT"], io["dsT"], io["s_emat_h"])):
        for c2 in range(8):
            steps.append((kind, pT, sTa, ema, c2))
    NS = len(steps)

    wA = lambda i: ab[i % 2] + 32 * (i // 2 + 1)
    pe = lambda i: pe0 + i + 1
    evac = lambda i: ev0 + 5 * i + 1
    cvt = lambda i, j: ev0 + 5 * i + 2 + j
    evsg = lambda i, j: ev0 + 5 * i + 4 + j
    ve_setup = ve0 + 1
    ve_ext = [0] * NS; ve_nib = [0] * (2 * NS)
    vec = ve_setup
    vec += 1; ve_ext[0] = vec
    for i in range(NS):
        if i + 1 < NS:
            vec += 1; ve_ext[i + 1] = vec
        for j in range(2):
            vec += 1; ve_nib[2 * i + j] = vec
    ext = lambda i: ve_ext[i]
    nib = lambda g: ve_nib[g]
    plmg = lambda g: pl0 + 3 * g + 1
    plsg = lambda g: pl0 + 3 * g + 2
    plval = lambda g: pl0 + 3 * g + 3
    stc = [0] * (2 * NS)
    acc2 = [ob[0], ob[1]]
    for g in range(2 * NS):
        acc2[g % 2] += 16
        stc[g] = acc2[g % 2]

    def _stores(q, g):
        i, j = g // 2, g % 2
        kind, _, _, _, c2 = steps[i]
        kk = 2 * c2 + j
        base = {"g": 0, "u": 2048, "d": 4096}[kind]
        b.dma(q, dq[base + kk * 128: base + (kk + 1) * 128, :], val[j][:, :], "dS%d" % j)

    def f_sync(q):
        for i, (kind, pT, sTa, ema, c2) in enumerate(steps):
            if i >= 2:
                q.wait_ge(b.sems["VE"], ext(i - 2))
                q.wait_ge(b.sems["PE"], pe(i - 2))
            b.dma(q, pk[i % 2][:, :], pT[c2 * 128:(c2 + 1) * 128, :], "dA%d" % (i % 2))
            b.dma(q, sT[i % 2][:, :], sTa[c2 * 8:(c2 + 1) * 8, :], "dA%d" % (i % 2))
            if i >= 2:
                for j in range(2):
                    g = 2 * (i - 2) + j
                    q.wait_ge(b.sems["PL"], plval(g))
                    _stores(q, g)
        for g in (2 * NS - 4, 2 * NS - 3, 2 * NS - 2, 2 * NS - 1):
            q.wait_ge(b.sems["PL"], plval(g))
            _stores(q, g)
    block.sync(f_sync)

    def f_pe(t):
        for i in range(NS):
            ema = steps[i][3]
            t.wait_ge(b.sems["dA%d" % (i % 2)], wA(i))
            if i >= 2:
                t.wait_ge(b.sems["EV"], evac(i - 2))
            for s4 in range(4):
                ins = t.matmul(PS[4 * (i % 2) + s4][:, :], ema[:, :],
                               sT[i % 2][:, s4 * 512:(s4 + 1) * 512], start=True, stop=True)
            b.inc(ins, "PE", 1)
    block.tensor(f_pe)

    def f_act(s):
        for i in range(NS):
            s.wait_ge(b.sems["PE"], pe(i))
            if i >= 2:
                s.wait_ge(b.sems["PL"], plval(2 * (i - 2) + 1))
            for s4 in range(4):
                ins = s.copy(scl[i % 2][:, s4 * 512:(s4 + 1) * 512], PS[4 * (i % 2) + s4][:, :])
            b.inc(ins, "EV", 1)
            s.wait_ge(b.sems["VE"], ext(i))
            for j in range(2):
                if i >= 2:
                    s.wait_ge(b.sems["VE"], nib(2 * (i - 2) + j))
                b.inc(s.copy(cf[2 * (i % 2) + j][:, :],
                             (loi if j == 0 else hii)[i % 2][:, :]), "EV", 1)
            for j in range(2):
                if i == 0 and j == 0:
                    s.wait_ge(b.sems["VE"], ve_setup)
                if i >= 1:
                    s.wait_ge(b.sems["PL"], plsg(2 * (i - 1) + j))
                b.inc(s.activation(sgs[j][:, :], cf[2 * (i % 2) + j][:, :], AF.Sign,
                                   bias=nb75[:, :]), "EV", 1)
    block.scalar(f_act)

    def f_dve(v):
        b.inc(v.tensor_scalar(nb75[:, :], io["s_ones_col"][:, :], -7.5, None, ALU.mult),
              "VE", 1)
        def _ext(i):
            v.wait_ge(b.sems["dA%d" % (i % 2)], wA(i))
            if i >= 2:
                v.wait_ge(b.sems["EV"], cvt(i - 2, 1))
            v.tensor_scalar(loi[i % 2][:, :], pk[i % 2][:, :], 15, None, ALU.bitwise_and)
            b.inc(v.tensor_scalar(hii[i % 2][:, :], pk[i % 2][:, :], 4, None,
                                  ALU.logical_shift_right), "VE", 1)
        _ext(0)
        for i in range(NS):
            if i + 1 < NS:
                _ext(i + 1)
            for j in range(2):
                g = 2 * i + j
                cfs = cf[2 * (i % 2) + j]
                v.wait_ge(b.sems["EV"], cvt(i, j))
                if g >= 2:
                    v.wait_ge(b.sems["PL"], plmg(g - 2))
                v.tensor_scalar(m8[:, :], cfs[:, :], 8.0, 8.0, ALU.is_ge, ALU.mult)
                v.tensor_tensor(mt[:, :], cfs[:, :], m8[:, :], ALU.subtract)
                v.tensor_scalar(t4h[:, :], mt[:, :], 4.0, 0.5, ALU.max, ALU.mult)
                v.tensor_scalar(m8[:, :], mt[:, :], 0.5, None, ALU.mult)
                v.tensor_tensor(uuh[j][:, :], m8[:, :], t4h[:, :], ALU.add)
                b.inc(v.tensor_scalar(t6[j][:, :], mt[:, :], 6.0, -8.0, ALU.max, ALU.add),
                      "VE", 1)
    block.vector(f_dve)

    def f_pool(gq):
        for i in range(NS):
            for j in range(2):
                g = 2 * i + j
                gq.wait_ge(b.sems["VE"], nib(g))
                b.inc(gq.tensor_tensor(mg[:, :], uuh[j][:, :], t6[j][:, :], ALU.add), "PL", 1)
                gq.wait_ge(b.sems["EV"], evsg(i, j))
                b.inc(gq.tensor_tensor(mg[:, :], mg[:, :], sgs[j][:, :], ALU.mult), "PL", 1)
                gq.wait_ge(b.sems["EV"], evac(i))
                if g >= 2:
                    gq.wait_ge(b.sems["dS%d" % j], stc[g - 2])
                b.inc(gq.tensor_tensor(val[j][:, :], mg[:, :], scl[i % 2][:, :], ALU.mult),
                      "PL", 1)
            if i % 4 == 3:
                fam = "gud"[i // 8]
                h2 = (i // 4) % 2
                gq.wait_ge(b.sems["dS0"], stc[2 * i])
                gq.wait_ge(b.sems["dS1"], stc[2 * i + 1])
                base = {"g": 0, "u": 2048, "d": 4096}[fam]
                src_ap = io["dq_loc"][base + h2 * 1024: base + (h2 + 1) * 1024, :]
                gq.collective_compute(
                    "AllGather", ALU.bypass, replica_groups=[list(range(8))],
                    ins=[src_ap.opt()], outs=[io["dqg_%s%d" % (fam, h2)].opt()],
                ).then_inc(b.sems["CC"])
                b.cnt["CC"] += 1
    block.gpsimd(f_pool)

    b.barrier()
    es.close()



def _rstd(v, s, row_v, row_out, row_tmp, eps, width):
    raise NotImplementedError



def _emit_ln_dram(b, io, src_ap, dst_ap, ncols, eps):
    nc, block = b.nc, b.block
    PS = io["PS"]
    es = ExitStack()
    u = b.uid()
    sb = lambda nm, shape, dt=F32: es.enter_context(nc.sbuf_tensor(f"{nm}_{u}", shape, dt))
    nsp = ncols // 512
    xa = [sb(f"ln_x{i}", [128, ncols]) for i in range(2)]
    sq = sb("ln_sq", [128, ncols])
    m_row = sb("ln_m", [1, ncols])
    s_row = sb("ln_s", [1, ncols])
    t_row = sb("ln_t", [1, ncols])
    m_bc = sb("ln_mbc", [128, ncols])
    r_bc = sb("ln_rbc", [128, ncols])
    mh = sb("ln_mh", [1, ncols], BF16)
    ml = sb("ln_ml", [1, ncols], BF16)
    rh = sb("ln_rh", [1, ncols], BF16)
    rl = sb("ln_rl", [1, ncols], BF16)
    xb = [sb(f"ln_xb{i}", [128, ncols], BF16) for i in range(2)]

    for s_ in ("dA0", "dA1", "dS0", "dS1", "PE", "EV", "VE"):
        b.sem(s_)
    wb = [b.cnt["dA0"], b.cnt["dA1"]]
    sb0 = [b.cnt["dS0"], b.cnt["dS1"]]
    pe0 = b.cnt["PE"]; ev0 = b.cnt["EV"]; ve0 = b.cnt["VE"]
    wv = lambda c: wb[c % 2] + 16 * (c // 2 + 1)
    wv2 = lambda c: wb[c % 2] + 16 * (HC // 2) + 16 * (c // 2 + 1)
    sv = lambda c: sb0[c % 2] + 16 * (c // 2 + 1)

    def f_sync(q):
        for c in range(HC):
            if c >= 2:
                q.wait_ge(b.sems["PE"], pe0 + (c - 1))
            b.dma(q, xa[c % 2][:, :], src_ap[c * 128:(c + 1) * 128, 0:ncols], "dA%d" % (c % 2))
    block.sync(f_sync)

    def f_act(s):
        for c in range(HC):
            s.wait_ge(b.sems["dA%d" % (c % 2)], wv(c))
            if c >= 1:
                s.wait_ge(b.sems["PE"], pe0 + c)
            b.inc(s.activation(sq[:, :], xa[c % 2][:, :], AF.Square), "EV", 1)
    block.scalar(f_act)

    def f_pe(t):
        for c in range(HC):
            t.wait_ge(b.sems["dA%d" % (c % 2)], wv(c))
            t.wait_ge(b.sems["EV"], ev0 + c + 1)
            for s4 in range(nsp):
                t.matmul(PS[s4][0:1, :], io["s_ones_col"][:, :],
                         xa[c % 2][:, s4 * 512:(s4 + 1) * 512],
                         start=(c == 0), stop=(c == HC - 1))
                ins = t.matmul(PS[4 + s4 % 4][0:1, :] if nsp == 4 else PS[4 + s4][0:1, :],
                               io["s_ones_col"][:, :],
                               sq[:, s4 * 512:(s4 + 1) * 512],
                               start=(c == 0), stop=(c == HC - 1))
            b.inc(ins, "PE", 1)
    block.tensor(f_pe)
    pe_stats = b.cnt["PE"]

    def f_fin_v(v):
        v.wait_ge(b.sems["PE"], pe_stats)
        for s4 in range(nsp):
            v.tensor_scalar(m_row[:, s4 * 512:(s4 + 1) * 512], PS[s4][0:1, :], 1.0 / D, None, ALU.mult)
            v.tensor_scalar(s_row[:, s4 * 512:(s4 + 1) * 512], PS[4 + s4][0:1, :], 1.0 / D, None, ALU.mult)
        v.tensor_tensor(t_row[:, :], m_row[:, :], m_row[:, :], ALU.mult)
        v.tensor_tensor(s_row[:, :], s_row[:, :], t_row[:, :], ALU.subtract)
        b.inc(v.tensor_scalar(s_row[:, :], s_row[:, :], eps, None, ALU.add), "VE", 1)
    block.vector(f_fin_v)
    ve_var = b.cnt["VE"]

    def f_fin_s(s):
        s.wait_ge(b.sems["VE"], ve_var)
        b.inc(s.activation(t_row[:, :], s_row[:, :], AF.Sqrt), "EV", 1)
    block.scalar(f_fin_s)
    ev_sqrt = b.cnt["EV"]

    def f_fin_v2(v):
        v.wait_ge(b.sems["EV"], ev_sqrt)
        v.reciprocal(t_row[:, :], t_row[:, :])
        v.tensor_tensor(sq[0:1, 0:ncols], t_row[:, :], t_row[:, :], ALU.mult)
        v.tensor_tensor(sq[0:1, 0:ncols], sq[0:1, 0:ncols], s_row[:, :], ALU.mult)
        v.tensor_scalar(sq[0:1, 0:ncols], sq[0:1, 0:ncols], -0.5, 1.5, ALU.mult, ALU.add)
        v.tensor_tensor(t_row[:, :], t_row[:, :], sq[0:1, 0:ncols], ALU.mult)
        v.tensor_copy(mh[:, :], m_row[:, :])
        v.tensor_tensor(sq[0:1, 0:ncols], m_row[:, :], mh[:, :], ALU.subtract)
        v.tensor_copy(ml[:, :], sq[0:1, 0:ncols])
        v.tensor_copy(rh[:, :], t_row[:, :])
        v.tensor_tensor(sq[0:1, 0:ncols], t_row[:, :], rh[:, :], ALU.subtract)
        b.inc(v.tensor_copy(rl[:, :], sq[0:1, 0:ncols]), "VE", 1)
    block.vector(f_fin_v2)
    ve_rstd = b.cnt["VE"]

    def f_bc_pe(t):
        t.wait_ge(b.sems["VE"], ve_rstd)
        for s4 in range(nsp):
            if s4 >= 2:
                t.wait_ge(b.sems["EV"], ev_sqrt + 2 * (s4 - 2) + 2)
            sl = slice(s4 * 512, (s4 + 1) * 512)
            t.matmul(PS[s4 % 2][:, :], io["s_onesbf"][:, :], mh[:, sl], start=True, stop=False)
            b.inc(t.matmul(PS[s4 % 2][:, :], io["s_onesbf"][:, :], ml[:, sl],
                           start=False, stop=True), "PE", 1)
            t.wait_ge(b.sems["EV"], ev_sqrt + 2 * s4 + 1)
            t.matmul(PS[s4 % 2][:, :], io["s_onesbf"][:, :], rh[:, sl], start=True, stop=False)
            b.inc(t.matmul(PS[s4 % 2][:, :], io["s_onesbf"][:, :], rl[:, sl],
                           start=False, stop=True), "PE", 1)
    block.tensor(f_bc_pe)

    def f_bc_act(s):
        for s4 in range(nsp):
            s.wait_ge(b.sems["PE"], pe_stats + 2 * s4 + 1)
            b.inc(s.copy(m_bc[:, s4 * 512:(s4 + 1) * 512], PS[s4 % 2][:, :]), "EV", 1)
            s.wait_ge(b.sems["PE"], pe_stats + 2 * s4 + 2)
            b.inc(s.copy(r_bc[:, s4 * 512:(s4 + 1) * 512], PS[s4 % 2][:, :]), "EV", 1)
    block.scalar(f_bc_act)
    ev_bc = b.cnt["EV"]

    def f_sync2(q):
        for c in range(HC):
            if c < 2:
                q.wait_ge(b.sems["PE"], pe_stats)
            else:
                q.wait_ge(b.sems["VE"], ve_rstd + (c - 1))
            b.dma(q, xa[c % 2][:, :], src_ap[c * 128:(c + 1) * 128, 0:ncols], "dA%d" % (c % 2))
    block.sync(f_sync2)

    def f_norm(v):
        v.wait_ge(b.sems["EV"], ev_bc)
        for c in range(HC):
            v.wait_ge(b.sems["dA%d" % (c % 2)], wv2(c))
            if c >= 2:
                v.wait_ge(b.sems["dS%d" % (c % 2)], sv(c - 2))
            v.tensor_tensor(xa[c % 2][:, :], xa[c % 2][:, :], m_bc[:, :], ALU.subtract)
            b.inc(v.tensor_tensor(xb[c % 2][:, :], xa[c % 2][:, :], r_bc[:, :], ALU.mult), "VE", 1)
    block.vector(f_norm)

    def f_store(s):
        for c in range(HC):
            s.wait_ge(b.sems["VE"], ve_rstd + c + 1)
            b.dma(s, dst_ap[c * 128:(c + 1) * 128, 0:ncols], xb[c % 2][:, :], "dS%d" % (c % 2))
        if "dbg_mbc" in io and ncols == T:
            b.dma(s, io["dbg_mbc"][:, :], m_bc[:, :], "dS0")
            b.dma(s, io["dbg_rbc"][:, :], r_bc[:, :], "dS0")
    block.scalar(f_store)

    b.barrier()
    es.close()



def _emit_ffn_tblock(b, io, ar, xn_src, acc_init_src):
    nc, block = b.nc, b.block
    PS = io["PS"]
    xn, acc = ar["xn"], ar["acc"]
    inter = ar["inter"]
    gbuf, ubuf, dbuf = ar["gbuf"], ar["ubuf"], ar["dbuf"]
    gf, uf = ar["gf"], ar["uf"]
    At, Bt, Ct = ar["At"], ar["Bt"], ar["Ct"]

    NDB = len(dbuf)
    for s_ in ("dX", "dC", "dA0", "dA1", "dB0", "dB1", "GU", "DN", "EV", "VE", "PL", "DA", "CC"):
        b.sem(s_)
    gu0 = b.cnt["GU"]; dn0 = b.cnt["DN"]
    ev0 = b.cnt["EV"]; ve0 = b.cnt["VE"]; pl0 = b.cnt["PL"]; da0 = b.cnt["DA"]
    ab = [b.cnt["dA0"], b.cnt["dA1"]]
    bb = [b.cnt["dB0"], b.cnt["dB1"]]
    x0c = b.cnt["dX"]
    c0c = b.cnt["dC"]

    pre = x0c + 16 * (16 if xn_src is not None else 0)
    prec = c0c + 16 * (16 if acc_init_src is not None else 0)
    wA = lambda s, k: ab[(16 * s + k) % 2] + 64 * ((16 * s + k) // 2 + 1)
    wB = lambda s, q, k: bb[(64 * s + 16 * q + k) % 2] + 16 * ((64 * s + 16 * q + k) // 2 + 1)
    evv = lambda i, kk: ev0 + 2 * i + kk
    vev = lambda i, kk: ve0 + 2 * i + kk
    plv = lambda i, kk: pl0 + 3 * i + kk
    dnv = lambda s, q, k: dn0 + 64 * s + 16 * q + k + 1
    dav = lambda s, q, g8: da0 + 32 * s + 8 * q + g8 + 1

    def f_sync(q):
        q.wait_ge(b.sems["CC"], 4)
        for s in range(8):
            for k in range(16):
                i = 16 * s + k
                if i >= 2:
                    q.wait_ge(b.sems["GU"], gu0 + i - 1)
                for h2 in range(2):
                    b.dma(q, gbuf[i % 2][:, h2 * 1024:(h2 + 1) * 1024]
                          .rearrange("p (k c) -> p k c", k=8),
                          io["sgap"](i, h2), "dA%d" % (i % 2))
                    b.dma(q, ubuf[i % 2][:, h2 * 1024:(h2 + 1) * 1024]
                          .rearrange("p (k c) -> p k c", k=8),
                          io["suap"](i, h2), "dA%d" % (i % 2))
            if s == 0:
                q.wait_ge(b.sems["CC"], 6)
            for qq in range(4):
                for k in range(16):
                    i = 16 * s + k
                    j = 64 * s + 16 * qq + k
                    if j >= NDB:
                        q.wait_ge(b.sems["DN"], dn0 + j - (NDB - 1))
                    b.dma(q, dbuf[j % NDB][:, :], io["sdap"](i, qq), "dB%d" % (j % 2))
    block.sync(f_sync)

    def f_pe(t):
        for s in range(8):
            for k in range(16):
                i = 16 * s + k
                t.wait_ge(b.sems["dA%d" % (i % 2)], wA(s, k))
                if i == 0:
                    t.wait_ge(b.sems["dX"], pre)
                if i >= 2:
                    t.wait_ge(b.sems["EV"], evv(i - 2, 1))
                if s >= 1 and k < 2:
                    t.wait_ge(b.sems["DA"], da0 + 32 * s)
                pb = 4 * (i % 2)
                for which, buf, bank0 in (("g", gbuf[i % 2], pb), ("u", ubuf[i % 2], pb + 2)):
                    for kk in range(HC):
                        for sp in range(2):
                            ins = t.matmul(PS[bank0 + sp][:, :], buf[:, kk * 128:(kk + 1) * 128],
                                           xn[:, kk * 1024 + sp * 512: kk * 1024 + (sp + 1) * 512],
                                           start=(kk == 0), stop=(kk == HC - 1))
                b.inc(ins, "GU", 1)
            t.wait_ge(b.sems["EV"], evv(16 * s + 15, 1))
            for qq in range(4):
                if not (s == 0 and qq == 0):
                    t.wait_ge(b.sems["DA"], da0 + 32 * s + 8 * qq)
                for k in range(16):
                    j = 64 * s + 16 * qq + k
                    if qq == 0:
                        t.wait_ge(b.sems["VE"], vev(16 * s + k, 2))
                    t.wait_ge(b.sems["dB%d" % (j % 2)], wB(s, qq, k))
                    for g8 in range(8):
                        h4, sp = g8 // 2, g8 % 2
                        ins = t.matmul(PS[g8][:, :], dbuf[j % NDB][:, h4 * 128:(h4 + 1) * 128],
                                       inter[:, k * 1024 + sp * 512: k * 1024 + (sp + 1) * 512],
                                       start=(k == 0), stop=(k == 15))
                    b.inc(ins, "DN", 1)
    block.tensor(f_pe)

    def f_act(s):
        if xn_src is not None:
            for c in range(HC):
                b.dma(s, xn[:, c * 1024:(c + 1) * 1024], xn_src[c * 128:(c + 1) * 128, :], "dX")
        for ss in range(8):
            for k in range(16):
                i = 16 * ss + k
                s.wait_ge(b.sems["GU"], gu0 + i + 1)
                if i >= 1:
                    s.wait_ge(b.sems["PL"], plv(i - 1, 3))
                pb = 4 * (i % 2)
                s.copy(gf[:, 0:512], PS[pb][:, :])
                s.copy(gf[:, 512:1024], PS[pb + 1][:, :])
                s.copy(uf[:, 0:512], PS[pb + 2][:, :])
                b.inc(s.copy(uf[:, 512:1024], PS[pb + 3][:, :]), "EV", 1)
                s.wait_ge(b.sems["PL"], plv(i, 2))
                b.inc(s.activation(At[:, :], Ct[:, :], AF.Tanh), "EV", 1)
    block.scalar(f_act)

    def f_pool(g):
        if acc_init_src is not None:
            for c in range(HC):
                b.dma(g, acc[:, c * 1024:(c + 1) * 1024], acc_init_src[c * 128:(c + 1) * 128, :], "dC")
        for ss in range(8):
            for k in range(16):
                i = 16 * ss + k
                g.wait_ge(b.sems["EV"], evv(i, 1))
                if i >= 1:
                    g.wait_ge(b.sems["VE"], vev(i - 1, 2))
                b.inc(g.tensor_tensor(At[:, :], gf[:, :], gf[:, :], ALU.mult), "PL", 1)
                g.wait_ge(b.sems["VE"], vev(i, 1))
                b.inc(g.tensor_tensor(Ct[:, :], gf[:, :], Bt[:, :], ALU.mult), "PL", 1)
                g.wait_ge(b.sems["EV"], evv(i, 2))
                b.inc(g.tensor_tensor(Ct[:, :], gf[:, :], uf[:, :], ALU.mult), "PL", 1)
    block.gpsimd(f_pool)

    def f_dve(v):
        for ss in range(8):
            for k in range(16):
                i = 16 * ss + k
                v.wait_ge(b.sems["PL"], plv(i, 1))
                b.inc(v.tensor_scalar(Bt[:, :], At[:, :], GELU_C * GELU_A, GELU_C, ALU.mult, ALU.add),
                      "VE", 1)
                v.wait_ge(b.sems["EV"], evv(i, 2))
                v.tensor_scalar(Bt[:, :], At[:, :], 0.5, 0.5, ALU.mult, ALU.add)
                v.wait_ge(b.sems["PL"], plv(i, 3))
                if ss >= 1:
                    v.wait_ge(b.sems["DN"], dn0 + 64 * ss)
                b.inc(v.tensor_tensor(inter[:, i % 16 * 1024:(i % 16 + 1) * 1024],
                                      Ct[:, :], Bt[:, :], ALU.mult), "VE", 1)
            if acc_init_src is not None and ss == 0:
                v.wait_ge(b.sems["dC"], prec)
            for qq in range(4):
                v.wait_ge(b.sems["DN"], dn0 + 64 * ss + 16 * (qq + 1))
                for g8 in range(8):
                    h4, sp = g8 // 2, g8 % 2
                    col = (4 * qq + h4) * 1024 + sp * 512
                    b.inc(v.tensor_tensor(acc[:, col:col + 512], acc[:, col:col + 512],
                                          PS[g8][:, :], ALU.add), "DA", 1)
    block.vector(f_dve)

    b.barrier()



def _emit_ln_sbuf(b, io, ar, src, dst_ap, r1o_ap, eps):
    nc, block = b.nc, b.block
    PS = io["PS"]
    es = ExitStack()
    u = b.uid()
    sb = lambda nm, shape, dt=F32: es.enter_context(nc.sbuf_tensor(f"{nm}_{u}", shape, dt))
    sq4 = [sb(f"l2_sq{i}", [128, 4096]) for i in range(2)]
    m_row = sb("l2_m", [1, 1024])
    s_row = sb("l2_s", [1, 1024])
    t_row = sb("l2_t", [1, 1024])
    w_row = sb("l2_w", [1, 1024])
    m_bc = sb("l2_mbc", [128, 4096])
    r_bc = sb("l2_rbc", [128, 4096])
    mh = sb("l2_mh", [1, 1024], BF16)
    ml = sb("l2_ml", [1, 1024], BF16)
    rh = sb("l2_rh", [1, 1024], BF16)
    rl = sb("l2_rl", [1, 1024], BF16)
    xb = [sb(f"l2_xb{i}", [128, 4096], BF16) for i in range(2)]

    for s_ in ("dS0", "dS1", "PE", "EV", "VE"):
        b.sem(s_)
    pe0 = b.cnt["PE"]; ev0 = b.cnt["EV"]; ve0 = b.cnt["VE"]
    sb0 = [b.cnt["dS0"], b.cnt["dS1"]]
    per_st = 8 if r1o_ap is not None else 4
    sv = lambda qi: sb0[qi % 2] + 16 * per_st * (qi // 2 + 1)

    def f_act(s):
        for qi in range(4):
            if qi >= 2:
                s.wait_ge(b.sems["PE"], pe0 + qi - 1)
            b.inc(s.activation(sq4[qi % 2][:, :], src[:, qi * 4096:(qi + 1) * 4096],
                               AF.Square), "EV", 1)
        s.wait_ge(b.sems["VE"], ve0 + 1)
        b.inc(s.activation(t_row[:, :], s_row[:, :], AF.Sqrt), "EV", 1)
        for which in range(2):
            s.wait_ge(b.sems["PE"], pe0 + 5 + which)
            dstb = m_bc if which == 0 else r_bc
            for t in range(8):
                ins = s.copy(dstb[:, t * 512:(t + 1) * 512], PS[4 + t % 2][:, :])
            b.inc(ins, "EV", 1)
    block.scalar(f_act)

    def f_pe(t_):
        for qi in range(4):
            t_.wait_ge(b.sems["EV"], ev0 + qi + 1)
            ins = None
            for st in range(8):
                c, half = st // 2, st % 2
                sl = slice(c * 1024 + half * 512, c * 1024 + (half + 1) * 512)
                t_.matmul(PS[half][0:1, :], io["s_ones_col"][:, :],
                          src[:, qi * 4096 + c * 1024 + half * 512:
                              qi * 4096 + c * 1024 + (half + 1) * 512],
                          start=(qi == 0 and c == 0), stop=(qi == 3 and c == 3))
                ins = t_.matmul(PS[2 + half][0:1, :], io["s_ones_col"][:, :],
                                sq4[qi % 2][:, c * 1024 + half * 512:
                                            c * 1024 + (half + 1) * 512],
                                start=(qi == 0 and c == 0), stop=(qi == 3 and c == 3))
            b.inc(ins, "PE", 1)
        t_.wait_ge(b.sems["VE"], ve0 + 2)
        for half in range(2):
            sl = slice(half * 512, (half + 1) * 512)
            t_.matmul(PS[4 + half][:, :], io["s_onesbf"][:, :], mh[:, sl], start=True, stop=False)
            ins = t_.matmul(PS[4 + half][:, :], io["s_onesbf"][:, :], ml[:, sl],
                            start=False, stop=True)
        b.inc(ins, "PE", 1)
        t_.wait_ge(b.sems["EV"], ev0 + 6)
        for half in range(2):
            sl = slice(half * 512, (half + 1) * 512)
            t_.matmul(PS[4 + half][:, :], io["s_onesbf"][:, :], rh[:, sl], start=True, stop=False)
            ins = t_.matmul(PS[4 + half][:, :], io["s_onesbf"][:, :], rl[:, sl],
                            start=False, stop=True)
        b.inc(ins, "PE", 1)
    block.tensor(f_pe)

    def f_dve(v):
        v.wait_ge(b.sems["PE"], pe0 + 4)
        for half in range(2):
            sl = slice(half * 512, (half + 1) * 512)
            v.tensor_scalar(m_row[:, sl], PS[half][0:1, :], 1.0 / D, None, ALU.mult)
            v.tensor_scalar(s_row[:, sl], PS[2 + half][0:1, :], 1.0 / D, None, ALU.mult)
        v.tensor_tensor(t_row[:, :], m_row[:, :], m_row[:, :], ALU.mult)
        v.tensor_tensor(s_row[:, :], s_row[:, :], t_row[:, :], ALU.subtract)
        b.inc(v.tensor_scalar(s_row[:, :], s_row[:, :], eps, None, ALU.add), "VE", 1)
        v.wait_ge(b.sems["EV"], ev0 + 5)
        v.reciprocal(t_row[:, :], t_row[:, :])
        v.tensor_tensor(w_row[:, :], t_row[:, :], t_row[:, :], ALU.mult)
        v.tensor_tensor(w_row[:, :], w_row[:, :], s_row[:, :], ALU.mult)
        v.tensor_scalar(w_row[:, :], w_row[:, :], -0.5, 1.5, ALU.mult, ALU.add)
        v.tensor_tensor(t_row[:, :], t_row[:, :], w_row[:, :], ALU.mult)
        v.tensor_copy(mh[:, :], m_row[:, :])
        v.tensor_tensor(w_row[:, :], m_row[:, :], mh[:, :], ALU.subtract)
        v.tensor_copy(ml[:, :], w_row[:, :])
        v.tensor_copy(rh[:, :], t_row[:, :])
        v.tensor_tensor(w_row[:, :], t_row[:, :], rh[:, :], ALU.subtract)
        b.inc(v.tensor_copy(rl[:, :], w_row[:, :]), "VE", 1)
        v.wait_ge(b.sems["EV"], ev0 + 7)
        for qi in range(4):
            if qi >= 2:
                v.wait_ge(b.sems["dS%d" % (qi % 2)], sv(qi - 2))
            v.tensor_tensor(sq4[qi % 2][:, :], src[:, qi * 4096:(qi + 1) * 4096],
                            m_bc[:, :], ALU.subtract)
            b.inc(v.tensor_tensor(xb[qi % 2][:, :], sq4[qi % 2][:, :], r_bc[:, :],
                                  ALU.mult), "VE", 1)
    block.vector(f_dve)

    def f_store(s):
        for qi in range(4):
            s.wait_ge(b.sems["VE"], ve0 + 3 + qi)
            for c in range(4):
                ch = qi * 4 + c
                b.dma(s, dst_ap[ch * 128:(ch + 1) * 128, :],
                      xb[qi % 2][:, c * 1024:(c + 1) * 1024], "dS%d" % (qi % 2))
                if r1o_ap is not None:
                    b.dma(s, r1o_ap[ch * 128:(ch + 1) * 128, :],
                          src[:, ch * 1024:(ch + 1) * 1024], "dS%d" % (qi % 2))
    block.sync(f_store)

    b.barrier()
    es.close()



def _emit_proj(b, io, K_own, V_own, Qr_sb, xn2_sb):
    nc, block = b.nc, b.block
    PS = io["PS"]
    es = ExitStack()
    u = b.uid()
    sb = lambda nm, shape, dt=BF16: es.enter_context(nc.sbuf_tensor(f"{nm}_{u}", shape, dt))
    wkb = [sb(f"wkb{i}", [128, 512]) for i in range(2)]
    wvb = [sb(f"wvb{i}", [128, 512]) for i in range(2)]
    wqb = [sb(f"wqb{i}", [128, 2048]) for i in range(2)]

    for s_ in ("dX", "dG0", "dG1", "PE", "EV"):
        b.sem(s_)
    pe0 = b.cnt["PE"]; ev0 = b.cnt["EV"]
    gb0 = [b.cnt["dG0"], b.cnt["dG1"]]
    gv = lambda li: gb0[li % 2] + 16 * (li // 2 + 1)

    def f_sync(q):
        for c in range(HC):
            b.dma(q, xn2_sb[:, c * 1024:(c + 1) * 1024], io["s_xn2"][c * 128:(c + 1) * 128, :], "dX")
    block.sync(f_sync)
    w_xn = b.cnt["dX"]

    jobs = [("k", 0), ("v", 0), ("q", 0), ("q", 1), ("q", 2), ("q", 3)]

    def f_gload(g):
        for jo, (kind, oc) in enumerate(jobs):
            for h in range(HC):
                li = jo * HC + h
                if li >= 2:
                    g.wait_ge(b.sems["PE"], pe0 + li - 1)
                if kind == "k":
                    b.dma(g, wkb[h % 2][:, :], io["wk"][h * 128:(h + 1) * 128, :], "dG%d" % (li % 2))
                elif kind == "v":
                    b.dma(g, wvb[h % 2][:, :], io["wv"][h * 128:(h + 1) * 128, :], "dG%d" % (li % 2))
                else:
                    b.dma(g, wqb[h % 2][:, :], io["wq"][h * 128:(h + 1) * 128, :], "dG%d" % (li % 2))
    block.gpsimd(f_gload)

    def f_pe(t):
        t.wait_ge(b.sems["dX"], w_xn)
        for jo, (kind, oc) in enumerate(jobs):
            if jo >= 1:
                t.wait_ge(b.sems["EV"], ev0 + 8 * jo)
            for h in range(HC):
                li = jo * HC + h
                t.wait_ge(b.sems["dG%d" % (li % 2)], gv(li))
                ins = None
                for g8 in range(8):
                    if kind == "k":
                        gg, sp = g8 // 2, g8 % 2
                        lhsT = wkb[h % 2][:, gg * 128:(gg + 1) * 128]
                        rhs = xn2_sb[:, h * 1024 + sp * 512: h * 1024 + (sp + 1) * 512]
                    elif kind == "v":
                        tc = g8
                        lhsT = xn2_sb[:, h * 1024 + tc * 128: h * 1024 + (tc + 1) * 128]
                        rhs = wvb[h % 2][:, :]
                    else:
                        dq, sp = (oc * 8 + g8) // 2, (oc * 8 + g8) % 2
                        lhsT = wqb[h % 2][:, dq * 128:(dq + 1) * 128]
                        rhs = xn2_sb[:, h * 1024 + sp * 512: h * 1024 + (sp + 1) * 512]
                    ins = t.matmul(PS[g8][:, :], lhsT, rhs, start=(h == 0), stop=(h == HC - 1))
                b.inc(ins, "PE", 1)
    block.tensor(f_pe)

    def f_act(s):
        for jo, (kind, oc) in enumerate(jobs):
            s.wait_ge(b.sems["PE"], pe0 + HC * (jo + 1))
            for g8 in range(8):
                if kind == "k":
                    gg, sp = g8 // 2, g8 % 2
                    dst = K_own[:, gg * 1024 + sp * 512: gg * 1024 + (sp + 1) * 512]
                elif kind == "v":
                    tc = g8
                    dst = V_own[:, tc * 512:(tc + 1) * 512]
                else:
                    dq, sp = (oc * 8 + g8) // 2, (oc * 8 + g8) % 2
                    dst = Qr_sb[:, dq * 1024 + sp * 512: dq * 1024 + (sp + 1) * 512]
                b.inc(s.copy(dst, PS[g8][:, :]), "EV", 1)
    block.scalar(f_act)

    b.barrier()
    es.close()



def _emit_rms_rope(b, io, K_own, Qr_sb, Qn_sb):
    nc, block = b.nc, b.block
    PS = io["PS"]
    es = ExitStack()
    u = b.uid()
    sbf = lambda nm, shape, dt=F32: es.enter_context(nc.sbuf_tensor(f"{nm}_{u}", shape, dt))
    sqf = sbf("rr_sq", [128, 4096])
    tmp1 = sbf("rr_t1", [128, 4096])
    t_row = sqf[0:1, :]
    w_row = tmp1[0:1, :]
    rh = sbf("rr_rh", [1, 4096], BF16)
    rl = sbf("rr_rl", [1, 4096], BF16)
    rbc = sbf("rr_rbc", [128, 4096])
    s_row = rbc[0:1, :]
    qb = sbf("rr_qb", [128, 4096], BF16)

    for s_ in ("PE", "EV", "VE"):
        b.sem(s_)
    pe0 = b.cnt["PE"]; ev0 = b.cnt["EV"]; ve0 = b.cnt["VE"]
    units = [("k", 0)] + [("q", bq) for bq in range(4)]
    evA = lambda i: ev0 + 4 * i + 1
    evB = lambda i: ev0 + 4 * i + 2
    evC1 = lambda i: ev0 + 4 * i + 3
    evC2 = lambda i: ev0 + 4 * i + 4
    peS1 = lambda i: pe0 + 6 * i + 1
    peS2 = lambda i: pe0 + 6 * i + 2
    peB1 = lambda i: pe0 + 6 * i + 3
    peB2 = lambda i: pe0 + 6 * i + 4
    peC1 = lambda i: pe0 + 6 * i + 5
    peC2 = lambda i: pe0 + 6 * i + 6
    veR1 = lambda i: ve0 + 6 * i + 1
    veA = lambda i: ve0 + 6 * i + 2
    veB = lambda i: ve0 + 6 * i + 3
    veC = lambda i: ve0 + 6 * i + 4
    veD1 = lambda i: ve0 + 6 * i + 5
    veD2 = lambda i: ve0 + 6 * i + 6

    def usrc(kind, idx):
        return K_own[:, :] if kind == "k" else Qr_sb[:, idx * 4096:(idx + 1) * 4096]

    def udst(kind, idx):
        return K_own[:, :] if kind == "k" else Qn_sb[:, idx * 4096:(idx + 1) * 4096]

    def f_act(s):
        for ui, (kind, idx) in enumerate(units):
            if ui >= 1:
                s.wait_ge(b.sems["PE"], peS2(ui - 1))
                s.wait_ge(b.sems["VE"], veB(ui - 1))
            b.inc(s.activation(sqf[:, :], usrc(kind, idx), AF.Square), "EV", 1)
            s.wait_ge(b.sems["VE"], veA(ui))
            b.inc(s.activation(t_row[:, :], s_row[:, :], AF.Sqrt), "EV", 1)
            for rnd in range(2):
                s.wait_ge(b.sems["PE"], peB1(ui) if rnd == 0 else peB2(ui))
                q4 = 4 * (ui % 2)
                for t in range(4):
                    ins = s.copy(rbc[:, (4 * rnd + t) * 512:(4 * rnd + t + 1) * 512],
                                 PS[q4 + t][:, :])
                b.inc(ins, "EV", 1)
    block.scalar(f_act)

    def f_pe(t_):
        for ui, (kind, idx) in enumerate(units):
            q4 = 4 * (ui % 2)
            t_.wait_ge(b.sems["EV"], evA(ui))
            if ui >= 2:
                t_.wait_ge(b.sems["VE"], veD2(ui - 2))
            for rnd in range(2):
                if rnd == 1:
                    t_.wait_ge(b.sems["VE"], veR1(ui))
                ins = None
                for t in range(4):
                    ins = t_.matmul(PS[q4 + t][0:1, :], io["s_ones_col"][:, :],
                                    sqf[:, (4 * rnd + t) * 512:(4 * rnd + t + 1) * 512],
                                    start=True, stop=True)
                b.inc(ins, "PE", 1)
            t_.wait_ge(b.sems["VE"], veB(ui))
            for rnd in range(2):
                if rnd == 1:
                    t_.wait_ge(b.sems["EV"], evC1(ui))
                for t in range(4):
                    sl = slice((4 * rnd + t) * 512, (4 * rnd + t + 1) * 512)
                    t_.matmul(PS[q4 + t][:, :], io["s_onesbf"][:, :], rh[:, sl],
                              start=True, stop=False)
                    ins = t_.matmul(PS[q4 + t][:, :], io["s_onesbf"][:, :], rl[:, sl],
                                    start=False, stop=True)
                b.inc(ins, "PE", 1)
            t_.wait_ge(b.sems["VE"], veC(ui))
            t_.wait_ge(b.sems["EV"], evC2(ui))
            for rnd in range(2):
                if rnd == 1:
                    t_.wait_ge(b.sems["VE"], veD1(ui))
                ins = None
                for t in range(4):
                    ins = t_.matmul(PS[q4 + t][:, :], io["s_sswap"][:, :],
                                    qb[:, (4 * rnd + t) * 512:(4 * rnd + t + 1) * 512],
                                    start=True, stop=True)
                b.inc(ins, "PE", 1)
    block.tensor(f_pe)

    def f_dve(v):
        for ui, (kind, idx) in enumerate(units):
            q4 = 4 * (ui % 2)
            for rnd in range(2):
                v.wait_ge(b.sems["PE"], peS1(ui) if rnd == 0 else peS2(ui))
                ins = None
                for t in range(4):
                    ins = v.tensor_copy(s_row[:, (4 * rnd + t) * 512:(4 * rnd + t + 1) * 512],
                                        PS[q4 + t][0:1, :])
                if rnd == 0:
                    b.inc(ins, "VE", 1)
            b.inc(v.tensor_scalar(s_row[:, :], s_row[:, :], 1.0 / DH, RMS_EPS,
                                  ALU.mult, ALU.add), "VE", 1)
            v.wait_ge(b.sems["EV"], evB(ui))
            v.reciprocal(t_row[:, :], t_row[:, :])
            v.tensor_tensor(w_row[:, :], t_row[:, :], t_row[:, :], ALU.mult)
            v.tensor_tensor(w_row[:, :], w_row[:, :], s_row[:, :], ALU.mult)
            v.tensor_scalar(w_row[:, :], w_row[:, :], -0.5, 1.5, ALU.mult, ALU.add)
            v.tensor_tensor(t_row[:, :], t_row[:, :], w_row[:, :], ALU.mult)
            if kind == "q":
                v.tensor_scalar(t_row[:, :], t_row[:, :], DH ** -0.5, None, ALU.mult)
            v.tensor_copy(rh[:, :], t_row[:, :])
            v.tensor_tensor(w_row[:, :], t_row[:, :], rh[:, :], ALU.subtract)
            b.inc(v.tensor_copy(rl[:, :], w_row[:, :]), "VE", 1)
            v.wait_ge(b.sems["EV"], evC2(ui))
            v.tensor_tensor(tmp1[:, :], usrc(kind, idx), rbc[:, :], ALU.mult)
            ins = None
            for sub in range(4):
                gcol = io["s_gkt"][:, sub:sub + 1] if kind == "k" else \
                    io["s_gqt"][:, idx * 4 + sub: idx * 4 + sub + 1]
                ins = v.tensor_scalar(qb[:, sub * 1024:(sub + 1) * 1024],
                                      tmp1[:, sub * 1024:(sub + 1) * 1024], gcol, None, ALU.mult)
            b.inc(ins, "VE", 1)
            dst = udst(kind, idx)
            for rnd in range(2):
                v.wait_ge(b.sems["PE"], peC1(ui) if rnd == 0 else peC2(ui))
                ins = None
                for t in range(4):
                    sl = slice((4 * rnd + t) * 512, (4 * rnd + t + 1) * 512)
                    tsl = slice(((4 * rnd + t) % 2) * 512, (((4 * rnd + t) % 2) + 1) * 512)
                    v.tensor_tensor(dst[:, sl], qb[:, sl], io["s_rcos"][:, tsl], ALU.mult)
                    v.tensor_tensor(tmp1[:, sl], PS[q4 + t][:, :], io["s_rsin"][:, tsl], ALU.mult)
                    ins = v.tensor_tensor(dst[:, sl], dst[:, sl], tmp1[:, sl], ALU.add)
                b.inc(ins, "VE", 1)
    block.vector(f_dve)

    es.close()
    return veD2(0)



def _emit_kv_exchange(b, io, K_own, V_own, K_sb, V_sb, k_done):
    nc, block = b.nc, b.block
    for s_ in ("dX", "dS0", "CK", "VE"):
        b.sem(s_)

    def f_sync(q):
        q.wait_ge(b.sems["VE"], k_done)
        for g in range(NG):
            b.dma(q, io["kv_loc"][g * 128:(g + 1) * 128, :],
                  K_own[:, g * 1024:(g + 1) * 1024], "dS0")
        for tc in range(8):
            b.dma(q, io["kv_loc"][512 + tc * 128: 512 + (tc + 1) * 128, 0:512],
                  V_own[:, tc * 512:(tc + 1) * 512], "dS0")
    block.sync(f_sync)
    st = b.cnt["dS0"]

    def f_pool(gq):
        gq.wait_ge(b.sems["dS0"], st)
        gq.collective_compute(
            "AllGather", ALU.bypass,
            replica_groups=[[0, 1], [2, 3], [4, 5], [6, 7]],
            ins=[io["kv_loc"].opt()], outs=[io["kv_glob"].opt()],
        ).then_inc(b.sems["CK"])
        b.cnt["CK"] = max(b.cnt["CK"], 1)
        gq.wait_ge(b.sems["CK"], 1)
        for g in range(NG):
            for s in range(16):
                m, ls = s % 2, s // 2
                b.dma(gq, K_sb[:, g * 2048 + s * 128: g * 2048 + (s + 1) * 128],
                      io["kv_glob"][m * 1536 + g * 128: m * 1536 + (g + 1) * 128,
                                    ls * 128:(ls + 1) * 128], "dX")
        for s in range(16):
            m, ls = s % 2, s // 2
            b.dma(gq, V_sb[:, s * 512:(s + 1) * 512],
                  io["kv_glob"][m * 1536 + 512 + ls * 128: m * 1536 + 512 + (ls + 1) * 128,
                                0:512], "dX")
    block.gpsimd(f_pool)

    b.barrier()



def _emit_attn_wo(b, io, K_sb, V_sb, Qn_sb, res2):
    nc, block = b.nc, b.block
    PS = io["PS"]
    PT6 = PS[6][:, :].bitcast(BF16)
    PT7 = PS[7][:, :].bitcast(BF16)
    es = ExitStack()
    u = b.uid()
    sbf = lambda nm, shape, dt=F32: es.enter_context(nc.sbuf_tensor(f"{nm}_{u}", shape, dt))
    sc_sb = [sbf(f"at_sc{i}", [128, 2048]) for i in range(2)]
    prb = [sbf(f"at_prb{i}", [128, 2048], BF16) for i in range(2)]
    prb2 = [sbf(f"at_prb2{i}", [128, 2048], BF16) for i in range(2)]
    pt_sb = [sbf(f"at_pt{i}", [128, 2048], BF16) for i in range(2)]
    nm_c = [sbf(f"at_nm{i}", [128, 1]) for i in range(2)]
    ssum = [sbf(f"at_ss{i}", [128, 1]) for i in range(2)]
    rs_c = [sbf(f"at_rs{i}", [128, 1]) for i in range(2)]
    wob = [sbf(f"at_wo{i}", [128, 2048], BF16) for i in range(2)]

    for s_ in ("dX", "dG0", "dG1", "PE", "EV", "VE", "TR"):
        b.sem(s_)
    pe0 = b.cnt["PE"]; ev0 = b.cnt["EV"]; ve0 = b.cnt["VE"]; tr0 = b.cnt["TR"]

    its = []
    for g in range(NG):
        for h4 in range(HPG):
            for j in range(8):
                its.append((g, g * HPG + h4, j))

    NT = len(its)
    pe_sc = [0] * NT; pe_pv = [0] * NT; tr_b = [0] * NT
    pec, trc = pe0, tr0
    for tau in range(NT + 1):
        if tau < NT:
            pec += 1; pe_sc[tau] = pec
        if tau >= 1:
            trc += 2; tr_b[tau - 1] = trc
            pec += 1; pe_pv[tau - 1] = pec
    ve_ev = [0] * NT; ve_nm = [0] * NT; ve_pc = [0] * NT
    vec = ve0
    for tau in range(NT + 1):
        if tau < NT:
            vec += 1; ve_ev[tau] = vec
        if tau >= 1:
            vec += 1; ve_nm[tau - 1] = vec
            vec += 1; ve_pc[tau - 1] = vec
    ev_ex = [0] * NT; ev_pc = [0] * NT; ev_oe = [0] * NT
    evc = ev0
    for tau in range(NT + 1):
        if tau < NT:
            evc += 1; ev_ex[tau] = evc
        if tau >= 1:
            evc += 1; ev_pc[tau - 1] = evc
            evc += 1; ev_oe[tau - 1] = evc

    def nblocks(j):
        return 2 * (j + 1)

    def kv_tc(j, i):
        return i

    def f_pe(t):
        for tau in range(NT + 1):
            if tau < NT:
                g, hh, j = its[tau]
                W = (j + 1) * 128
                if tau >= 1:
                    t.wait_ge(b.sems["VE"], ve_ev[tau - 1])
                qsl = Qn_sb[:, hh * 1024 + j * 128: hh * 1024 + (j + 1) * 128]
                segs = []
                ofs = 0
                while ofs < 2 * W:
                    wseg = min(512 - (ofs % 512), 2 * W - ofs)
                    segs.append((ofs, wseg, g * 2048 + ofs))
                    ofs += wseg
                ins = None
                for a, wseg, kc in segs:
                    ins = t.matmul(PS[a // 512][:, a % 512:(a % 512) + wseg], qsl,
                                   K_sb[:, kc:kc + wseg], start=True, stop=True)
                b.inc(ins, "PE", 1)
            if tau >= 1:
                g, hh, j = its[tau - 1]
                nb = nblocks(j)
                p = (tau - 1) % 2
                t.wait_ge(b.sems["VE"], ve_nm[tau - 1])
                if tau >= 2:
                    t.wait_ge(b.sems["EV"], ev_pc[tau - 2])
                for bi in range(nb):
                    pv = PT6 if bi < 8 else PT7
                    sl = (bi % 8) * 128
                    ins = t.transpose(pv[:, sl:sl + 128], prb2[p][:, bi * 128:(bi + 1) * 128],
                                      io["s_ident"][:, :])
                    if bi == min(nb, 8) - 1:
                        b.inc(ins, "TR", 1)
                if nb <= 8:
                    b.inc(t.nop(), "TR", 1)
                else:
                    b.inc(ins, "TR", 1)
                t.wait_ge(b.sems["VE"], ve_pc[tau - 1])
                t.wait_ge(b.sems["EV"], ev_pc[tau - 1])
                if tau >= 2:
                    t.wait_ge(b.sems["EV"], ev_oe[tau - 2])
                ins = None
                for bi in range(nb):
                    tc = kv_tc(j, bi)
                    ins = t.matmul(PS[5][:, 0:128],
                                   V_sb[:, tc * 512 + g * 128: tc * 512 + (g + 1) * 128],
                                   pt_sb[p][:, bi * 128:(bi + 1) * 128],
                                   start=(bi == 0), stop=(bi == nb - 1))
                b.inc(ins, "PE", 1)
    block.tensor(f_pe)

    def f_dve(v):
        for tau in range(NT + 1):
            if tau < NT:
                g, hh, j = its[tau]
                nb = nblocks(j)
                W = (j + 1) * 128
                p = tau % 2
                v.wait_ge(b.sems["PE"], pe_sc[tau])
                for bi in range(nb):
                    a = bi * 128
                    seg = PS[a // 512][:, a % 512:(a % 512) + 128]
                    if bi == nb - 2:
                        v.tensor_tensor(sc_sb[p][:, a:a + 128], seg, io["s_mask_a"][:, :], ALU.add)
                    elif bi == nb - 1:
                        v.tensor_tensor(sc_sb[p][:, a:a + 128], seg, io["s_mask_b"][:, :], ALU.add)
                    else:
                        v.tensor_copy(sc_sb[p][:, a:a + 128], seg)
                b.inc(v.reduce_max(nm_c[p][:, :], sc_sb[p][:, 0:2 * W], axis=AX, negate=True),
                      "VE", 1)
            if tau >= 1:
                g, hh, j = its[tau - 1]
                nb = nblocks(j)
                W = (j + 1) * 128
                p = (tau - 1) % 2
                v.wait_ge(b.sems["EV"], ev_ex[tau - 1])
                v.reciprocal(rs_c[p][:, :], ssum[p][:, :])
                b.inc(v.tensor_scalar(prb2[p][:, 0:2 * W], prb[p][:, 0:2 * W], rs_c[p][:, :],
                                      None, ALU.mult), "VE", 1)
                v.wait_ge(b.sems["TR"], tr_b[tau - 1])
                ins = None
                for bi in range(min(nb, 8)):
                    ins = v.tensor_copy(pt_sb[p][:, bi * 128:(bi + 1) * 128],
                                        PT6[:, (bi % 8) * 128:(bi % 8) * 128 + 128])
                b.inc(ins, "VE", 1)
    block.vector(f_dve)

    def f_act(s):
        for tau in range(NT + 1):
            if tau < NT:
                g, hh, j = its[tau]
                W = (j + 1) * 128
                p = tau % 2
                s.wait_ge(b.sems["VE"], ve_ev[tau])
                b.inc(s.activation(prb[p][:, 0:2 * W], sc_sb[p][:, 0:2 * W], AF.Exp,
                                   bias=nm_c[p][:, :], accum_out=ssum[p][:, :]), "EV", 1)
            if tau >= 1:
                g, hh, j = its[tau - 1]
                nb = nblocks(j)
                p = (tau - 1) % 2
                s.wait_ge(b.sems["TR"], tr_b[tau - 1])
                ins = s.nop()
                for bi in range(8, nb):
                    ins = s.copy(pt_sb[p][:, bi * 128:(bi + 1) * 128],
                                 PT7[:, (bi % 8) * 128:(bi % 8) * 128 + 128])
                b.inc(ins, "EV", 1)
                s.wait_ge(b.sems["PE"], pe_pv[tau - 1])
                b.inc(s.copy(Qn_sb[:, hh * 1024 + j * 128: hh * 1024 + (j + 1) * 128],
                             PS[5][:, 0:128]), "EV", 1)
    block.scalar(f_act)

    b.barrier()

    pe1 = b.cnt["PE"]; ve1 = b.cnt["VE"]
    gb1 = [b.cnt["dG0"], b.cnt["dG1"]]
    gv1 = lambda li: gb1[li % 2] + 16 * (li // 2 + 1)

    def f_sync(q):
        for c in range(HC):
            b.dma(q, res2[:, c * 1024:(c + 1) * 1024], io["s_r1o"][c * 128:(c + 1) * 128, :], "dX")
    block.sync(f_sync)
    w_r = b.cnt["dX"]

    def f_gload(g):
        for oc in range(4):
            for d in range(HC):
                li = oc * HC + d
                if li >= 2:
                    g.wait_ge(b.sems["PE"], pe1 + li - 1)
                b.dma(g, wob[d % 2][:, :], io["wo"][d * 128:(d + 1) * 128, :], "dG%d" % (li % 2))
    block.gpsimd(f_gload)

    def f_pe2(t):
        for oc in range(4):
            if oc >= 1:
                t.wait_ge(b.sems["VE"], ve1 + 8 * oc)
            for d in range(HC):
                li = oc * HC + d
                t.wait_ge(b.sems["dG%d" % (li % 2)], gv1(li))
                ins = None
                for g8 in range(8):
                    ho, sp = (oc * 8 + g8) // 2, (oc * 8 + g8) % 2
                    ins = t.matmul(PS[g8][:, :], wob[d % 2][:, ho * 128:(ho + 1) * 128],
                                   Qn_sb[:, d * 1024 + sp * 512: d * 1024 + (sp + 1) * 512],
                                   start=(d == 0), stop=(d == HC - 1))
                b.inc(ins, "PE", 1)
    block.tensor(f_pe2)

    def f_dve2(v):
        v.wait_ge(b.sems["dX"], w_r)
        for oc in range(4):
            v.wait_ge(b.sems["PE"], pe1 + HC * (oc + 1))
            for g8 in range(8):
                ho, sp = (oc * 8 + g8) // 2, (oc * 8 + g8) % 2
                col = ho * 1024 + sp * 512
                b.inc(v.tensor_tensor(res2[:, col:col + 512], res2[:, col:col + 512],
                                      PS[g8][:, :], ALU.add), "VE", 1)
    block.vector(f_dve2)

    b.barrier()
    es.close()



def _build(debug=None):
    nc = bass.Bass("TRN2", target_bir_lowering=False)
    io = {}
    dt_in = [
        ("x", [D, T], F32), ("wq", [D, D], F32), ("wk", [D, DKV], F32), ("wv", [D, DKV], F32),
        ("wo", [D, D], F32), ("gqt", [128, 16], F32), ("gkt", [128, 4], F32),
        ("gpT", [1024, 2048], I32), ("gsT", [64, 2048], F32),
        ("upT", [1024, 2048], I32), ("usT", [64, 2048], F32),
        ("dpT", [1024, D], I32), ("dsT", [64, D], F32),
        ("ident", [128, 128], BF16), ("ones_col", [128, 1], F32), ("ones_row", [1, 128], F32),
        ("ones_row_bf", [1, 128], BF16),
        ("emat", [8, 128], F32), ("emat_h", [8, 128], F32),
        ("mask_a", [128, 128], F32), ("mask_b", [128, 128], F32),
        ("rcos", [128, T], F32), ("rsin", [128, T], F32), ("sswap", [128, 128], BF16),
    ]
    for nm, shape, dt in dt_in:
        io[nm] = nc.dram_tensor(nm, shape, dt, kind="ExternalInput").ap()
    io["y"] = nc.dram_tensor("y", [D, TO], F32, kind="ExternalOutput").ap()
    io["dq_loc"] = nc.dram_tensor("dq_loc", [6144, 2048], BF16, kind="Internal").ap()
    for fam in ("g0", "g1", "u0", "u1", "d0", "d1"):
        io["dqg_" + fam] = nc.dram_tensor("dqg_" + fam, [8192, 2048], BF16, kind="Internal",
                                          addr_space="Shared").ap()

    def _sgap(i, h2):
        r0 = 1024 * (i // 16)
        c0 = (i % 16) * 128
        return io["dqg_g%d" % h2][r0:r0 + 1024, c0:c0 + 128].rearrange("(k p) c -> p k c", k=8)

    def _suap(i, h2):
        r0 = 1024 * (i // 16)
        c0 = (i % 16) * 128
        return io["dqg_u%d" % h2][r0:r0 + 1024, c0:c0 + 128].rearrange("(k p) c -> p k c", k=8)

    def _sdap(i, qq):
        c = i % 16
        r0 = 1024 * (i // 16) + (c % 8) * 128
        return io["dqg_d%d" % (c // 8)][r0:r0 + 128, qq * 512:(qq + 1) * 512]

    io["sgap"] = _sgap
    io["suap"] = _suap
    io["sdap"] = _sdap
    io["s_xn1"] = nc.dram_tensor("s_xn1", [D, T], BF16, kind="Internal").ap()
    io["s_xn2"] = nc.dram_tensor("s_xn2", [D, T], BF16, kind="Internal").ap()
    io["s_r1o"] = nc.dram_tensor("s_r1o", [D, TO], F32, kind="Internal").ap()
    io["s_xn3"] = nc.dram_tensor("s_xn3", [D, TO], BF16, kind="Internal").ap()
    io["kv_loc"] = nc.dram_tensor("kv_loc", [1536, 1024], BF16, kind="Internal").ap()
    io["kv_glob"] = nc.dram_tensor("kv_glob", [3072, 1024], BF16, kind="Internal").ap()
    if debug == "dequant":
        io["dbg_dq"] = nc.dram_tensor("dbg_dq", [49152, 2048], BF16, kind="ExternalOutput").ap()
    if debug == "ffn1":
        io["dbg_xn2"] = nc.dram_tensor("dbg_xn2", [D, T], BF16, kind="ExternalOutput").ap()
        io["dbg_r1o"] = nc.dram_tensor("dbg_r1o", [D, TO], F32, kind="ExternalOutput").ap()
    if debug == "attn":
        io["dbg_res2"] = nc.dram_tensor("dbg_res2", [D, TO], F32, kind="ExternalOutput").ap()
    if debug == "ln1":
        io["dbg_xn1"] = nc.dram_tensor("dbg_xn1", [D, T], BF16, kind="ExternalOutput").ap()
        io["dbg_mbc"] = nc.dram_tensor("dbg_mbc", [128, T], F32, kind="ExternalOutput").ap()
        io["dbg_rbc"] = nc.dram_tensor("dbg_rbc", [128, T], F32, kind="ExternalOutput").ap()

    es = ExitStack()
    block = es.enter_context(nc.Block())
    b = Build(nc, es, block)
    sbc = lambda nm, shape, dt=F32: es.enter_context(nc.sbuf_tensor(nm, shape, dt))
    consts = [("s_ident", "ident", [128, 128], BF16), ("s_ones_col", "ones_col", [128, 1], F32),
              ("s_ones_row", "ones_row", [1, 128], F32), ("s_onesbf", "ones_row_bf", [1, 128], BF16),
              ("s_emat", "emat", [8, 128], F32),
              ("s_emat_h", "emat_h", [8, 128], F32), ("s_gqt", "gqt", [128, 16], F32),
              ("s_gkt", "gkt", [128, 4], F32), ("s_mask_a", "mask_a", [128, 128], F32),
              ("s_mask_b", "mask_b", [128, 128], F32), ("s_rcos", "rcos", [128, T], F32),
              ("s_rsin", "rsin", [128, T], F32), ("s_sswap", "sswap", [128, 128], BF16)]
    for snm, inm, shape, dt in consts:
        io[snm] = sbc(snm, shape, dt)
    io["PS"] = [es.enter_context(nc.psum_tensor(f"PS{i}", [128, 512], F32)) for i in range(8)]

    b.sem("dX")

    def f_const(q):
        for snm, inm, shape, dt in consts:
            b.dma(q, io[snm][:, :], io[inm][:, :], "dX")
    block.sync(f_const)
    b.barrier()

    _emit_dequant(b, io)

    esL = ExitStack()
    xarena = esL.enter_context(nc.sbuf_tensor("x_ln1", [128, HC * 1024], F32))

    def f_ldx(q):
        for c in range(HC):
            b.dma(q, xarena[:, c * 1024:(c + 1) * 1024], io["x"][c * 128:(c + 1) * 128, :], "dX")
    block.sync(f_ldx)
    b.barrier()
    _emit_ln_sbuf(b, io, {}, xarena, io["s_xn1"], None, LN_EPS)
    esL.close()

    if debug == "dequant":
        def f_dbg(q):
            q.wait_ge(b.sems["CC"], 6)
            for fi, fam in enumerate(("d0", "d1")):
                for blk in range(4):
                    b.dma(q, io["dbg_dq"][(fi * 4 + blk) * 2048:(fi * 4 + blk + 1) * 2048, :],
                          io["dqg_" + fam][blk * 2048:(blk + 1) * 2048, :], "dX")
            q.wait_ge(b.sems["dX"], b.cnt["dX"])
        block.sync(f_dbg)
        es.close()
        return nc

    if debug == "ln1":
        def f_dbg(q):
            b.dma(q, io["dbg_xn1"][:, :], io["s_xn1"][:, :], "dX")
            q.wait_ge(b.sems["dX"], b.cnt["dX"])
        block.sync(f_dbg)
        es.close()
        return nc

    esF = ExitStack()
    sbF = lambda nm, shape, dt=F32: esF.enter_context(nc.sbuf_tensor(nm, shape, dt))
    acc_a = sbF("acc0", [128, HC * 1024], F32)
    esT = ExitStack()
    sbT = lambda nm, shape, dt=F32: esT.enter_context(nc.sbuf_tensor(nm, shape, dt))
    ar = {
        "xn": sbT("xn0", [128, HC * 1024], BF16),
        "acc": acc_a,
        "inter": sbT("int0", [128, 16 * 1024], BF16),
        "gbuf": [sbT(f"gb0{i}", [128, 2048], BF16) for i in range(2)],
        "ubuf": [sbT(f"ub0{i}", [128, 2048], BF16) for i in range(2)],
        "dbuf": [sbT(f"db0{i}", [128, 512], BF16) for i in range(8)],
        "gf": sbT("gf0", [128, 1024], F32),
        "uf": sbT("uf0", [128, 1024], F32),
        "At": sbT("At0", [128, 1024], F32),
        "Bt": sbT("Bt0", [128, 1024], F32),
        "Ct": sbT("Ct0", [128, 1024], F32),
    }
    _emit_ffn_tblock(b, io, ar, io["s_xn1"], io["x"])
    esT.close()
    _emit_ln_sbuf(b, io, {}, acc_a, io["s_xn2"], io["s_r1o"], LN_EPS)
    esF.close()

    if debug == "ffn1":
        def f_dbg(q):
            b.dma(q, io["dbg_xn2"][:, :], io["s_xn2"][:, :], "dX")
            b.dma(q, io["dbg_r1o"][:, :], io["s_r1o"][:, :], "dX")
            q.wait_ge(b.sems["dX"], b.cnt["dX"])
        block.sync(f_dbg)
        es.close()
        return nc

    esA = ExitStack()
    sbA = lambda nm, shape, dt=BF16: esA.enter_context(nc.sbuf_tensor(nm, shape, dt))
    K_sb = sbA("K_sb", [128, NG * 2048])
    V_sb = sbA("V_sb", [128, 16 * 512])
    Qn_sb = sbA("Qn_sb", [128, NHEAD * 1024])
    esP = ExitStack()
    Qr_sb = esP.enter_context(nc.sbuf_tensor("Qr_sb", [128, NHEAD * 1024], BF16))
    K_own = esP.enter_context(nc.sbuf_tensor("K_own", [128, NG * 1024], BF16))
    V_own = esP.enter_context(nc.sbuf_tensor("V_own", [128, 8 * 512], BF16))
    esP2 = ExitStack()
    xn2_sb = esP2.enter_context(nc.sbuf_tensor("xn2_sb", [128, HC * 1024], BF16))
    _emit_proj(b, io, K_own, V_own, Qr_sb, xn2_sb)
    esP2.close()
    k_done = _emit_rms_rope(b, io, K_own, Qr_sb, Qn_sb)
    _emit_kv_exchange(b, io, K_own, V_own, K_sb, V_sb, k_done)
    esP.close()
    esR2 = ExitStack()
    res2 = esR2.enter_context(nc.sbuf_tensor("res2", [128, HC * 1024], F32, side="right"))
    _emit_attn_wo(b, io, K_sb, V_sb, Qn_sb, res2)
    esA.close()

    if debug == "attn":
        def f_dbg(q):
            for c in range(HC):
                b.dma(q, io["dbg_res2"][c * 128:(c + 1) * 128, :], res2[:, c * 1024:(c + 1) * 1024], "dX")
            q.wait_ge(b.sems["dX"], b.cnt["dX"])
        block.sync(f_dbg)
        es.close()
        return nc

    _emit_ln_sbuf(b, io, {}, res2, io["s_xn3"], None, LN_EPS)

    esF3 = ExitStack()
    sb3 = lambda nm, shape, dt=F32: esF3.enter_context(nc.sbuf_tensor(nm, shape, dt))
    ar3 = {
        "xn": sb3("xn3", [128, HC * 1024], BF16),
        "acc": res2,
        "inter": sb3("int3", [128, 16 * 1024], BF16),
        "gbuf": [sb3(f"gb3{i}", [128, 2048], BF16) for i in range(2)],
        "ubuf": [sb3(f"ub3{i}", [128, 2048], BF16) for i in range(2)],
        "dbuf": [sb3(f"db3{i}", [128, 512], BF16) for i in range(8)],
        "gf": sb3("gf3", [128, 1024], F32),
        "uf": sb3("uf3", [128, 1024], F32),
        "At": sb3("At3", [128, 1024], F32),
        "Bt": sb3("Bt3", [128, 1024], F32),
        "Ct": sb3("Ct3", [128, 1024], F32),
    }
    _emit_ffn_tblock(b, io, ar3, io["s_xn3"], None)
    esF3.close()

    def f_out(q):
        for c in range(HC):
            b.dma(q, io["y"][c * 128:(c + 1) * 128, :], res2[:, c * 1024:(c + 1) * 1024], "dX")
        q.wait_ge(b.sems["dX"], b.cnt["dX"])
    block.sync(f_out)
    esR2.close()

    es.close()
    return nc


_CACHE = {}


def kernel(**inputs):
    maps = _host_inputs(inputs)
    if "nc" not in _CACHE:
        _CACHE["nc"] = _build()
    nc = _CACHE["nc"]
    res = run_bass_kernel_spmd(nc, maps, core_ids=list(range(8)))
    return _host_output(res.results)


# version 31
# speedup vs baseline: 1.0227x; 1.0227x over previous
import numpy as np
from contextlib import ExitStack

import concourse.bass as bass
import concourse.mybir as mybir
from concourse.bass_utils import run_bass_kernel_spmd

F32 = mybir.dt.float32
BF16 = mybir.dt.bfloat16
I32 = mybir.dt.int32
AF = mybir.ActivationFunctionType
ALU = mybir.AluOpType
AX = mybir.AxisListType.X

D = 2048
T = 1024
TO = 1024
HC = D // 128
NM = 16384
NCH = NM // 128
DH = 128
NG = 4
HPG = 4
DKV = 512
NHEAD = 16
LN_EPS = 1e-5
RMS_EPS = 1e-6
GELU_C = 0.7978845608028654
GELU_A = 0.044715
NEG = -1e30



def _pair_perm(n):
    out = np.empty(n, np.int64)
    for bb in range(n // 256):
        out[bb * 256: bb * 256 + 128] = bb * 256 + 2 * np.arange(128)
        out[bb * 256 + 128: bb * 256 + 256] = bb * 256 + 2 * np.arange(128) + 1
    return out


PI = _pair_perm(D)
SIGMA = _pair_perm(NM)


def _slot_tokens(par):
    return np.concatenate([np.arange(s * 128, (s + 1) * 128)
                           for s in range(par, 16, 2)])


def _rope_tables(tok):
    pos = tok.astype(np.float64)[None, :]
    j = np.arange(64, dtype=np.float64)[:, None]
    ang = np.power(10000.0, -j / 64.0)
    rad = (pos * ang).astype(np.float32)
    return np.cos(rad).astype(np.float32), np.sin(rad).astype(np.float32)


def _host_inputs(inp):
    x = np.asarray(inp["L0_attn_x"], np.float32)
    wq = np.asarray(inp["L0_attn_wq"], np.float32)
    wk = np.asarray(inp["L0_attn_wk"], np.float32)
    wv = np.asarray(inp["L0_attn_wv"], np.float32)
    wo = np.asarray(inp["L0_attn_wo"], np.float32)
    gq = np.asarray(inp["L0_attn_gq"], np.float32)
    gk = np.asarray(inp["L0_attn_gk"], np.float32)
    gp = np.asarray(inp["L0_ffn_gate_W_packed"], np.int32)
    gs = np.asarray(inp["L0_ffn_gate_scales"], np.float32)
    up = np.asarray(inp["L0_ffn_up_W_packed"], np.int32)
    us = np.asarray(inp["L0_ffn_up_scales"], np.float32)
    dp = np.asarray(inp["L0_ffn_down_W_packed"], np.int32)
    ds = np.asarray(inp["L0_ffn_down_scales"], np.float32)

    C = np.ascontiguousarray
    import ml_dtypes
    gpT = gp[SIGMA, :].T
    gsT = gs[SIGMA, :].T
    upT = up[SIGMA, :].T
    usT = us[SIGMA, :].T
    dpT = dp[PI, :].T
    dsT = ds[PI, :].T
    shared = {
        "wq": C(wq[PI, :]),
        "wk": C(wk[PI, :]),
        "wv": C(wv[PI, :]),
        "wo": C(wo[:, PI]),
        "gqt": C(np.transpose(gq, (2, 0, 1)).reshape(128, 16)),
        "gkt": C(gk.T),
        "ident": np.eye(128, dtype=ml_dtypes.bfloat16),
        "ones_col": np.ones((128, 1), np.float32),
        "ones_row": np.ones((1, 128), np.float32),
        "ones_row_bf": np.ones((1, 128), dtype=ml_dtypes.bfloat16),
    }
    em = np.zeros((8, 128), np.float32)
    for p in range(128):
        em[p // 16, p] = 1.0
    shared["emat"] = -em
    shared["emat_h"] = -0.5 * em
    tri = np.where(np.arange(128)[None, :] > np.arange(128)[:, None],
                   np.float32(NEG), np.float32(0.0))
    sw = np.zeros((128, 128), dtype=ml_dtypes.bfloat16)
    for i in range(128):
        sw[(i + 64) % 128, i] = 1
    shared["sswap"] = sw

    maps = []
    for c in range(8):
        p, par = c // 2, c % 2
        tok = _slot_tokens(par)
        m = dict(shared)
        if par == 0:
            m["mask_a"] = C(tri)
            m["mask_b"] = np.full((128, 128), NEG, np.float32)
        else:
            m["mask_a"] = np.zeros((128, 128), np.float32)
            m["mask_b"] = C(tri)
        m["x"] = C(x[p].T[PI][:, tok])
        m["gpT"] = C(gpT[:, c * 2048:(c + 1) * 2048])
        m["gsT"] = C(gsT[:, c * 2048:(c + 1) * 2048])
        m["upT"] = C(upT[:, c * 2048:(c + 1) * 2048])
        m["usT"] = C(usT[:, c * 2048:(c + 1) * 2048])
        m["dpT"] = C(dpT[c * 1024:(c + 1) * 1024, :])
        m["dsT"] = C(dsT[c * 64:(c + 1) * 64, :])
        cos, sin = _rope_tables(tok)
        m["rcos"] = np.vstack([cos, cos])
        m["rsin"] = np.vstack([-sin, sin])
        maps.append(m)
    return maps


def _host_output(results):
    out = np.empty((4, 2048, 2048), np.float32)
    inv = np.empty(D, np.int64)
    inv[PI] = np.arange(D)
    for p in range(4):
        full = np.empty((2048, 2048), np.float32)
        for par in range(2):
            y = np.asarray(results[2 * p + par]["y"])
            for j in range(8):
                s = 2 * j + par
                full[:, s * 128:(s + 1) * 128] = y[:, j * 128:(j + 1) * 128]
        out[p] = full[inv].T
    return out



class Build:
    def __init__(self, nc, es, block):
        self.nc = nc
        self.es = es
        self.block = block
        self.cnt = {}
        self.sems = {}
        self.phase = 0
        self.nuid = 0

    def uid(self):
        self.nuid += 1
        return self.nuid

    def sem(self, name):
        if name not in self.sems:
            self.sems[name] = self.es.enter_context(self.nc.semaphore(name))
            self.cnt[name] = 0
        return self.sems[name]

    def inc(self, instr, name, amt):
        instr.then_inc(self.sems[name], amt)
        self.cnt[name] += amt
        return self.cnt[name]

    def dma(self, q, out, in_, name):
        return self.inc(q.dma_start(out=out, in_=in_), name, 16)

    def barrier(self):
        self.sem("PH")
        dma_names = [n for n in ("dX", "dA0", "dA1", "dB0", "dB1", "dG0", "dG1", "dS0", "dS1")
                     if n in self.sems]
        targets = {n: self.cnt[n] for n in dma_names}

        def _sync(eng):
            for n in dma_names:
                eng.wait_ge(self.sems[n], targets[n])
            self.inc(eng.nop(), "PH", 1)
        self.block.sync(_sync)

        for name in ("scalar", "vector", "gpsimd", "tensor"):
            def _mk(eng):
                self_inc = eng.nop().then_inc(self.sems["PH"], 1)
            getattr(self.block, name)(_mk)
            self.cnt["PH"] += 1
        tgt = self.cnt["PH"]

        def _wait(eng):
            eng.wait_ge(self.sems["PH"], tgt)
        for name in ("sync", "scalar", "vector", "gpsimd", "tensor"):
            getattr(self.block, name)(_wait)
        self.phase += 1



def _emit_dequant(b, io):
    nc, block = b.nc, b.block
    es = ExitStack()
    sb = lambda nm, shape, dt=F32: es.enter_context(nc.sbuf_tensor(nm, shape, dt))
    pk = [sb(f"dq_pk{i}", [128, 2048], I32) for i in range(2)]
    sT = [sb(f"dq_sT{i}", [8, 2048]) for i in range(2)]
    scl = [sb(f"dq_scl{i}", [128, 2048]) for i in range(2)]
    loi = [sb(f"dq_loi{i}", [128, 2048], I32) for i in range(2)]
    hii = [sb(f"dq_hii{i}", [128, 2048], I32) for i in range(2)]
    cf = [sb(f"dq_cf{i}", [128, 2048]) for i in range(4)]
    m8 = sb("dq_m8", [128, 2048])
    mt = sb("dq_mt", [128, 2048])
    t4h = sb("dq_t4h", [128, 2048])
    uuh = [sb(f"dq_uh{i}", [128, 2048]) for i in range(2)]
    t6 = [sb(f"dq_t6{i}", [128, 2048]) for i in range(2)]
    sgs = [sb(f"dq_sgs{i}", [128, 2048], BF16) for i in range(2)]
    mg = sb("dq_mg", [128, 2048])
    nb75 = sb("dq_n75", [128, 1])
    val = [sb(f"dq_val{i}", [128, 2048], BF16) for i in range(2)]
    PS = io["PS"]
    dq = io["dq_loc"]

    for s_ in ("dA0", "dA1", "dS0", "dS1", "PE", "EV", "VE", "PL", "CC"):
        b.sem(s_)
    ab = [b.cnt["dA0"], b.cnt["dA1"]]
    ob = [b.cnt["dS0"], b.cnt["dS1"]]
    pe0 = b.cnt["PE"]; ev0 = b.cnt["EV"]; ve0 = b.cnt["VE"]; pl0 = b.cnt["PL"]

    steps = []
    for kind, pT, sTa, ema in (("g", io["gpT"], io["gsT"], io["s_emat"]),
                               ("u", io["upT"], io["usT"], io["s_emat"]),
                               ("d", io["dpT"], io["dsT"], io["s_emat_h"])):
        for c2 in range(8):
            steps.append((kind, pT, sTa, ema, c2))
    NS = len(steps)

    wA = lambda i: ab[i % 2] + 32 * (i // 2 + 1)
    pe = lambda i: pe0 + i + 1
    evac = lambda i: ev0 + 5 * i + 1
    cvt = lambda i, j: ev0 + 5 * i + 2 + j
    evsg = lambda i, j: ev0 + 5 * i + 4 + j
    ve_setup = ve0 + 1
    ve_ext = [0] * NS; ve_nib = [0] * (2 * NS)
    vec = ve_setup
    vec += 1; ve_ext[0] = vec
    for i in range(NS):
        if i + 1 < NS:
            vec += 1; ve_ext[i + 1] = vec
        for j in range(2):
            vec += 1; ve_nib[2 * i + j] = vec
    ext = lambda i: ve_ext[i]
    nib = lambda g: ve_nib[g]
    plmg = lambda g: pl0 + 3 * g + 1
    plsg = lambda g: pl0 + 3 * g + 2
    plval = lambda g: pl0 + 3 * g + 3
    stc = [0] * (2 * NS)
    acc2 = [ob[0], ob[1]]
    for g in range(2 * NS):
        acc2[g % 2] += 16
        stc[g] = acc2[g % 2]

    def _stores(q, g):
        i, j = g // 2, g % 2
        kind, _, _, _, c2 = steps[i]
        kk = 2 * c2 + j
        base = {"g": 0, "u": 2048, "d": 4096}[kind]
        b.dma(q, dq[base + kk * 128: base + (kk + 1) * 128, :], val[j][:, :], "dS%d" % j)

    def f_sync(q):
        for i, (kind, pT, sTa, ema, c2) in enumerate(steps):
            if i >= 2:
                q.wait_ge(b.sems["VE"], ext(i - 2))
                q.wait_ge(b.sems["PE"], pe(i - 2))
            b.dma(q, pk[i % 2][:, :], pT[c2 * 128:(c2 + 1) * 128, :], "dA%d" % (i % 2))
            b.dma(q, sT[i % 2][:, :], sTa[c2 * 8:(c2 + 1) * 8, :], "dA%d" % (i % 2))
            if i >= 2:
                for j in range(2):
                    g = 2 * (i - 2) + j
                    q.wait_ge(b.sems["PL"], plval(g))
                    _stores(q, g)
        for g in (2 * NS - 4, 2 * NS - 3, 2 * NS - 2, 2 * NS - 1):
            q.wait_ge(b.sems["PL"], plval(g))
            _stores(q, g)
    block.sync(f_sync)

    def f_pe(t):
        for i in range(NS):
            ema = steps[i][3]
            t.wait_ge(b.sems["dA%d" % (i % 2)], wA(i))
            if i >= 2:
                t.wait_ge(b.sems["EV"], evac(i - 2))
            for s4 in range(4):
                ins = t.matmul(PS[4 * (i % 2) + s4][:, :], ema[:, :],
                               sT[i % 2][:, s4 * 512:(s4 + 1) * 512], start=True, stop=True)
            b.inc(ins, "PE", 1)
    block.tensor(f_pe)

    def f_act(s):
        for i in range(NS):
            s.wait_ge(b.sems["PE"], pe(i))
            if i >= 2:
                s.wait_ge(b.sems["PL"], plval(2 * (i - 2) + 1))
            for s4 in range(4):
                ins = s.copy(scl[i % 2][:, s4 * 512:(s4 + 1) * 512], PS[4 * (i % 2) + s4][:, :])
            b.inc(ins, "EV", 1)
            s.wait_ge(b.sems["VE"], ext(i))
            for j in range(2):
                if i >= 2:
                    s.wait_ge(b.sems["VE"], nib(2 * (i - 2) + j))
                b.inc(s.copy(cf[2 * (i % 2) + j][:, :],
                             (loi if j == 0 else hii)[i % 2][:, :]), "EV", 1)
            for j in range(2):
                if i == 0 and j == 0:
                    s.wait_ge(b.sems["VE"], ve_setup)
                if i >= 1:
                    s.wait_ge(b.sems["PL"], plsg(2 * (i - 1) + j))
                b.inc(s.activation(sgs[j][:, :], cf[2 * (i % 2) + j][:, :], AF.Sign,
                                   bias=nb75[:, :]), "EV", 1)
    block.scalar(f_act)

    def f_dve(v):
        b.inc(v.tensor_scalar(nb75[:, :], io["s_ones_col"][:, :], -7.5, None, ALU.mult),
              "VE", 1)
        def _ext(i):
            v.wait_ge(b.sems["dA%d" % (i % 2)], wA(i))
            if i >= 2:
                v.wait_ge(b.sems["EV"], cvt(i - 2, 1))
            v.tensor_scalar(loi[i % 2][:, :], pk[i % 2][:, :], 15, None, ALU.bitwise_and)
            b.inc(v.tensor_scalar(hii[i % 2][:, :], pk[i % 2][:, :], 4, None,
                                  ALU.logical_shift_right), "VE", 1)
        _ext(0)
        for i in range(NS):
            if i + 1 < NS:
                _ext(i + 1)
            for j in range(2):
                g = 2 * i + j
                cfs = cf[2 * (i % 2) + j]
                v.wait_ge(b.sems["EV"], cvt(i, j))
                if g >= 2:
                    v.wait_ge(b.sems["PL"], plmg(g - 2))
                v.tensor_scalar(m8[:, :], cfs[:, :], 8.0, None, ALU.is_ge)
                v.scalar_tensor_tensor(mt[:, :], m8[:, :], -8.0, cfs[:, :], ALU.mult, ALU.add)
                v.tensor_scalar(t4h[:, :], mt[:, :], 4.0, 0.5, ALU.max, ALU.mult)
                v.scalar_tensor_tensor(uuh[j][:, :], mt[:, :], 0.5, t4h[:, :],
                                       ALU.mult, ALU.add)
                b.inc(v.tensor_scalar(t6[j][:, :], mt[:, :], 6.0, -8.0, ALU.max, ALU.add),
                      "VE", 1)
    block.vector(f_dve)

    def f_pool(gq):
        for i in range(NS):
            for j in range(2):
                g = 2 * i + j
                gq.wait_ge(b.sems["VE"], nib(g))
                b.inc(gq.tensor_tensor(mg[:, :], uuh[j][:, :], t6[j][:, :], ALU.add), "PL", 1)
                gq.wait_ge(b.sems["EV"], evsg(i, j))
                b.inc(gq.tensor_tensor(mg[:, :], mg[:, :], sgs[j][:, :], ALU.mult), "PL", 1)
                gq.wait_ge(b.sems["EV"], evac(i))
                if g >= 2:
                    gq.wait_ge(b.sems["dS%d" % j], stc[g - 2])
                b.inc(gq.tensor_tensor(val[j][:, :], mg[:, :], scl[i % 2][:, :], ALU.mult),
                      "PL", 1)
            if i % 4 == 3:
                fam = "gud"[i // 8]
                h2 = (i // 4) % 2
                gq.wait_ge(b.sems["dS0"], stc[2 * i])
                gq.wait_ge(b.sems["dS1"], stc[2 * i + 1])
                base = {"g": 0, "u": 2048, "d": 4096}[fam]
                src_ap = io["dq_loc"][base + h2 * 1024: base + (h2 + 1) * 1024, :]
                gq.collective_compute(
                    "AllGather", ALU.bypass, replica_groups=[list(range(8))],
                    ins=[src_ap.opt()], outs=[io["dqg_%s%d" % (fam, h2)].opt()],
                ).then_inc(b.sems["CC"])
                b.cnt["CC"] += 1
    block.gpsimd(f_pool)

    b.barrier()
    es.close()



def _rstd(v, s, row_v, row_out, row_tmp, eps, width):
    raise NotImplementedError



def _emit_ln_dram(b, io, src_ap, dst_ap, ncols, eps):
    nc, block = b.nc, b.block
    PS = io["PS"]
    es = ExitStack()
    u = b.uid()
    sb = lambda nm, shape, dt=F32: es.enter_context(nc.sbuf_tensor(f"{nm}_{u}", shape, dt))
    nsp = ncols // 512
    xa = [sb(f"ln_x{i}", [128, ncols]) for i in range(2)]
    sq = sb("ln_sq", [128, ncols])
    m_row = sb("ln_m", [1, ncols])
    s_row = sb("ln_s", [1, ncols])
    t_row = sb("ln_t", [1, ncols])
    m_bc = sb("ln_mbc", [128, ncols])
    r_bc = sb("ln_rbc", [128, ncols])
    mh = sb("ln_mh", [1, ncols], BF16)
    ml = sb("ln_ml", [1, ncols], BF16)
    rh = sb("ln_rh", [1, ncols], BF16)
    rl = sb("ln_rl", [1, ncols], BF16)
    xb = [sb(f"ln_xb{i}", [128, ncols], BF16) for i in range(2)]

    for s_ in ("dA0", "dA1", "dS0", "dS1", "PE", "EV", "VE"):
        b.sem(s_)
    wb = [b.cnt["dA0"], b.cnt["dA1"]]
    sb0 = [b.cnt["dS0"], b.cnt["dS1"]]
    pe0 = b.cnt["PE"]; ev0 = b.cnt["EV"]; ve0 = b.cnt["VE"]
    wv = lambda c: wb[c % 2] + 16 * (c // 2 + 1)
    wv2 = lambda c: wb[c % 2] + 16 * (HC // 2) + 16 * (c // 2 + 1)
    sv = lambda c: sb0[c % 2] + 16 * (c // 2 + 1)

    def f_sync(q):
        for c in range(HC):
            if c >= 2:
                q.wait_ge(b.sems["PE"], pe0 + (c - 1))
            b.dma(q, xa[c % 2][:, :], src_ap[c * 128:(c + 1) * 128, 0:ncols], "dA%d" % (c % 2))
    block.sync(f_sync)

    def f_act(s):
        for c in range(HC):
            s.wait_ge(b.sems["dA%d" % (c % 2)], wv(c))
            if c >= 1:
                s.wait_ge(b.sems["PE"], pe0 + c)
            b.inc(s.activation(sq[:, :], xa[c % 2][:, :], AF.Square), "EV", 1)
    block.scalar(f_act)

    def f_pe(t):
        for c in range(HC):
            t.wait_ge(b.sems["dA%d" % (c % 2)], wv(c))
            t.wait_ge(b.sems["EV"], ev0 + c + 1)
            for s4 in range(nsp):
                t.matmul(PS[s4][0:1, :], io["s_ones_col"][:, :],
                         xa[c % 2][:, s4 * 512:(s4 + 1) * 512],
                         start=(c == 0), stop=(c == HC - 1))
                ins = t.matmul(PS[4 + s4 % 4][0:1, :] if nsp == 4 else PS[4 + s4][0:1, :],
                               io["s_ones_col"][:, :],
                               sq[:, s4 * 512:(s4 + 1) * 512],
                               start=(c == 0), stop=(c == HC - 1))
            b.inc(ins, "PE", 1)
    block.tensor(f_pe)
    pe_stats = b.cnt["PE"]

    def f_fin_v(v):
        v.wait_ge(b.sems["PE"], pe_stats)
        for s4 in range(nsp):
            v.tensor_scalar(m_row[:, s4 * 512:(s4 + 1) * 512], PS[s4][0:1, :], 1.0 / D, None, ALU.mult)
            v.tensor_scalar(s_row[:, s4 * 512:(s4 + 1) * 512], PS[4 + s4][0:1, :], 1.0 / D, None, ALU.mult)
        v.tensor_tensor(t_row[:, :], m_row[:, :], m_row[:, :], ALU.mult)
        v.tensor_tensor(s_row[:, :], s_row[:, :], t_row[:, :], ALU.subtract)
        b.inc(v.tensor_scalar(s_row[:, :], s_row[:, :], eps, None, ALU.add), "VE", 1)
    block.vector(f_fin_v)
    ve_var = b.cnt["VE"]

    def f_fin_s(s):
        s.wait_ge(b.sems["VE"], ve_var)
        b.inc(s.activation(t_row[:, :], s_row[:, :], AF.Sqrt), "EV", 1)
    block.scalar(f_fin_s)
    ev_sqrt = b.cnt["EV"]

    def f_fin_v2(v):
        v.wait_ge(b.sems["EV"], ev_sqrt)
        v.reciprocal(t_row[:, :], t_row[:, :])
        v.tensor_tensor(sq[0:1, 0:ncols], t_row[:, :], t_row[:, :], ALU.mult)
        v.tensor_tensor(sq[0:1, 0:ncols], sq[0:1, 0:ncols], s_row[:, :], ALU.mult)
        v.tensor_scalar(sq[0:1, 0:ncols], sq[0:1, 0:ncols], -0.5, 1.5, ALU.mult, ALU.add)
        v.tensor_tensor(t_row[:, :], t_row[:, :], sq[0:1, 0:ncols], ALU.mult)
        v.tensor_copy(mh[:, :], m_row[:, :])
        v.tensor_tensor(sq[0:1, 0:ncols], m_row[:, :], mh[:, :], ALU.subtract)
        v.tensor_copy(ml[:, :], sq[0:1, 0:ncols])
        v.tensor_copy(rh[:, :], t_row[:, :])
        v.tensor_tensor(sq[0:1, 0:ncols], t_row[:, :], rh[:, :], ALU.subtract)
        b.inc(v.tensor_copy(rl[:, :], sq[0:1, 0:ncols]), "VE", 1)
    block.vector(f_fin_v2)
    ve_rstd = b.cnt["VE"]

    def f_bc_pe(t):
        t.wait_ge(b.sems["VE"], ve_rstd)
        for s4 in range(nsp):
            if s4 >= 2:
                t.wait_ge(b.sems["EV"], ev_sqrt + 2 * (s4 - 2) + 2)
            sl = slice(s4 * 512, (s4 + 1) * 512)
            t.matmul(PS[s4 % 2][:, :], io["s_onesbf"][:, :], mh[:, sl], start=True, stop=False)
            b.inc(t.matmul(PS[s4 % 2][:, :], io["s_onesbf"][:, :], ml[:, sl],
                           start=False, stop=True), "PE", 1)
            t.wait_ge(b.sems["EV"], ev_sqrt + 2 * s4 + 1)
            t.matmul(PS[s4 % 2][:, :], io["s_onesbf"][:, :], rh[:, sl], start=True, stop=False)
            b.inc(t.matmul(PS[s4 % 2][:, :], io["s_onesbf"][:, :], rl[:, sl],
                           start=False, stop=True), "PE", 1)
    block.tensor(f_bc_pe)

    def f_bc_act(s):
        for s4 in range(nsp):
            s.wait_ge(b.sems["PE"], pe_stats + 2 * s4 + 1)
            b.inc(s.copy(m_bc[:, s4 * 512:(s4 + 1) * 512], PS[s4 % 2][:, :]), "EV", 1)
            s.wait_ge(b.sems["PE"], pe_stats + 2 * s4 + 2)
            b.inc(s.copy(r_bc[:, s4 * 512:(s4 + 1) * 512], PS[s4 % 2][:, :]), "EV", 1)
    block.scalar(f_bc_act)
    ev_bc = b.cnt["EV"]

    def f_sync2(q):
        for c in range(HC):
            if c < 2:
                q.wait_ge(b.sems["PE"], pe_stats)
            else:
                q.wait_ge(b.sems["VE"], ve_rstd + (c - 1))
            b.dma(q, xa[c % 2][:, :], src_ap[c * 128:(c + 1) * 128, 0:ncols], "dA%d" % (c % 2))
    block.sync(f_sync2)

    def f_norm(v):
        v.wait_ge(b.sems["EV"], ev_bc)
        for c in range(HC):
            v.wait_ge(b.sems["dA%d" % (c % 2)], wv2(c))
            if c >= 2:
                v.wait_ge(b.sems["dS%d" % (c % 2)], sv(c - 2))
            v.tensor_tensor(xa[c % 2][:, :], xa[c % 2][:, :], m_bc[:, :], ALU.subtract)
            b.inc(v.tensor_tensor(xb[c % 2][:, :], xa[c % 2][:, :], r_bc[:, :], ALU.mult), "VE", 1)
    block.vector(f_norm)

    def f_store(s):
        for c in range(HC):
            s.wait_ge(b.sems["VE"], ve_rstd + c + 1)
            b.dma(s, dst_ap[c * 128:(c + 1) * 128, 0:ncols], xb[c % 2][:, :], "dS%d" % (c % 2))
        if "dbg_mbc" in io and ncols == T:
            b.dma(s, io["dbg_mbc"][:, :], m_bc[:, :], "dS0")
            b.dma(s, io["dbg_rbc"][:, :], r_bc[:, :], "dS0")
    block.scalar(f_store)

    b.barrier()
    es.close()



def _emit_ffn_tblock(b, io, ar, xn_src, acc_init_src):
    nc, block = b.nc, b.block
    PS = io["PS"]
    xn, acc = ar["xn"], ar["acc"]
    inter = ar["inter"]
    gbuf, ubuf, dbuf = ar["gbuf"], ar["ubuf"], ar["dbuf"]
    gf, uf = ar["gf"], ar["uf"]
    At, Bt, Ct = ar["At"], ar["Bt"], ar["Ct"]

    NDB = len(dbuf)
    for s_ in ("dX", "dC", "dA0", "dA1", "dB0", "dB1", "GU", "DN", "EV", "VE", "PL", "DA", "CC"):
        b.sem(s_)
    gu0 = b.cnt["GU"]; dn0 = b.cnt["DN"]
    ev0 = b.cnt["EV"]; ve0 = b.cnt["VE"]; pl0 = b.cnt["PL"]; da0 = b.cnt["DA"]
    ab = [b.cnt["dA0"], b.cnt["dA1"]]
    bb = [b.cnt["dB0"], b.cnt["dB1"]]
    x0c = b.cnt["dX"]
    c0c = b.cnt["dC"]

    pre = x0c + 16 * (16 if xn_src is not None else 0)
    prec = c0c + 16 * (16 if acc_init_src is not None else 0)
    wA = lambda s, k: ab[(16 * s + k) % 2] + 64 * ((16 * s + k) // 2 + 1)
    wB = lambda s, q, k: bb[(64 * s + 16 * q + k) % 2] + 16 * ((64 * s + 16 * q + k) // 2 + 1)
    evv = lambda i, kk: ev0 + 2 * i + kk
    vev = lambda i, kk: ve0 + 2 * i + kk
    plv = lambda i, kk: pl0 + 3 * i + kk
    dnv = lambda s, q, k: dn0 + 64 * s + 16 * q + k + 1
    dav = lambda s, q, g8: da0 + 32 * s + 8 * q + g8 + 1

    def f_sync(q):
        q.wait_ge(b.sems["CC"], 4)
        for s in range(8):
            for k in range(16):
                i = 16 * s + k
                if i >= 2:
                    q.wait_ge(b.sems["GU"], gu0 + i - 1)
                for h2 in range(2):
                    b.dma(q, gbuf[i % 2][:, h2 * 1024:(h2 + 1) * 1024]
                          .rearrange("p (k c) -> p k c", k=8),
                          io["sgap"](i, h2), "dA%d" % (i % 2))
                    b.dma(q, ubuf[i % 2][:, h2 * 1024:(h2 + 1) * 1024]
                          .rearrange("p (k c) -> p k c", k=8),
                          io["suap"](i, h2), "dA%d" % (i % 2))
            if s == 0:
                q.wait_ge(b.sems["CC"], 6)
            for qq in range(4):
                for k in range(16):
                    i = 16 * s + k
                    j = 64 * s + 16 * qq + k
                    if j >= NDB:
                        q.wait_ge(b.sems["DN"], dn0 + j - (NDB - 1))
                    b.dma(q, dbuf[j % NDB][:, :], io["sdap"](i, qq), "dB%d" % (j % 2))
    block.sync(f_sync)

    def f_pe(t):
        for s in range(8):
            for k in range(16):
                i = 16 * s + k
                t.wait_ge(b.sems["dA%d" % (i % 2)], wA(s, k))
                if i == 0:
                    t.wait_ge(b.sems["dX"], pre)
                if i >= 2:
                    t.wait_ge(b.sems["EV"], evv(i - 2, 1))
                if s >= 1 and k < 2:
                    t.wait_ge(b.sems["DA"], da0 + 32 * s)
                pb = 4 * (i % 2)
                for which, buf, bank0 in (("g", gbuf[i % 2], pb), ("u", ubuf[i % 2], pb + 2)):
                    for kk in range(HC):
                        for sp in range(2):
                            ins = t.matmul(PS[bank0 + sp][:, :], buf[:, kk * 128:(kk + 1) * 128],
                                           xn[:, kk * 1024 + sp * 512: kk * 1024 + (sp + 1) * 512],
                                           start=(kk == 0), stop=(kk == HC - 1))
                b.inc(ins, "GU", 1)
            t.wait_ge(b.sems["EV"], evv(16 * s + 15, 1))
            for qq in range(4):
                if not (s == 0 and qq == 0):
                    t.wait_ge(b.sems["DA"], da0 + 32 * s + 8 * qq)
                for k in range(16):
                    j = 64 * s + 16 * qq + k
                    if qq == 0:
                        t.wait_ge(b.sems["VE"], vev(16 * s + k, 2))
                    t.wait_ge(b.sems["dB%d" % (j % 2)], wB(s, qq, k))
                    for g8 in range(8):
                        h4, sp = g8 // 2, g8 % 2
                        ins = t.matmul(PS[g8][:, :], dbuf[j % NDB][:, h4 * 128:(h4 + 1) * 128],
                                       inter[:, k * 1024 + sp * 512: k * 1024 + (sp + 1) * 512],
                                       start=(k == 0), stop=(k == 15))
                    b.inc(ins, "DN", 1)
    block.tensor(f_pe)

    def f_act(s):
        if xn_src is not None:
            for c in range(HC):
                b.dma(s, xn[:, c * 1024:(c + 1) * 1024], xn_src[c * 128:(c + 1) * 128, :], "dX")
        for ss in range(8):
            for k in range(16):
                i = 16 * ss + k
                s.wait_ge(b.sems["GU"], gu0 + i + 1)
                if i >= 1:
                    s.wait_ge(b.sems["PL"], plv(i - 1, 3))
                pb = 4 * (i % 2)
                s.copy(gf[:, 0:512], PS[pb][:, :])
                s.copy(gf[:, 512:1024], PS[pb + 1][:, :])
                s.copy(uf[:, 0:512], PS[pb + 2][:, :])
                b.inc(s.copy(uf[:, 512:1024], PS[pb + 3][:, :]), "EV", 1)
                s.wait_ge(b.sems["PL"], plv(i, 2))
                b.inc(s.activation(At[:, :], Ct[:, :], AF.Tanh), "EV", 1)
    block.scalar(f_act)

    def f_pool(g):
        if acc_init_src is not None:
            for c in range(HC):
                b.dma(g, acc[:, c * 1024:(c + 1) * 1024], acc_init_src[c * 128:(c + 1) * 128, :], "dC")
        for ss in range(8):
            for k in range(16):
                i = 16 * ss + k
                g.wait_ge(b.sems["EV"], evv(i, 1))
                if i >= 1:
                    g.wait_ge(b.sems["VE"], vev(i - 1, 2))
                b.inc(g.tensor_tensor(At[:, :], gf[:, :], gf[:, :], ALU.mult), "PL", 1)
                g.wait_ge(b.sems["VE"], vev(i, 1))
                b.inc(g.tensor_tensor(Ct[:, :], gf[:, :], Bt[:, :], ALU.mult), "PL", 1)
                g.wait_ge(b.sems["EV"], evv(i, 2))
                b.inc(g.tensor_tensor(Ct[:, :], gf[:, :], uf[:, :], ALU.mult), "PL", 1)
    block.gpsimd(f_pool)

    def f_dve(v):
        for ss in range(8):
            for k in range(16):
                i = 16 * ss + k
                v.wait_ge(b.sems["PL"], plv(i, 1))
                b.inc(v.tensor_scalar(Bt[:, :], At[:, :], GELU_C * GELU_A, GELU_C, ALU.mult, ALU.add),
                      "VE", 1)
                v.wait_ge(b.sems["EV"], evv(i, 2))
                v.tensor_scalar(Bt[:, :], At[:, :], 0.5, 0.5, ALU.mult, ALU.add)
                v.wait_ge(b.sems["PL"], plv(i, 3))
                if ss >= 1:
                    v.wait_ge(b.sems["DN"], dn0 + 64 * ss)
                b.inc(v.tensor_tensor(inter[:, i % 16 * 1024:(i % 16 + 1) * 1024],
                                      Ct[:, :], Bt[:, :], ALU.mult), "VE", 1)
            if acc_init_src is not None and ss == 0:
                v.wait_ge(b.sems["dC"], prec)
            for qq in range(4):
                v.wait_ge(b.sems["DN"], dn0 + 64 * ss + 16 * (qq + 1))
                for g8 in range(8):
                    h4, sp = g8 // 2, g8 % 2
                    col = (4 * qq + h4) * 1024 + sp * 512
                    b.inc(v.tensor_tensor(acc[:, col:col + 512], acc[:, col:col + 512],
                                          PS[g8][:, :], ALU.add), "DA", 1)
    block.vector(f_dve)

    b.barrier()



def _emit_ln_sbuf(b, io, ar, src, dst_ap, r1o_ap, eps):
    nc, block = b.nc, b.block
    PS = io["PS"]
    es = ExitStack()
    u = b.uid()
    sb = lambda nm, shape, dt=F32: es.enter_context(nc.sbuf_tensor(f"{nm}_{u}", shape, dt))
    sq4 = [sb(f"l2_sq{i}", [128, 4096]) for i in range(2)]
    m_row = sb("l2_m", [1, 1024])
    s_row = sb("l2_s", [1, 1024])
    t_row = sb("l2_t", [1, 1024])
    w_row = sb("l2_w", [1, 1024])
    m_bc = sb("l2_mbc", [128, 4096])
    r_bc = sb("l2_rbc", [128, 4096])
    mh = sb("l2_mh", [1, 1024], BF16)
    ml = sb("l2_ml", [1, 1024], BF16)
    rh = sb("l2_rh", [1, 1024], BF16)
    rl = sb("l2_rl", [1, 1024], BF16)
    xb = [sb(f"l2_xb{i}", [128, 4096], BF16) for i in range(2)]

    for s_ in ("dS0", "dS1", "PE", "EV", "VE"):
        b.sem(s_)
    pe0 = b.cnt["PE"]; ev0 = b.cnt["EV"]; ve0 = b.cnt["VE"]
    sb0 = [b.cnt["dS0"], b.cnt["dS1"]]
    per_st = 8 if r1o_ap is not None else 4
    sv = lambda qi: sb0[qi % 2] + 16 * per_st * (qi // 2 + 1)

    def f_act(s):
        for qi in range(4):
            if qi >= 2:
                s.wait_ge(b.sems["PE"], pe0 + qi - 1)
            b.inc(s.activation(sq4[qi % 2][:, :], src[:, qi * 4096:(qi + 1) * 4096],
                               AF.Square), "EV", 1)
        s.wait_ge(b.sems["VE"], ve0 + 1)
        s.activation(t_row[:, :], s_row[:, :], AF.Ln)
        b.inc(s.activation(t_row[:, :], t_row[:, :], AF.Exp, scale=-0.5), "EV", 1)
        for which in range(2):
            s.wait_ge(b.sems["PE"], pe0 + 5 + which)
            dstb = m_bc if which == 0 else r_bc
            for t in range(8):
                ins = s.copy(dstb[:, t * 512:(t + 1) * 512], PS[4 + t % 2][:, :])
            b.inc(ins, "EV", 1)
    block.scalar(f_act)

    def f_pe(t_):
        for qi in range(4):
            t_.wait_ge(b.sems["EV"], ev0 + qi + 1)
            ins = None
            for st in range(8):
                c, half = st // 2, st % 2
                sl = slice(c * 1024 + half * 512, c * 1024 + (half + 1) * 512)
                t_.matmul(PS[half][0:1, :], io["s_ones_col"][:, :],
                          src[:, qi * 4096 + c * 1024 + half * 512:
                              qi * 4096 + c * 1024 + (half + 1) * 512],
                          start=(qi == 0 and c == 0), stop=(qi == 3 and c == 3))
                ins = t_.matmul(PS[2 + half][0:1, :], io["s_ones_col"][:, :],
                                sq4[qi % 2][:, c * 1024 + half * 512:
                                            c * 1024 + (half + 1) * 512],
                                start=(qi == 0 and c == 0), stop=(qi == 3 and c == 3))
            b.inc(ins, "PE", 1)
        t_.wait_ge(b.sems["VE"], ve0 + 2)
        for half in range(2):
            sl = slice(half * 512, (half + 1) * 512)
            t_.matmul(PS[4 + half][:, :], io["s_onesbf"][:, :], mh[:, sl], start=True, stop=False)
            ins = t_.matmul(PS[4 + half][:, :], io["s_onesbf"][:, :], ml[:, sl],
                            start=False, stop=True)
        b.inc(ins, "PE", 1)
        t_.wait_ge(b.sems["EV"], ev0 + 6)
        for half in range(2):
            sl = slice(half * 512, (half + 1) * 512)
            t_.matmul(PS[4 + half][:, :], io["s_onesbf"][:, :], rh[:, sl], start=True, stop=False)
            ins = t_.matmul(PS[4 + half][:, :], io["s_onesbf"][:, :], rl[:, sl],
                            start=False, stop=True)
        b.inc(ins, "PE", 1)
    block.tensor(f_pe)

    def f_dve(v):
        v.wait_ge(b.sems["PE"], pe0 + 4)
        for half in range(2):
            sl = slice(half * 512, (half + 1) * 512)
            v.tensor_scalar(m_row[:, sl], PS[half][0:1, :], 1.0 / D, None, ALU.mult)
            v.tensor_scalar(s_row[:, sl], PS[2 + half][0:1, :], 1.0 / D, None, ALU.mult)
        v.tensor_tensor(t_row[:, :], m_row[:, :], m_row[:, :], ALU.mult)
        v.tensor_tensor(s_row[:, :], s_row[:, :], t_row[:, :], ALU.subtract)
        b.inc(v.tensor_scalar(s_row[:, :], s_row[:, :], eps, None, ALU.add), "VE", 1)
        v.wait_ge(b.sems["EV"], ev0 + 5)
        v.tensor_copy(mh[:, :], m_row[:, :])
        v.tensor_tensor(w_row[:, :], m_row[:, :], mh[:, :], ALU.subtract)
        v.tensor_copy(ml[:, :], w_row[:, :])
        v.tensor_copy(rh[:, :], t_row[:, :])
        v.tensor_tensor(w_row[:, :], t_row[:, :], rh[:, :], ALU.subtract)
        b.inc(v.tensor_copy(rl[:, :], w_row[:, :]), "VE", 1)
        v.wait_ge(b.sems["EV"], ev0 + 7)
        for qi in range(4):
            if qi >= 2:
                v.wait_ge(b.sems["dS%d" % (qi % 2)], sv(qi - 2))
            v.tensor_tensor(sq4[qi % 2][:, :], src[:, qi * 4096:(qi + 1) * 4096],
                            m_bc[:, :], ALU.subtract)
            b.inc(v.tensor_tensor(xb[qi % 2][:, :], sq4[qi % 2][:, :], r_bc[:, :],
                                  ALU.mult), "VE", 1)
    block.vector(f_dve)

    def f_store(s):
        for qi in range(4):
            s.wait_ge(b.sems["VE"], ve0 + 3 + qi)
            for c in range(4):
                ch = qi * 4 + c
                b.dma(s, dst_ap[ch * 128:(ch + 1) * 128, :],
                      xb[qi % 2][:, c * 1024:(c + 1) * 1024], "dS%d" % (qi % 2))
                if r1o_ap is not None:
                    b.dma(s, r1o_ap[ch * 128:(ch + 1) * 128, :],
                          src[:, ch * 1024:(ch + 1) * 1024], "dS%d" % (qi % 2))
    block.sync(f_store)

    b.barrier()
    es.close()



def _emit_proj(b, io, K_own, V_own, Qr_sb, xn2_sb):
    nc, block = b.nc, b.block
    PS = io["PS"]
    es = ExitStack()
    u = b.uid()
    sb = lambda nm, shape, dt=BF16: es.enter_context(nc.sbuf_tensor(f"{nm}_{u}", shape, dt))
    wkb = [sb(f"wkb{i}", [128, 512]) for i in range(2)]
    wvb = [sb(f"wvb{i}", [128, 512]) for i in range(2)]
    wqb = [sb(f"wqb{i}", [128, 2048]) for i in range(2)]

    for s_ in ("dX", "dG0", "dG1", "PE", "EV"):
        b.sem(s_)
    pe0 = b.cnt["PE"]; ev0 = b.cnt["EV"]
    gb0 = [b.cnt["dG0"], b.cnt["dG1"]]
    gv = lambda li: gb0[li % 2] + 16 * (li // 2 + 1)

    def f_sync(q):
        for c in range(HC):
            b.dma(q, xn2_sb[:, c * 1024:(c + 1) * 1024], io["s_xn2"][c * 128:(c + 1) * 128, :], "dX")
    block.sync(f_sync)
    w_xn = b.cnt["dX"]

    jobs = [("k", 0), ("v", 0), ("q", 0), ("q", 1), ("q", 2), ("q", 3)]

    def f_gload(g):
        for jo, (kind, oc) in enumerate(jobs):
            for h in range(HC):
                li = jo * HC + h
                if li >= 2:
                    g.wait_ge(b.sems["PE"], pe0 + li - 1)
                if kind == "k":
                    b.dma(g, wkb[h % 2][:, :], io["wk"][h * 128:(h + 1) * 128, :], "dG%d" % (li % 2))
                elif kind == "v":
                    b.dma(g, wvb[h % 2][:, :], io["wv"][h * 128:(h + 1) * 128, :], "dG%d" % (li % 2))
                else:
                    b.dma(g, wqb[h % 2][:, :], io["wq"][h * 128:(h + 1) * 128, :], "dG%d" % (li % 2))
    block.gpsimd(f_gload)

    def f_pe(t):
        t.wait_ge(b.sems["dX"], w_xn)
        for jo, (kind, oc) in enumerate(jobs):
            if jo >= 1:
                t.wait_ge(b.sems["EV"], ev0 + 8 * jo)
            for h in range(HC):
                li = jo * HC + h
                t.wait_ge(b.sems["dG%d" % (li % 2)], gv(li))
                ins = None
                for g8 in range(8):
                    if kind == "k":
                        gg, sp = g8 // 2, g8 % 2
                        lhsT = wkb[h % 2][:, gg * 128:(gg + 1) * 128]
                        rhs = xn2_sb[:, h * 1024 + sp * 512: h * 1024 + (sp + 1) * 512]
                    elif kind == "v":
                        tc = g8
                        lhsT = xn2_sb[:, h * 1024 + tc * 128: h * 1024 + (tc + 1) * 128]
                        rhs = wvb[h % 2][:, :]
                    else:
                        dq, sp = (oc * 8 + g8) // 2, (oc * 8 + g8) % 2
                        lhsT = wqb[h % 2][:, dq * 128:(dq + 1) * 128]
                        rhs = xn2_sb[:, h * 1024 + sp * 512: h * 1024 + (sp + 1) * 512]
                    ins = t.matmul(PS[g8][:, :], lhsT, rhs, start=(h == 0), stop=(h == HC - 1))
                b.inc(ins, "PE", 1)
    block.tensor(f_pe)

    def f_act(s):
        for jo, (kind, oc) in enumerate(jobs):
            s.wait_ge(b.sems["PE"], pe0 + HC * (jo + 1))
            for g8 in range(8):
                if kind == "k":
                    gg, sp = g8 // 2, g8 % 2
                    dst = K_own[:, gg * 1024 + sp * 512: gg * 1024 + (sp + 1) * 512]
                elif kind == "v":
                    tc = g8
                    dst = V_own[:, tc * 512:(tc + 1) * 512]
                else:
                    dq, sp = (oc * 8 + g8) // 2, (oc * 8 + g8) % 2
                    dst = Qr_sb[:, dq * 1024 + sp * 512: dq * 1024 + (sp + 1) * 512]
                b.inc(s.copy(dst, PS[g8][:, :]), "EV", 1)
    block.scalar(f_act)

    b.barrier()
    es.close()



def _emit_rms_rope(b, io, K_own, Qr_sb, Qn_sb):
    nc, block = b.nc, b.block
    PS = io["PS"]
    es = ExitStack()
    u = b.uid()
    sbf = lambda nm, shape, dt=F32: es.enter_context(nc.sbuf_tensor(f"{nm}_{u}", shape, dt))
    sqf = sbf("rr_sq", [128, 4096])
    tmp1 = sbf("rr_t1", [128, 4096])
    t_row = sqf[0:1, :]
    w_row = tmp1[0:1, :]
    rh = sbf("rr_rh", [1, 4096], BF16)
    rl = sbf("rr_rl", [1, 4096], BF16)
    rbc = sbf("rr_rbc", [128, 4096])
    s_row = rbc[0:1, :]
    qb = sbf("rr_qb", [128, 4096], BF16)

    for s_ in ("PE", "EV", "VE"):
        b.sem(s_)
    pe0 = b.cnt["PE"]; ev0 = b.cnt["EV"]; ve0 = b.cnt["VE"]
    units = [("k", 0)] + [("q", bq) for bq in range(4)]
    evA = lambda i: ev0 + 4 * i + 1
    evB = lambda i: ev0 + 4 * i + 2
    evC1 = lambda i: ev0 + 4 * i + 3
    evC2 = lambda i: ev0 + 4 * i + 4
    peS1 = lambda i: pe0 + 6 * i + 1
    peS2 = lambda i: pe0 + 6 * i + 2
    peB1 = lambda i: pe0 + 6 * i + 3
    peB2 = lambda i: pe0 + 6 * i + 4
    peC1 = lambda i: pe0 + 6 * i + 5
    peC2 = lambda i: pe0 + 6 * i + 6
    veR1 = lambda i: ve0 + 6 * i + 1
    veA = lambda i: ve0 + 6 * i + 2
    veB = lambda i: ve0 + 6 * i + 3
    veC = lambda i: ve0 + 6 * i + 4
    veD1 = lambda i: ve0 + 6 * i + 5
    veD2 = lambda i: ve0 + 6 * i + 6

    def usrc(kind, idx):
        return K_own[:, :] if kind == "k" else Qr_sb[:, idx * 4096:(idx + 1) * 4096]

    def udst(kind, idx):
        return K_own[:, :] if kind == "k" else Qn_sb[:, idx * 4096:(idx + 1) * 4096]

    def f_act(s):
        for ui, (kind, idx) in enumerate(units):
            if ui >= 1:
                s.wait_ge(b.sems["PE"], peS2(ui - 1))
                s.wait_ge(b.sems["VE"], veB(ui - 1))
            b.inc(s.activation(sqf[:, :], usrc(kind, idx), AF.Square), "EV", 1)
            s.wait_ge(b.sems["VE"], veA(ui))
            s.activation(t_row[:, :], s_row[:, :], AF.Ln)
            b.inc(s.activation(t_row[:, :], t_row[:, :], AF.Exp, scale=-0.5), "EV", 1)
            for rnd in range(2):
                s.wait_ge(b.sems["PE"], peB1(ui) if rnd == 0 else peB2(ui))
                q4 = 4 * (ui % 2)
                for t in range(4):
                    ins = s.copy(rbc[:, (4 * rnd + t) * 512:(4 * rnd + t + 1) * 512],
                                 PS[q4 + t][:, :])
                b.inc(ins, "EV", 1)
    block.scalar(f_act)

    def f_pe(t_):
        for ui, (kind, idx) in enumerate(units):
            q4 = 4 * (ui % 2)
            t_.wait_ge(b.sems["EV"], evA(ui))
            if ui >= 2:
                t_.wait_ge(b.sems["VE"], veD2(ui - 2))
            for rnd in range(2):
                if rnd == 1:
                    t_.wait_ge(b.sems["VE"], veR1(ui))
                ins = None
                for t in range(4):
                    ins = t_.matmul(PS[q4 + t][0:1, :], io["s_ones_col"][:, :],
                                    sqf[:, (4 * rnd + t) * 512:(4 * rnd + t + 1) * 512],
                                    start=True, stop=True)
                b.inc(ins, "PE", 1)
            t_.wait_ge(b.sems["VE"], veB(ui))
            for rnd in range(2):
                if rnd == 1:
                    t_.wait_ge(b.sems["EV"], evC1(ui))
                for t in range(4):
                    sl = slice((4 * rnd + t) * 512, (4 * rnd + t + 1) * 512)
                    t_.matmul(PS[q4 + t][:, :], io["s_onesbf"][:, :], rh[:, sl],
                              start=True, stop=False)
                    ins = t_.matmul(PS[q4 + t][:, :], io["s_onesbf"][:, :], rl[:, sl],
                                    start=False, stop=True)
                b.inc(ins, "PE", 1)
            t_.wait_ge(b.sems["VE"], veC(ui))
            t_.wait_ge(b.sems["EV"], evC2(ui))
            for rnd in range(2):
                if rnd == 1:
                    t_.wait_ge(b.sems["VE"], veD1(ui))
                ins = None
                for t in range(4):
                    ins = t_.matmul(PS[q4 + t][:, :], io["s_sswap"][:, :],
                                    qb[:, (4 * rnd + t) * 512:(4 * rnd + t + 1) * 512],
                                    start=True, stop=True)
                b.inc(ins, "PE", 1)
    block.tensor(f_pe)

    def f_dve(v):
        for ui, (kind, idx) in enumerate(units):
            q4 = 4 * (ui % 2)
            for rnd in range(2):
                v.wait_ge(b.sems["PE"], peS1(ui) if rnd == 0 else peS2(ui))
                ins = None
                for t in range(4):
                    ins = v.tensor_copy(s_row[:, (4 * rnd + t) * 512:(4 * rnd + t + 1) * 512],
                                        PS[q4 + t][0:1, :])
                if rnd == 0:
                    b.inc(ins, "VE", 1)
            b.inc(v.tensor_scalar(s_row[:, :], s_row[:, :], 1.0 / DH, RMS_EPS,
                                  ALU.mult, ALU.add), "VE", 1)
            v.wait_ge(b.sems["EV"], evB(ui))
            if kind == "q":
                v.tensor_scalar(t_row[:, :], t_row[:, :], DH ** -0.5, None, ALU.mult)
            v.tensor_copy(rh[:, :], t_row[:, :])
            v.tensor_tensor(w_row[:, :], t_row[:, :], rh[:, :], ALU.subtract)
            b.inc(v.tensor_copy(rl[:, :], w_row[:, :]), "VE", 1)
            v.wait_ge(b.sems["EV"], evC2(ui))
            v.tensor_tensor(tmp1[:, :], usrc(kind, idx), rbc[:, :], ALU.mult)
            ins = None
            for sub in range(4):
                gcol = io["s_gkt"][:, sub:sub + 1] if kind == "k" else \
                    io["s_gqt"][:, idx * 4 + sub: idx * 4 + sub + 1]
                ins = v.tensor_scalar(qb[:, sub * 1024:(sub + 1) * 1024],
                                      tmp1[:, sub * 1024:(sub + 1) * 1024], gcol, None, ALU.mult)
            b.inc(ins, "VE", 1)
            dst = udst(kind, idx)
            for rnd in range(2):
                v.wait_ge(b.sems["PE"], peC1(ui) if rnd == 0 else peC2(ui))
                ins = None
                for t in range(4):
                    sl = slice((4 * rnd + t) * 512, (4 * rnd + t + 1) * 512)
                    tsl = slice(((4 * rnd + t) % 2) * 512, (((4 * rnd + t) % 2) + 1) * 512)
                    v.tensor_tensor(dst[:, sl], qb[:, sl], io["s_rcos"][:, tsl], ALU.mult)
                    v.tensor_tensor(tmp1[:, sl], PS[q4 + t][:, :], io["s_rsin"][:, tsl], ALU.mult)
                    ins = v.tensor_tensor(dst[:, sl], dst[:, sl], tmp1[:, sl], ALU.add)
                b.inc(ins, "VE", 1)
    block.vector(f_dve)

    es.close()
    return veD2(0)



def _emit_kv_exchange(b, io, K_own, V_own, K_sb, V_sb, k_done):
    nc, block = b.nc, b.block
    for s_ in ("dX", "dS0", "CK", "VE"):
        b.sem(s_)

    def f_sync(q):
        q.wait_ge(b.sems["VE"], k_done)
        for g in range(NG):
            b.dma(q, io["kv_loc"][g * 128:(g + 1) * 128, :],
                  K_own[:, g * 1024:(g + 1) * 1024], "dS0")
        for tc in range(8):
            b.dma(q, io["kv_loc"][512 + tc * 128: 512 + (tc + 1) * 128, 0:512],
                  V_own[:, tc * 512:(tc + 1) * 512], "dS0")
    block.sync(f_sync)
    st = b.cnt["dS0"]

    def f_pool(gq):
        gq.wait_ge(b.sems["dS0"], st)
        gq.collective_compute(
            "AllGather", ALU.bypass,
            replica_groups=[[0, 1], [2, 3], [4, 5], [6, 7]],
            ins=[io["kv_loc"].opt()], outs=[io["kv_glob"].opt()],
        ).then_inc(b.sems["CK"])
        b.cnt["CK"] = max(b.cnt["CK"], 1)
        gq.wait_ge(b.sems["CK"], 1)
        for g in range(NG):
            for s in range(16):
                m, ls = s % 2, s // 2
                b.dma(gq, K_sb[:, g * 2048 + s * 128: g * 2048 + (s + 1) * 128],
                      io["kv_glob"][m * 1536 + g * 128: m * 1536 + (g + 1) * 128,
                                    ls * 128:(ls + 1) * 128], "dX")
        for s in range(16):
            m, ls = s % 2, s // 2
            b.dma(gq, V_sb[:, s * 512:(s + 1) * 512],
                  io["kv_glob"][m * 1536 + 512 + ls * 128: m * 1536 + 512 + (ls + 1) * 128,
                                0:512], "dX")
    block.gpsimd(f_pool)

    b.barrier()



def _emit_attn_wo(b, io, K_sb, V_sb, Qn_sb, res2):
    nc, block = b.nc, b.block
    PS = io["PS"]
    PT6 = PS[6][:, :].bitcast(BF16)
    PT7 = PS[7][:, :].bitcast(BF16)
    es = ExitStack()
    u = b.uid()
    sbf = lambda nm, shape, dt=F32: es.enter_context(nc.sbuf_tensor(f"{nm}_{u}", shape, dt))
    sc_sb = [sbf(f"at_sc{i}", [128, 2048]) for i in range(2)]
    prb = [sbf(f"at_prb{i}", [128, 2048], BF16) for i in range(2)]
    prb2 = [sbf(f"at_prb2{i}", [128, 2048], BF16) for i in range(2)]
    pt_sb = [sbf(f"at_pt{i}", [128, 2048], BF16) for i in range(2)]
    nm_c = [sbf(f"at_nm{i}", [128, 1]) for i in range(2)]
    ssum = [sbf(f"at_ss{i}", [128, 1]) for i in range(2)]
    rs_c = [sbf(f"at_rs{i}", [128, 1]) for i in range(2)]
    wob = [sbf(f"at_wo{i}", [128, 2048], BF16) for i in range(2)]

    for s_ in ("dX", "dG0", "dG1", "PE", "EV", "VE", "TR"):
        b.sem(s_)
    pe0 = b.cnt["PE"]; ev0 = b.cnt["EV"]; ve0 = b.cnt["VE"]; tr0 = b.cnt["TR"]

    its = []
    for g in range(NG):
        for h4 in range(HPG):
            for j in range(8):
                its.append((g, g * HPG + h4, j))

    NT = len(its)
    pe_sc = [0] * NT; pe_pv = [0] * NT; tr_b = [0] * NT
    pec, trc = pe0, tr0
    for tau in range(NT + 1):
        if tau < NT:
            pec += 1; pe_sc[tau] = pec
        if tau >= 1:
            trc += 2; tr_b[tau - 1] = trc
            pec += 1; pe_pv[tau - 1] = pec
    ve_ev = [0] * NT; ve_nm = [0] * NT; ve_pc = [0] * NT
    vec = ve0
    for tau in range(NT + 1):
        if tau < NT:
            vec += 1; ve_ev[tau] = vec
        if tau >= 1:
            vec += 1; ve_nm[tau - 1] = vec
            vec += 1; ve_pc[tau - 1] = vec
    ev_ex = [0] * NT; ev_pc = [0] * NT; ev_oe = [0] * NT
    evc = ev0
    for tau in range(NT + 1):
        if tau < NT:
            evc += 1; ev_ex[tau] = evc
        if tau >= 1:
            evc += 1; ev_pc[tau - 1] = evc
            evc += 1; ev_oe[tau - 1] = evc

    def nblocks(j):
        return 2 * (j + 1)

    def kv_tc(j, i):
        return i

    def f_pe(t):
        for tau in range(NT + 1):
            if tau < NT:
                g, hh, j = its[tau]
                W = (j + 1) * 128
                if tau >= 1:
                    t.wait_ge(b.sems["VE"], ve_ev[tau - 1])
                qsl = Qn_sb[:, hh * 1024 + j * 128: hh * 1024 + (j + 1) * 128]
                segs = []
                ofs = 0
                while ofs < 2 * W:
                    wseg = min(512 - (ofs % 512), 2 * W - ofs)
                    segs.append((ofs, wseg, g * 2048 + ofs))
                    ofs += wseg
                ins = None
                for a, wseg, kc in segs:
                    ins = t.matmul(PS[a // 512][:, a % 512:(a % 512) + wseg], qsl,
                                   K_sb[:, kc:kc + wseg], start=True, stop=True)
                b.inc(ins, "PE", 1)
            if tau >= 1:
                g, hh, j = its[tau - 1]
                nb = nblocks(j)
                p = (tau - 1) % 2
                t.wait_ge(b.sems["VE"], ve_nm[tau - 1])
                if tau >= 2:
                    t.wait_ge(b.sems["EV"], ev_pc[tau - 2])
                for bi in range(nb):
                    pv = PT6 if bi < 8 else PT7
                    sl = (bi % 8) * 128
                    ins = t.transpose(pv[:, sl:sl + 128], prb2[p][:, bi * 128:(bi + 1) * 128],
                                      io["s_ident"][:, :])
                    if bi == min(nb, 8) - 1:
                        b.inc(ins, "TR", 1)
                if nb <= 8:
                    b.inc(t.nop(), "TR", 1)
                else:
                    b.inc(ins, "TR", 1)
                t.wait_ge(b.sems["VE"], ve_pc[tau - 1])
                t.wait_ge(b.sems["EV"], ev_pc[tau - 1])
                if tau >= 2:
                    t.wait_ge(b.sems["EV"], ev_oe[tau - 2])
                ins = None
                for bi in range(nb):
                    tc = kv_tc(j, bi)
                    ins = t.matmul(PS[5][:, 0:128],
                                   V_sb[:, tc * 512 + g * 128: tc * 512 + (g + 1) * 128],
                                   pt_sb[p][:, bi * 128:(bi + 1) * 128],
                                   start=(bi == 0), stop=(bi == nb - 1))
                b.inc(ins, "PE", 1)
    block.tensor(f_pe)

    def f_dve(v):
        for tau in range(NT + 1):
            if tau < NT:
                g, hh, j = its[tau]
                nb = nblocks(j)
                W = (j + 1) * 128
                p = tau % 2
                v.wait_ge(b.sems["PE"], pe_sc[tau])
                for bi in range(nb):
                    a = bi * 128
                    seg = PS[a // 512][:, a % 512:(a % 512) + 128]
                    if bi == nb - 2:
                        v.tensor_tensor(sc_sb[p][:, a:a + 128], seg, io["s_mask_a"][:, :], ALU.add)
                    elif bi == nb - 1:
                        v.tensor_tensor(sc_sb[p][:, a:a + 128], seg, io["s_mask_b"][:, :], ALU.add)
                    else:
                        v.tensor_copy(sc_sb[p][:, a:a + 128], seg)
                b.inc(v.reduce_max(nm_c[p][:, :], sc_sb[p][:, 0:2 * W], axis=AX, negate=True),
                      "VE", 1)
            if tau >= 1:
                g, hh, j = its[tau - 1]
                nb = nblocks(j)
                W = (j + 1) * 128
                p = (tau - 1) % 2
                v.wait_ge(b.sems["EV"], ev_ex[tau - 1])
                v.reciprocal(rs_c[p][:, :], ssum[p][:, :])
                b.inc(v.tensor_scalar(prb2[p][:, 0:2 * W], prb[p][:, 0:2 * W], rs_c[p][:, :],
                                      None, ALU.mult), "VE", 1)
                v.wait_ge(b.sems["TR"], tr_b[tau - 1])
                ins = None
                for bi in range(min(nb, 8)):
                    ins = v.tensor_copy(pt_sb[p][:, bi * 128:(bi + 1) * 128],
                                        PT6[:, (bi % 8) * 128:(bi % 8) * 128 + 128])
                b.inc(ins, "VE", 1)
    block.vector(f_dve)

    def f_act(s):
        for tau in range(NT + 1):
            if tau < NT:
                g, hh, j = its[tau]
                W = (j + 1) * 128
                p = tau % 2
                s.wait_ge(b.sems["VE"], ve_ev[tau])
                b.inc(s.activation(prb[p][:, 0:2 * W], sc_sb[p][:, 0:2 * W], AF.Exp,
                                   bias=nm_c[p][:, :], accum_out=ssum[p][:, :]), "EV", 1)
            if tau >= 1:
                g, hh, j = its[tau - 1]
                nb = nblocks(j)
                p = (tau - 1) % 2
                s.wait_ge(b.sems["TR"], tr_b[tau - 1])
                ins = s.nop()
                for bi in range(8, nb):
                    ins = s.copy(pt_sb[p][:, bi * 128:(bi + 1) * 128],
                                 PT7[:, (bi % 8) * 128:(bi % 8) * 128 + 128])
                b.inc(ins, "EV", 1)
                s.wait_ge(b.sems["PE"], pe_pv[tau - 1])
                b.inc(s.copy(Qn_sb[:, hh * 1024 + j * 128: hh * 1024 + (j + 1) * 128],
                             PS[5][:, 0:128]), "EV", 1)
    block.scalar(f_act)

    b.barrier()

    pe1 = b.cnt["PE"]; ve1 = b.cnt["VE"]
    gb1 = [b.cnt["dG0"], b.cnt["dG1"]]
    gv1 = lambda li: gb1[li % 2] + 16 * (li // 2 + 1)

    def f_sync(q):
        for c in range(HC):
            b.dma(q, res2[:, c * 1024:(c + 1) * 1024], io["s_r1o"][c * 128:(c + 1) * 128, :], "dX")
    block.sync(f_sync)
    w_r = b.cnt["dX"]

    def f_gload(g):
        for oc in range(4):
            for d in range(HC):
                li = oc * HC + d
                if li >= 2:
                    g.wait_ge(b.sems["PE"], pe1 + li - 1)
                b.dma(g, wob[d % 2][:, :], io["wo"][d * 128:(d + 1) * 128, :], "dG%d" % (li % 2))
    block.gpsimd(f_gload)

    def f_pe2(t):
        for oc in range(4):
            if oc >= 1:
                t.wait_ge(b.sems["VE"], ve1 + 8 * oc)
            for d in range(HC):
                li = oc * HC + d
                t.wait_ge(b.sems["dG%d" % (li % 2)], gv1(li))
                ins = None
                for g8 in range(8):
                    ho, sp = (oc * 8 + g8) // 2, (oc * 8 + g8) % 2
                    ins = t.matmul(PS[g8][:, :], wob[d % 2][:, ho * 128:(ho + 1) * 128],
                                   Qn_sb[:, d * 1024 + sp * 512: d * 1024 + (sp + 1) * 512],
                                   start=(d == 0), stop=(d == HC - 1))
                b.inc(ins, "PE", 1)
    block.tensor(f_pe2)

    def f_dve2(v):
        v.wait_ge(b.sems["dX"], w_r)
        for oc in range(4):
            v.wait_ge(b.sems["PE"], pe1 + HC * (oc + 1))
            for g8 in range(8):
                ho, sp = (oc * 8 + g8) // 2, (oc * 8 + g8) % 2
                col = ho * 1024 + sp * 512
                b.inc(v.tensor_tensor(res2[:, col:col + 512], res2[:, col:col + 512],
                                      PS[g8][:, :], ALU.add), "VE", 1)
    block.vector(f_dve2)

    b.barrier()
    es.close()



def _build(debug=None):
    nc = bass.Bass("TRN2", target_bir_lowering=False)
    io = {}
    dt_in = [
        ("x", [D, T], F32), ("wq", [D, D], F32), ("wk", [D, DKV], F32), ("wv", [D, DKV], F32),
        ("wo", [D, D], F32), ("gqt", [128, 16], F32), ("gkt", [128, 4], F32),
        ("gpT", [1024, 2048], I32), ("gsT", [64, 2048], F32),
        ("upT", [1024, 2048], I32), ("usT", [64, 2048], F32),
        ("dpT", [1024, D], I32), ("dsT", [64, D], F32),
        ("ident", [128, 128], BF16), ("ones_col", [128, 1], F32), ("ones_row", [1, 128], F32),
        ("ones_row_bf", [1, 128], BF16),
        ("emat", [8, 128], F32), ("emat_h", [8, 128], F32),
        ("mask_a", [128, 128], F32), ("mask_b", [128, 128], F32),
        ("rcos", [128, T], F32), ("rsin", [128, T], F32), ("sswap", [128, 128], BF16),
    ]
    for nm, shape, dt in dt_in:
        io[nm] = nc.dram_tensor(nm, shape, dt, kind="ExternalInput").ap()
    io["y"] = nc.dram_tensor("y", [D, TO], F32, kind="ExternalOutput").ap()
    io["dq_loc"] = nc.dram_tensor("dq_loc", [6144, 2048], BF16, kind="Internal").ap()
    for fam in ("g0", "g1", "u0", "u1", "d0", "d1"):
        io["dqg_" + fam] = nc.dram_tensor("dqg_" + fam, [8192, 2048], BF16, kind="Internal",
                                          addr_space="Shared").ap()

    def _sgap(i, h2):
        r0 = 1024 * (i // 16)
        c0 = (i % 16) * 128
        return io["dqg_g%d" % h2][r0:r0 + 1024, c0:c0 + 128].rearrange("(k p) c -> p k c", k=8)

    def _suap(i, h2):
        r0 = 1024 * (i // 16)
        c0 = (i % 16) * 128
        return io["dqg_u%d" % h2][r0:r0 + 1024, c0:c0 + 128].rearrange("(k p) c -> p k c", k=8)

    def _sdap(i, qq):
        c = i % 16
        r0 = 1024 * (i // 16) + (c % 8) * 128
        return io["dqg_d%d" % (c // 8)][r0:r0 + 128, qq * 512:(qq + 1) * 512]

    io["sgap"] = _sgap
    io["suap"] = _suap
    io["sdap"] = _sdap
    io["s_xn1"] = nc.dram_tensor("s_xn1", [D, T], BF16, kind="Internal").ap()
    io["s_xn2"] = nc.dram_tensor("s_xn2", [D, T], BF16, kind="Internal").ap()
    io["s_r1o"] = nc.dram_tensor("s_r1o", [D, TO], F32, kind="Internal").ap()
    io["s_xn3"] = nc.dram_tensor("s_xn3", [D, TO], BF16, kind="Internal").ap()
    io["kv_loc"] = nc.dram_tensor("kv_loc", [1536, 1024], BF16, kind="Internal").ap()
    io["kv_glob"] = nc.dram_tensor("kv_glob", [3072, 1024], BF16, kind="Internal").ap()
    if debug == "dequant":
        io["dbg_dq"] = nc.dram_tensor("dbg_dq", [49152, 2048], BF16, kind="ExternalOutput").ap()
    if debug == "ffn1":
        io["dbg_xn2"] = nc.dram_tensor("dbg_xn2", [D, T], BF16, kind="ExternalOutput").ap()
        io["dbg_r1o"] = nc.dram_tensor("dbg_r1o", [D, TO], F32, kind="ExternalOutput").ap()
    if debug == "attn":
        io["dbg_res2"] = nc.dram_tensor("dbg_res2", [D, TO], F32, kind="ExternalOutput").ap()
    if debug == "ln1":
        io["dbg_xn1"] = nc.dram_tensor("dbg_xn1", [D, T], BF16, kind="ExternalOutput").ap()
        io["dbg_mbc"] = nc.dram_tensor("dbg_mbc", [128, T], F32, kind="ExternalOutput").ap()
        io["dbg_rbc"] = nc.dram_tensor("dbg_rbc", [128, T], F32, kind="ExternalOutput").ap()

    es = ExitStack()
    block = es.enter_context(nc.Block())
    b = Build(nc, es, block)
    sbc = lambda nm, shape, dt=F32: es.enter_context(nc.sbuf_tensor(nm, shape, dt))
    consts = [("s_ident", "ident", [128, 128], BF16), ("s_ones_col", "ones_col", [128, 1], F32),
              ("s_ones_row", "ones_row", [1, 128], F32), ("s_onesbf", "ones_row_bf", [1, 128], BF16),
              ("s_emat", "emat", [8, 128], F32),
              ("s_emat_h", "emat_h", [8, 128], F32), ("s_gqt", "gqt", [128, 16], F32),
              ("s_gkt", "gkt", [128, 4], F32), ("s_mask_a", "mask_a", [128, 128], F32),
              ("s_mask_b", "mask_b", [128, 128], F32), ("s_rcos", "rcos", [128, T], F32),
              ("s_rsin", "rsin", [128, T], F32), ("s_sswap", "sswap", [128, 128], BF16)]
    for snm, inm, shape, dt in consts:
        io[snm] = sbc(snm, shape, dt)
    io["PS"] = [es.enter_context(nc.psum_tensor(f"PS{i}", [128, 512], F32)) for i in range(8)]

    b.sem("dX")

    def f_const(q):
        for snm, inm, shape, dt in consts:
            b.dma(q, io[snm][:, :], io[inm][:, :], "dX")
    block.sync(f_const)
    b.barrier()

    _emit_dequant(b, io)

    esL = ExitStack()
    xarena = esL.enter_context(nc.sbuf_tensor("x_ln1", [128, HC * 1024], F32))

    def f_ldx(q):
        for c in range(HC):
            b.dma(q, xarena[:, c * 1024:(c + 1) * 1024], io["x"][c * 128:(c + 1) * 128, :], "dX")
    block.sync(f_ldx)
    b.barrier()
    _emit_ln_sbuf(b, io, {}, xarena, io["s_xn1"], None, LN_EPS)
    esL.close()

    if debug == "dequant":
        def f_dbg(q):
            q.wait_ge(b.sems["CC"], 6)
            for fi, fam in enumerate(("d0", "d1")):
                for blk in range(4):
                    b.dma(q, io["dbg_dq"][(fi * 4 + blk) * 2048:(fi * 4 + blk + 1) * 2048, :],
                          io["dqg_" + fam][blk * 2048:(blk + 1) * 2048, :], "dX")
            q.wait_ge(b.sems["dX"], b.cnt["dX"])
        block.sync(f_dbg)
        es.close()
        return nc

    if debug == "ln1":
        def f_dbg(q):
            b.dma(q, io["dbg_xn1"][:, :], io["s_xn1"][:, :], "dX")
            q.wait_ge(b.sems["dX"], b.cnt["dX"])
        block.sync(f_dbg)
        es.close()
        return nc

    esF = ExitStack()
    sbF = lambda nm, shape, dt=F32: esF.enter_context(nc.sbuf_tensor(nm, shape, dt))
    acc_a = sbF("acc0", [128, HC * 1024], F32)
    esT = ExitStack()
    sbT = lambda nm, shape, dt=F32: esT.enter_context(nc.sbuf_tensor(nm, shape, dt))
    ar = {
        "xn": sbT("xn0", [128, HC * 1024], BF16),
        "acc": acc_a,
        "inter": sbT("int0", [128, 16 * 1024], BF16),
        "gbuf": [sbT(f"gb0{i}", [128, 2048], BF16) for i in range(2)],
        "ubuf": [sbT(f"ub0{i}", [128, 2048], BF16) for i in range(2)],
        "dbuf": [sbT(f"db0{i}", [128, 512], BF16) for i in range(8)],
        "gf": sbT("gf0", [128, 1024], F32),
        "uf": sbT("uf0", [128, 1024], F32),
        "At": sbT("At0", [128, 1024], F32),
        "Bt": sbT("Bt0", [128, 1024], F32),
        "Ct": sbT("Ct0", [128, 1024], F32),
    }
    _emit_ffn_tblock(b, io, ar, io["s_xn1"], io["x"])
    esT.close()
    _emit_ln_sbuf(b, io, {}, acc_a, io["s_xn2"], io["s_r1o"], LN_EPS)
    esF.close()

    if debug == "ffn1":
        def f_dbg(q):
            b.dma(q, io["dbg_xn2"][:, :], io["s_xn2"][:, :], "dX")
            b.dma(q, io["dbg_r1o"][:, :], io["s_r1o"][:, :], "dX")
            q.wait_ge(b.sems["dX"], b.cnt["dX"])
        block.sync(f_dbg)
        es.close()
        return nc

    esA = ExitStack()
    sbA = lambda nm, shape, dt=BF16: esA.enter_context(nc.sbuf_tensor(nm, shape, dt))
    K_sb = sbA("K_sb", [128, NG * 2048])
    V_sb = sbA("V_sb", [128, 16 * 512])
    Qn_sb = sbA("Qn_sb", [128, NHEAD * 1024])
    esP = ExitStack()
    Qr_sb = esP.enter_context(nc.sbuf_tensor("Qr_sb", [128, NHEAD * 1024], BF16))
    K_own = esP.enter_context(nc.sbuf_tensor("K_own", [128, NG * 1024], BF16))
    V_own = esP.enter_context(nc.sbuf_tensor("V_own", [128, 8 * 512], BF16))
    esP2 = ExitStack()
    xn2_sb = esP2.enter_context(nc.sbuf_tensor("xn2_sb", [128, HC * 1024], BF16))
    _emit_proj(b, io, K_own, V_own, Qr_sb, xn2_sb)
    esP2.close()
    k_done = _emit_rms_rope(b, io, K_own, Qr_sb, Qn_sb)
    _emit_kv_exchange(b, io, K_own, V_own, K_sb, V_sb, k_done)
    esP.close()
    esR2 = ExitStack()
    res2 = esR2.enter_context(nc.sbuf_tensor("res2", [128, HC * 1024], F32, side="right"))
    _emit_attn_wo(b, io, K_sb, V_sb, Qn_sb, res2)
    esA.close()

    if debug == "attn":
        def f_dbg(q):
            for c in range(HC):
                b.dma(q, io["dbg_res2"][c * 128:(c + 1) * 128, :], res2[:, c * 1024:(c + 1) * 1024], "dX")
            q.wait_ge(b.sems["dX"], b.cnt["dX"])
        block.sync(f_dbg)
        es.close()
        return nc

    _emit_ln_sbuf(b, io, {}, res2, io["s_xn3"], None, LN_EPS)

    esF3 = ExitStack()
    sb3 = lambda nm, shape, dt=F32: esF3.enter_context(nc.sbuf_tensor(nm, shape, dt))
    ar3 = {
        "xn": sb3("xn3", [128, HC * 1024], BF16),
        "acc": res2,
        "inter": sb3("int3", [128, 16 * 1024], BF16),
        "gbuf": [sb3(f"gb3{i}", [128, 2048], BF16) for i in range(2)],
        "ubuf": [sb3(f"ub3{i}", [128, 2048], BF16) for i in range(2)],
        "dbuf": [sb3(f"db3{i}", [128, 512], BF16) for i in range(8)],
        "gf": sb3("gf3", [128, 1024], F32),
        "uf": sb3("uf3", [128, 1024], F32),
        "At": sb3("At3", [128, 1024], F32),
        "Bt": sb3("Bt3", [128, 1024], F32),
        "Ct": sb3("Ct3", [128, 1024], F32),
    }
    _emit_ffn_tblock(b, io, ar3, io["s_xn3"], None)
    esF3.close()

    def f_out(q):
        for c in range(HC):
            b.dma(q, io["y"][c * 128:(c + 1) * 128, :], res2[:, c * 1024:(c + 1) * 1024], "dX")
        q.wait_ge(b.sems["dX"], b.cnt["dX"])
    block.sync(f_out)
    esR2.close()

    es.close()
    return nc


_CACHE = {}


def kernel(**inputs):
    maps = _host_inputs(inputs)
    if "nc" not in _CACHE:
        _CACHE["nc"] = _build()
    nc = _CACHE["nc"]
    res = run_bass_kernel_spmd(nc, maps, core_ids=list(range(8)))
    return _host_output(res.results)


# version 33 (final)
# speedup vs baseline: 1.0250x; 1.0022x over previous
import numpy as np
from contextlib import ExitStack

import concourse.bass as bass
import concourse.mybir as mybir
from concourse.bass_utils import run_bass_kernel_spmd

F32 = mybir.dt.float32
BF16 = mybir.dt.bfloat16
I32 = mybir.dt.int32
AF = mybir.ActivationFunctionType
ALU = mybir.AluOpType
AX = mybir.AxisListType.X

D = 2048
T = 1024
TO = 1024
HC = D // 128
NM = 16384
NCH = NM // 128
DH = 128
NG = 4
HPG = 4
DKV = 512
NHEAD = 16
LN_EPS = 1e-5
RMS_EPS = 1e-6
GELU_C = 0.7978845608028654
GELU_A = 0.044715
NEG = -1e30



def _pair_perm(n):
    out = np.empty(n, np.int64)
    for bb in range(n // 256):
        out[bb * 256: bb * 256 + 128] = bb * 256 + 2 * np.arange(128)
        out[bb * 256 + 128: bb * 256 + 256] = bb * 256 + 2 * np.arange(128) + 1
    return out


PI = _pair_perm(D)
SIGMA = _pair_perm(NM)


def _slot_tokens(par):
    return np.concatenate([np.arange(s * 128, (s + 1) * 128)
                           for s in range(par, 16, 2)])


def _rope_tables(tok):
    pos = tok.astype(np.float64)[None, :]
    j = np.arange(64, dtype=np.float64)[:, None]
    ang = np.power(10000.0, -j / 64.0)
    rad = (pos * ang).astype(np.float32)
    return np.cos(rad).astype(np.float32), np.sin(rad).astype(np.float32)


def _host_inputs(inp):
    x = np.asarray(inp["L0_attn_x"], np.float32)
    wq = np.asarray(inp["L0_attn_wq"], np.float32)
    wk = np.asarray(inp["L0_attn_wk"], np.float32)
    wv = np.asarray(inp["L0_attn_wv"], np.float32)
    wo = np.asarray(inp["L0_attn_wo"], np.float32)
    gq = np.asarray(inp["L0_attn_gq"], np.float32)
    gk = np.asarray(inp["L0_attn_gk"], np.float32)
    gp = np.asarray(inp["L0_ffn_gate_W_packed"], np.int32)
    gs = np.asarray(inp["L0_ffn_gate_scales"], np.float32)
    up = np.asarray(inp["L0_ffn_up_W_packed"], np.int32)
    us = np.asarray(inp["L0_ffn_up_scales"], np.float32)
    dp = np.asarray(inp["L0_ffn_down_W_packed"], np.int32)
    ds = np.asarray(inp["L0_ffn_down_scales"], np.float32)

    C = np.ascontiguousarray
    import ml_dtypes
    gpT = gp[SIGMA, :].T
    gsT = gs[SIGMA, :].T
    upT = up[SIGMA, :].T
    usT = us[SIGMA, :].T
    dpT = dp[PI, :].T
    dsT = ds[PI, :].T
    shared = {
        "wq": C(wq[PI, :]),
        "wk": C(wk[PI, :]),
        "wv": C(wv[PI, :]),
        "wo": C(wo[:, PI]),
        "gqt": C(np.transpose(gq, (2, 0, 1)).reshape(128, 16)),
        "gkt": C(gk.T),
        "ident": np.eye(128, dtype=ml_dtypes.bfloat16),
        "ones_col": np.ones((128, 1), np.float32),
        "ones_row": np.ones((1, 128), np.float32),
        "ones_row_bf": np.ones((1, 128), dtype=ml_dtypes.bfloat16),
    }
    em = np.zeros((8, 128), np.float32)
    for p in range(128):
        em[p // 16, p] = 1.0
    shared["emat"] = -em
    shared["emat_h"] = -0.5 * em
    tri = np.where(np.arange(128)[None, :] > np.arange(128)[:, None],
                   np.float32(NEG), np.float32(0.0))
    sw = np.zeros((128, 128), dtype=ml_dtypes.bfloat16)
    for i in range(128):
        sw[(i + 64) % 128, i] = 1
    shared["sswap"] = sw

    maps = []
    for c in range(8):
        p, par = c // 2, c % 2
        tok = _slot_tokens(par)
        m = dict(shared)
        if par == 0:
            m["mask_a"] = C(tri)
            m["mask_b"] = np.full((128, 128), NEG, np.float32)
        else:
            m["mask_a"] = np.zeros((128, 128), np.float32)
            m["mask_b"] = C(tri)
        m["x"] = C(x[p].T[PI][:, tok])
        m["gpT"] = C(gpT[:, c * 2048:(c + 1) * 2048])
        m["gsT"] = C(gsT[:, c * 2048:(c + 1) * 2048])
        m["upT"] = C(upT[:, c * 2048:(c + 1) * 2048])
        m["usT"] = C(usT[:, c * 2048:(c + 1) * 2048])
        m["dpT"] = C(dpT[c * 1024:(c + 1) * 1024, :])
        m["dsT"] = C(dsT[c * 64:(c + 1) * 64, :])
        cos, sin = _rope_tables(tok)
        m["rcos"] = np.vstack([cos, cos])
        m["rsin"] = np.vstack([-sin, sin])
        maps.append(m)
    return maps


def _host_output(results):
    out = np.empty((4, 2048, 2048), np.float32)
    inv = np.empty(D, np.int64)
    inv[PI] = np.arange(D)
    for p in range(4):
        full = np.empty((2048, 2048), np.float32)
        for par in range(2):
            y = np.asarray(results[2 * p + par]["y"])
            for j in range(8):
                s = 2 * j + par
                full[:, s * 128:(s + 1) * 128] = y[:, j * 128:(j + 1) * 128]
        out[p] = full[inv].T
    return out



class Build:
    def __init__(self, nc, es, block):
        self.nc = nc
        self.es = es
        self.block = block
        self.cnt = {}
        self.sems = {}
        self.phase = 0
        self.nuid = 0

    def uid(self):
        self.nuid += 1
        return self.nuid

    def sem(self, name):
        if name not in self.sems:
            self.sems[name] = self.es.enter_context(self.nc.semaphore(name))
            self.cnt[name] = 0
        return self.sems[name]

    def inc(self, instr, name, amt):
        instr.then_inc(self.sems[name], amt)
        self.cnt[name] += amt
        return self.cnt[name]

    def dma(self, q, out, in_, name):
        return self.inc(q.dma_start(out=out, in_=in_), name, 16)

    def barrier(self):
        self.sem("PH")
        dma_names = [n for n in ("dX", "dA0", "dA1", "dB0", "dB1", "dG0", "dG1", "dS0", "dS1")
                     if n in self.sems]
        targets = {n: self.cnt[n] for n in dma_names}

        def _sync(eng):
            for n in dma_names:
                eng.wait_ge(self.sems[n], targets[n])
            self.inc(eng.nop(), "PH", 1)
        self.block.sync(_sync)

        for name in ("scalar", "vector", "gpsimd", "tensor"):
            def _mk(eng):
                self_inc = eng.nop().then_inc(self.sems["PH"], 1)
            getattr(self.block, name)(_mk)
            self.cnt["PH"] += 1
        tgt = self.cnt["PH"]

        def _wait(eng):
            eng.wait_ge(self.sems["PH"], tgt)
        for name in ("sync", "scalar", "vector", "gpsimd", "tensor"):
            getattr(self.block, name)(_wait)
        self.phase += 1



def _emit_dequant(b, io):
    nc, block = b.nc, b.block
    es = ExitStack()
    sb = lambda nm, shape, dt=F32: es.enter_context(nc.sbuf_tensor(nm, shape, dt))
    pk = [sb(f"dq_pk{i}", [128, 2048], I32) for i in range(2)]
    sT = [sb(f"dq_sT{i}", [8, 2048]) for i in range(2)]
    scl = [sb(f"dq_scl{i}", [128, 2048]) for i in range(2)]
    loi = [sb(f"dq_loi{i}", [128, 2048], I32) for i in range(2)]
    hii = [sb(f"dq_hii{i}", [128, 2048], I32) for i in range(2)]
    cf = [sb(f"dq_cf{i}", [128, 2048]) for i in range(4)]
    m8 = sb("dq_m8", [128, 2048])
    mt = sb("dq_mt", [128, 2048])
    t4h = sb("dq_t4h", [128, 2048])
    uuh = [sb(f"dq_uh{i}", [128, 2048]) for i in range(2)]
    t6 = [sb(f"dq_t6{i}", [128, 2048]) for i in range(2)]
    sgs = [sb(f"dq_sgs{i}", [128, 2048], BF16) for i in range(2)]
    mg = sb("dq_mg", [128, 2048])
    nb75 = sb("dq_n75", [128, 1])
    val = [sb(f"dq_val{i}", [128, 2048], BF16) for i in range(2)]
    PS = io["PS"]
    dq = io["dq_loc"]

    for s_ in ("dA0", "dA1", "dS0", "dS1", "PE", "EV", "VE", "PL", "CC"):
        b.sem(s_)
    ab = [b.cnt["dA0"], b.cnt["dA1"]]
    ob = [b.cnt["dS0"], b.cnt["dS1"]]
    pe0 = b.cnt["PE"]; ev0 = b.cnt["EV"]; ve0 = b.cnt["VE"]; pl0 = b.cnt["PL"]

    steps = []
    for kind, pT, sTa, ema in (("g", io["gpT"], io["gsT"], io["s_emat"]),
                               ("u", io["upT"], io["usT"], io["s_emat"]),
                               ("d", io["dpT"], io["dsT"], io["s_emat_h"])):
        for c2 in range(8):
            steps.append((kind, pT, sTa, ema, c2))
    NS = len(steps)

    wA = lambda i: ab[i % 2] + 32 * (i // 2 + 1)
    pe = lambda i: pe0 + i + 1
    evac = lambda i: ev0 + 5 * i + 1
    cvt = lambda i, j: ev0 + 5 * i + 2 + j
    evsg = lambda i, j: ev0 + 5 * i + 4 + j
    ve_setup = ve0 + 1
    ve_ext = [0] * NS; ve_nib = [0] * (2 * NS)
    vec = ve_setup
    vec += 1; ve_ext[0] = vec
    for i in range(NS):
        if i + 1 < NS:
            vec += 1; ve_ext[i + 1] = vec
        for j in range(2):
            vec += 1; ve_nib[2 * i + j] = vec
    ext = lambda i: ve_ext[i]
    nib = lambda g: ve_nib[g]
    plmg = lambda g: pl0 + 3 * g + 1
    plsg = lambda g: pl0 + 3 * g + 2
    plval = lambda g: pl0 + 3 * g + 3
    stc = [0] * (2 * NS)
    acc2 = [ob[0], ob[1]]
    for g in range(2 * NS):
        acc2[g % 2] += 16
        stc[g] = acc2[g % 2]

    def _stores(q, g):
        i, j = g // 2, g % 2
        kind, _, _, _, c2 = steps[i]
        kk = 2 * c2 + j
        base = {"g": 0, "u": 2048, "d": 4096}[kind]
        b.dma(q, dq[base + kk * 128: base + (kk + 1) * 128, :], val[j][:, :], "dS%d" % j)

    def f_sync(q):
        for i, (kind, pT, sTa, ema, c2) in enumerate(steps):
            if i >= 2:
                q.wait_ge(b.sems["VE"], ext(i - 2))
                q.wait_ge(b.sems["PE"], pe(i - 2))
            b.dma(q, pk[i % 2][:, :], pT[c2 * 128:(c2 + 1) * 128, :], "dA%d" % (i % 2))
            b.dma(q, sT[i % 2][:, :], sTa[c2 * 8:(c2 + 1) * 8, :], "dA%d" % (i % 2))
            if i >= 2:
                for j in range(2):
                    g = 2 * (i - 2) + j
                    q.wait_ge(b.sems["PL"], plval(g))
                    _stores(q, g)
        for g in (2 * NS - 4, 2 * NS - 3, 2 * NS - 2, 2 * NS - 1):
            q.wait_ge(b.sems["PL"], plval(g))
            _stores(q, g)
    block.sync(f_sync)

    def f_pe(t):
        for i in range(NS):
            ema = steps[i][3]
            t.wait_ge(b.sems["dA%d" % (i % 2)], wA(i))
            if i >= 2:
                t.wait_ge(b.sems["EV"], evac(i - 2))
            for s4 in range(4):
                ins = t.matmul(PS[4 * (i % 2) + s4][:, :], ema[:, :],
                               sT[i % 2][:, s4 * 512:(s4 + 1) * 512], start=True, stop=True)
            b.inc(ins, "PE", 1)
    block.tensor(f_pe)

    def f_act(s):
        for i in range(NS):
            s.wait_ge(b.sems["PE"], pe(i))
            if i >= 2:
                s.wait_ge(b.sems["PL"], plval(2 * (i - 2) + 1))
            for s4 in range(4):
                ins = s.copy(scl[i % 2][:, s4 * 512:(s4 + 1) * 512], PS[4 * (i % 2) + s4][:, :])
            b.inc(ins, "EV", 1)
            s.wait_ge(b.sems["VE"], ext(i))
            for j in range(2):
                if i >= 2:
                    s.wait_ge(b.sems["VE"], nib(2 * (i - 2) + j))
                b.inc(s.copy(cf[2 * (i % 2) + j][:, :],
                             (loi if j == 0 else hii)[i % 2][:, :]), "EV", 1)
            for j in range(2):
                if i == 0 and j == 0:
                    s.wait_ge(b.sems["VE"], ve_setup)
                if i >= 1:
                    s.wait_ge(b.sems["PL"], plsg(2 * (i - 1) + j))
                b.inc(s.activation(sgs[j][:, :], cf[2 * (i % 2) + j][:, :], AF.Sign,
                                   bias=nb75[:, :]), "EV", 1)
    block.scalar(f_act)

    def f_dve(v):
        b.inc(v.tensor_scalar(nb75[:, :], io["s_ones_col"][:, :], -7.5, None, ALU.mult),
              "VE", 1)
        def _ext(i):
            v.wait_ge(b.sems["dA%d" % (i % 2)], wA(i))
            if i >= 2:
                v.wait_ge(b.sems["EV"], cvt(i - 2, 1))
            v.tensor_scalar(loi[i % 2][:, :], pk[i % 2][:, :], 15, None, ALU.bitwise_and)
            b.inc(v.tensor_scalar(hii[i % 2][:, :], pk[i % 2][:, :], 4, None,
                                  ALU.logical_shift_right), "VE", 1)
        _ext(0)
        for i in range(NS):
            if i + 1 < NS:
                _ext(i + 1)
            for j in range(2):
                g = 2 * i + j
                cfs = cf[2 * (i % 2) + j]
                v.wait_ge(b.sems["EV"], cvt(i, j))
                if g >= 2:
                    v.wait_ge(b.sems["PL"], plmg(g - 2))
                v.tensor_scalar(m8[:, :], cfs[:, :], 8.0, None, ALU.is_ge)
                v.scalar_tensor_tensor(mt[:, :], m8[:, :], -8.0, cfs[:, :], ALU.mult, ALU.add)
                v.tensor_scalar(t4h[:, :], mt[:, :], 4.0, 0.5, ALU.max, ALU.mult)
                v.scalar_tensor_tensor(uuh[j][:, :], mt[:, :], 0.5, t4h[:, :],
                                       ALU.mult, ALU.add)
                b.inc(v.tensor_scalar(t6[j][:, :], mt[:, :], 6.0, -8.0, ALU.max, ALU.add),
                      "VE", 1)
    block.vector(f_dve)

    def f_pool(gq):
        for i in range(NS):
            for j in range(2):
                g = 2 * i + j
                gq.wait_ge(b.sems["VE"], nib(g))
                b.inc(gq.tensor_tensor(mg[:, :], uuh[j][:, :], t6[j][:, :], ALU.add), "PL", 1)
                gq.wait_ge(b.sems["EV"], evsg(i, j))
                b.inc(gq.tensor_tensor(mg[:, :], mg[:, :], sgs[j][:, :], ALU.mult), "PL", 1)
                gq.wait_ge(b.sems["EV"], evac(i))
                if g >= 2:
                    gq.wait_ge(b.sems["dS%d" % j], stc[g - 2])
                b.inc(gq.tensor_tensor(val[j][:, :], mg[:, :], scl[i % 2][:, :], ALU.mult),
                      "PL", 1)
            if i % 4 == 3:
                fam = "gud"[i // 8]
                h2 = (i // 4) % 2
                gq.wait_ge(b.sems["dS0"], stc[2 * i])
                gq.wait_ge(b.sems["dS1"], stc[2 * i + 1])
                base = {"g": 0, "u": 2048, "d": 4096}[fam]
                src_ap = io["dq_loc"][base + h2 * 1024: base + (h2 + 1) * 1024, :]
                gq.collective_compute(
                    "AllGather", ALU.bypass, replica_groups=[list(range(8))],
                    ins=[src_ap.opt()], outs=[io["dqg_%s%d" % (fam, h2)].opt()],
                ).then_inc(b.sems["CC"])
                b.cnt["CC"] += 1
    block.gpsimd(f_pool)

    b.barrier()
    es.close()



def _rstd(v, s, row_v, row_out, row_tmp, eps, width):
    raise NotImplementedError



def _emit_ln_dram(b, io, src_ap, dst_ap, ncols, eps):
    nc, block = b.nc, b.block
    PS = io["PS"]
    es = ExitStack()
    u = b.uid()
    sb = lambda nm, shape, dt=F32: es.enter_context(nc.sbuf_tensor(f"{nm}_{u}", shape, dt))
    nsp = ncols // 512
    xa = [sb(f"ln_x{i}", [128, ncols]) for i in range(2)]
    sq = sb("ln_sq", [128, ncols])
    m_row = sb("ln_m", [1, ncols])
    s_row = sb("ln_s", [1, ncols])
    t_row = sb("ln_t", [1, ncols])
    m_bc = sb("ln_mbc", [128, ncols])
    r_bc = sb("ln_rbc", [128, ncols])
    mh = sb("ln_mh", [1, ncols], BF16)
    ml = sb("ln_ml", [1, ncols], BF16)
    rh = sb("ln_rh", [1, ncols], BF16)
    rl = sb("ln_rl", [1, ncols], BF16)
    xb = [sb(f"ln_xb{i}", [128, ncols], BF16) for i in range(2)]

    for s_ in ("dA0", "dA1", "dS0", "dS1", "PE", "EV", "VE"):
        b.sem(s_)
    wb = [b.cnt["dA0"], b.cnt["dA1"]]
    sb0 = [b.cnt["dS0"], b.cnt["dS1"]]
    pe0 = b.cnt["PE"]; ev0 = b.cnt["EV"]; ve0 = b.cnt["VE"]
    wv = lambda c: wb[c % 2] + 16 * (c // 2 + 1)
    wv2 = lambda c: wb[c % 2] + 16 * (HC // 2) + 16 * (c // 2 + 1)
    sv = lambda c: sb0[c % 2] + 16 * (c // 2 + 1)

    def f_sync(q):
        for c in range(HC):
            if c >= 2:
                q.wait_ge(b.sems["PE"], pe0 + (c - 1))
            b.dma(q, xa[c % 2][:, :], src_ap[c * 128:(c + 1) * 128, 0:ncols], "dA%d" % (c % 2))
    block.sync(f_sync)

    def f_act(s):
        for c in range(HC):
            s.wait_ge(b.sems["dA%d" % (c % 2)], wv(c))
            if c >= 1:
                s.wait_ge(b.sems["PE"], pe0 + c)
            b.inc(s.activation(sq[:, :], xa[c % 2][:, :], AF.Square), "EV", 1)
    block.scalar(f_act)

    def f_pe(t):
        for c in range(HC):
            t.wait_ge(b.sems["dA%d" % (c % 2)], wv(c))
            t.wait_ge(b.sems["EV"], ev0 + c + 1)
            for s4 in range(nsp):
                t.matmul(PS[s4][0:1, :], io["s_ones_col"][:, :],
                         xa[c % 2][:, s4 * 512:(s4 + 1) * 512],
                         start=(c == 0), stop=(c == HC - 1))
                ins = t.matmul(PS[4 + s4 % 4][0:1, :] if nsp == 4 else PS[4 + s4][0:1, :],
                               io["s_ones_col"][:, :],
                               sq[:, s4 * 512:(s4 + 1) * 512],
                               start=(c == 0), stop=(c == HC - 1))
            b.inc(ins, "PE", 1)
    block.tensor(f_pe)
    pe_stats = b.cnt["PE"]

    def f_fin_v(v):
        v.wait_ge(b.sems["PE"], pe_stats)
        for s4 in range(nsp):
            v.tensor_scalar(m_row[:, s4 * 512:(s4 + 1) * 512], PS[s4][0:1, :], 1.0 / D, None, ALU.mult)
            v.tensor_scalar(s_row[:, s4 * 512:(s4 + 1) * 512], PS[4 + s4][0:1, :], 1.0 / D, None, ALU.mult)
        v.tensor_tensor(t_row[:, :], m_row[:, :], m_row[:, :], ALU.mult)
        v.tensor_tensor(s_row[:, :], s_row[:, :], t_row[:, :], ALU.subtract)
        b.inc(v.tensor_scalar(s_row[:, :], s_row[:, :], eps, None, ALU.add), "VE", 1)
    block.vector(f_fin_v)
    ve_var = b.cnt["VE"]

    def f_fin_s(s):
        s.wait_ge(b.sems["VE"], ve_var)
        b.inc(s.activation(t_row[:, :], s_row[:, :], AF.Sqrt), "EV", 1)
    block.scalar(f_fin_s)
    ev_sqrt = b.cnt["EV"]

    def f_fin_v2(v):
        v.wait_ge(b.sems["EV"], ev_sqrt)
        v.reciprocal(t_row[:, :], t_row[:, :])
        v.tensor_tensor(sq[0:1, 0:ncols], t_row[:, :], t_row[:, :], ALU.mult)
        v.tensor_tensor(sq[0:1, 0:ncols], sq[0:1, 0:ncols], s_row[:, :], ALU.mult)
        v.tensor_scalar(sq[0:1, 0:ncols], sq[0:1, 0:ncols], -0.5, 1.5, ALU.mult, ALU.add)
        v.tensor_tensor(t_row[:, :], t_row[:, :], sq[0:1, 0:ncols], ALU.mult)
        v.tensor_copy(mh[:, :], m_row[:, :])
        v.tensor_tensor(sq[0:1, 0:ncols], m_row[:, :], mh[:, :], ALU.subtract)
        v.tensor_copy(ml[:, :], sq[0:1, 0:ncols])
        v.tensor_copy(rh[:, :], t_row[:, :])
        v.tensor_tensor(sq[0:1, 0:ncols], t_row[:, :], rh[:, :], ALU.subtract)
        b.inc(v.tensor_copy(rl[:, :], sq[0:1, 0:ncols]), "VE", 1)
    block.vector(f_fin_v2)
    ve_rstd = b.cnt["VE"]

    def f_bc_pe(t):
        t.wait_ge(b.sems["VE"], ve_rstd)
        for s4 in range(nsp):
            if s4 >= 2:
                t.wait_ge(b.sems["EV"], ev_sqrt + 2 * (s4 - 2) + 2)
            sl = slice(s4 * 512, (s4 + 1) * 512)
            t.matmul(PS[s4 % 2][:, :], io["s_onesbf"][:, :], mh[:, sl], start=True, stop=False)
            b.inc(t.matmul(PS[s4 % 2][:, :], io["s_onesbf"][:, :], ml[:, sl],
                           start=False, stop=True), "PE", 1)
            t.wait_ge(b.sems["EV"], ev_sqrt + 2 * s4 + 1)
            t.matmul(PS[s4 % 2][:, :], io["s_onesbf"][:, :], rh[:, sl], start=True, stop=False)
            b.inc(t.matmul(PS[s4 % 2][:, :], io["s_onesbf"][:, :], rl[:, sl],
                           start=False, stop=True), "PE", 1)
    block.tensor(f_bc_pe)

    def f_bc_act(s):
        for s4 in range(nsp):
            s.wait_ge(b.sems["PE"], pe_stats + 2 * s4 + 1)
            b.inc(s.copy(m_bc[:, s4 * 512:(s4 + 1) * 512], PS[s4 % 2][:, :]), "EV", 1)
            s.wait_ge(b.sems["PE"], pe_stats + 2 * s4 + 2)
            b.inc(s.copy(r_bc[:, s4 * 512:(s4 + 1) * 512], PS[s4 % 2][:, :]), "EV", 1)
    block.scalar(f_bc_act)
    ev_bc = b.cnt["EV"]

    def f_sync2(q):
        for c in range(HC):
            if c < 2:
                q.wait_ge(b.sems["PE"], pe_stats)
            else:
                q.wait_ge(b.sems["VE"], ve_rstd + (c - 1))
            b.dma(q, xa[c % 2][:, :], src_ap[c * 128:(c + 1) * 128, 0:ncols], "dA%d" % (c % 2))
    block.sync(f_sync2)

    def f_norm(v):
        v.wait_ge(b.sems["EV"], ev_bc)
        for c in range(HC):
            v.wait_ge(b.sems["dA%d" % (c % 2)], wv2(c))
            if c >= 2:
                v.wait_ge(b.sems["dS%d" % (c % 2)], sv(c - 2))
            v.tensor_tensor(xa[c % 2][:, :], xa[c % 2][:, :], m_bc[:, :], ALU.subtract)
            b.inc(v.tensor_tensor(xb[c % 2][:, :], xa[c % 2][:, :], r_bc[:, :], ALU.mult), "VE", 1)
    block.vector(f_norm)

    def f_store(s):
        for c in range(HC):
            s.wait_ge(b.sems["VE"], ve_rstd + c + 1)
            b.dma(s, dst_ap[c * 128:(c + 1) * 128, 0:ncols], xb[c % 2][:, :], "dS%d" % (c % 2))
        if "dbg_mbc" in io and ncols == T:
            b.dma(s, io["dbg_mbc"][:, :], m_bc[:, :], "dS0")
            b.dma(s, io["dbg_rbc"][:, :], r_bc[:, :], "dS0")
    block.scalar(f_store)

    b.barrier()
    es.close()



def _emit_ffn_tblock(b, io, ar, xn_src, acc_init_src):
    nc, block = b.nc, b.block
    PS = io["PS"]
    xn, acc = ar["xn"], ar["acc"]
    inter = ar["inter"]
    gbuf, ubuf, dbuf = ar["gbuf"], ar["ubuf"], ar["dbuf"]
    gf, uf = ar["gf"], ar["uf"]
    At, Bt, Ct = ar["At"], ar["Bt"], ar["Ct"]

    NDB = len(dbuf)
    for s_ in ("dX", "dC", "dA0", "dA1", "dB0", "dB1", "GU", "DN", "EV", "VE", "PL", "DA", "CC"):
        b.sem(s_)
    gu0 = b.cnt["GU"]; dn0 = b.cnt["DN"]
    ev0 = b.cnt["EV"]; ve0 = b.cnt["VE"]; pl0 = b.cnt["PL"]; da0 = b.cnt["DA"]
    ab = [b.cnt["dA0"], b.cnt["dA1"]]
    bb = [b.cnt["dB0"], b.cnt["dB1"]]
    x0c = b.cnt["dX"]
    c0c = b.cnt["dC"]

    pre = x0c + 16 * (16 if xn_src is not None else 0)
    prec = c0c + 16 * (16 if acc_init_src is not None else 0)
    wA = lambda s, k: ab[(16 * s + k) % 2] + 64 * ((16 * s + k) // 2 + 1)
    wB = lambda s, q, k: bb[(64 * s + 16 * q + k) % 2] + 16 * ((64 * s + 16 * q + k) // 2 + 1)
    evv = lambda i, kk: ev0 + 2 * i + kk
    vev = lambda i, kk: ve0 + 2 * i + kk
    plv = lambda i, kk: pl0 + 3 * i + kk
    dnv = lambda s, q, k: dn0 + 64 * s + 16 * q + k + 1
    dav = lambda s, q, g8: da0 + 32 * s + 8 * q + g8 + 1

    def f_sync(q):
        q.wait_ge(b.sems["CC"], 4)
        for s in range(8):
            for k in range(16):
                i = 16 * s + k
                if i >= 2:
                    q.wait_ge(b.sems["GU"], gu0 + i - 1)
                for h2 in range(2):
                    b.dma(q, gbuf[i % 2][:, h2 * 1024:(h2 + 1) * 1024]
                          .rearrange("p (k c) -> p k c", k=8),
                          io["sgap"](i, h2), "dA%d" % (i % 2))
                    b.dma(q, ubuf[i % 2][:, h2 * 1024:(h2 + 1) * 1024]
                          .rearrange("p (k c) -> p k c", k=8),
                          io["suap"](i, h2), "dA%d" % (i % 2))
            if s == 0:
                q.wait_ge(b.sems["CC"], 6)
            for qq in range(4):
                for k in range(16):
                    i = 16 * s + k
                    j = 64 * s + 16 * qq + k
                    if j >= NDB:
                        q.wait_ge(b.sems["DN"], dn0 + j - (NDB - 1))
                    b.dma(q, dbuf[j % NDB][:, :], io["sdap"](i, qq), "dB%d" % (j % 2))
    block.sync(f_sync)

    def f_pe(t):
        for s in range(8):
            for k in range(16):
                i = 16 * s + k
                t.wait_ge(b.sems["dA%d" % (i % 2)], wA(s, k))
                if i == 0:
                    t.wait_ge(b.sems["dX"], pre)
                if i >= 2:
                    t.wait_ge(b.sems["EV"], evv(i - 2, 1))
                if s >= 1 and k < 2:
                    t.wait_ge(b.sems["DA"], da0 + 32 * s)
                pb = 4 * (i % 2)
                for kk in range(HC):
                    for which, buf, bank0 in (("g", gbuf[i % 2], pb), ("u", ubuf[i % 2], pb + 2)):
                        for sp in range(2):
                            ins = t.matmul(PS[bank0 + sp][:, :], buf[:, kk * 128:(kk + 1) * 128],
                                           xn[:, kk * 1024 + sp * 512: kk * 1024 + (sp + 1) * 512],
                                           start=(kk == 0), stop=(kk == HC - 1))
                b.inc(ins, "GU", 1)
            t.wait_ge(b.sems["EV"], evv(16 * s + 15, 1))
            for qq in range(4):
                if not (s == 0 and qq == 0):
                    t.wait_ge(b.sems["DA"], da0 + 32 * s + 8 * qq)
                for k in range(16):
                    j = 64 * s + 16 * qq + k
                    if qq == 0:
                        t.wait_ge(b.sems["VE"], vev(16 * s + k, 2))
                    t.wait_ge(b.sems["dB%d" % (j % 2)], wB(s, qq, k))
                    for g8 in range(8):
                        h4, sp = g8 // 2, g8 % 2
                        ins = t.matmul(PS[g8][:, :], dbuf[j % NDB][:, h4 * 128:(h4 + 1) * 128],
                                       inter[:, k * 1024 + sp * 512: k * 1024 + (sp + 1) * 512],
                                       start=(k == 0), stop=(k == 15))
                    b.inc(ins, "DN", 1)
    block.tensor(f_pe)

    def f_act(s):
        if xn_src is not None:
            for c in range(HC):
                b.dma(s, xn[:, c * 1024:(c + 1) * 1024], xn_src[c * 128:(c + 1) * 128, :], "dX")
        for ss in range(8):
            for k in range(16):
                i = 16 * ss + k
                s.wait_ge(b.sems["GU"], gu0 + i + 1)
                if i >= 1:
                    s.wait_ge(b.sems["PL"], plv(i - 1, 3))
                pb = 4 * (i % 2)
                s.copy(gf[:, 0:512], PS[pb][:, :])
                s.copy(gf[:, 512:1024], PS[pb + 1][:, :])
                s.copy(uf[:, 0:512], PS[pb + 2][:, :])
                b.inc(s.copy(uf[:, 512:1024], PS[pb + 3][:, :]), "EV", 1)
                s.wait_ge(b.sems["PL"], plv(i, 2))
                b.inc(s.activation(At[:, :], Ct[:, :], AF.Tanh), "EV", 1)
    block.scalar(f_act)

    def f_pool(g):
        if acc_init_src is not None:
            for c in range(HC):
                b.dma(g, acc[:, c * 1024:(c + 1) * 1024], acc_init_src[c * 128:(c + 1) * 128, :], "dC")
        for ss in range(8):
            for k in range(16):
                i = 16 * ss + k
                g.wait_ge(b.sems["EV"], evv(i, 1))
                if i >= 1:
                    g.wait_ge(b.sems["VE"], vev(i - 1, 2))
                b.inc(g.tensor_tensor(At[:, :], gf[:, :], gf[:, :], ALU.mult), "PL", 1)
                g.wait_ge(b.sems["VE"], vev(i, 1))
                b.inc(g.tensor_tensor(Ct[:, :], gf[:, :], Bt[:, :], ALU.mult), "PL", 1)
                g.wait_ge(b.sems["EV"], evv(i, 2))
                b.inc(g.tensor_tensor(Ct[:, :], gf[:, :], uf[:, :], ALU.mult), "PL", 1)
    block.gpsimd(f_pool)

    def f_dve(v):
        for ss in range(8):
            for k in range(16):
                i = 16 * ss + k
                v.wait_ge(b.sems["PL"], plv(i, 1))
                b.inc(v.tensor_scalar(Bt[:, :], At[:, :], GELU_C * GELU_A, GELU_C, ALU.mult, ALU.add),
                      "VE", 1)
                v.wait_ge(b.sems["EV"], evv(i, 2))
                v.tensor_scalar(Bt[:, :], At[:, :], 0.5, 0.5, ALU.mult, ALU.add)
                v.wait_ge(b.sems["PL"], plv(i, 3))
                if ss >= 1:
                    v.wait_ge(b.sems["DN"], dn0 + 64 * ss)
                b.inc(v.tensor_tensor(inter[:, i % 16 * 1024:(i % 16 + 1) * 1024],
                                      Ct[:, :], Bt[:, :], ALU.mult), "VE", 1)
            if acc_init_src is not None and ss == 0:
                v.wait_ge(b.sems["dC"], prec)
            for qq in range(4):
                v.wait_ge(b.sems["DN"], dn0 + 64 * ss + 16 * (qq + 1))
                for g8 in range(8):
                    h4, sp = g8 // 2, g8 % 2
                    col = (4 * qq + h4) * 1024 + sp * 512
                    b.inc(v.tensor_tensor(acc[:, col:col + 512], acc[:, col:col + 512],
                                          PS[g8][:, :], ALU.add), "DA", 1)
    block.vector(f_dve)

    b.barrier()



def _emit_ln_sbuf(b, io, ar, src, dst_ap, r1o_ap, eps):
    nc, block = b.nc, b.block
    PS = io["PS"]
    es = ExitStack()
    u = b.uid()
    sb = lambda nm, shape, dt=F32: es.enter_context(nc.sbuf_tensor(f"{nm}_{u}", shape, dt))
    sq4 = [sb(f"l2_sq{i}", [128, 4096]) for i in range(2)]
    m_row = sb("l2_m", [1, 1024])
    s_row = sb("l2_s", [1, 1024])
    t_row = sb("l2_t", [1, 1024])
    w_row = sb("l2_w", [1, 1024])
    m_bc = sb("l2_mbc", [128, 4096])
    r_bc = sb("l2_rbc", [128, 4096])
    mh = sb("l2_mh", [1, 1024], BF16)
    ml = sb("l2_ml", [1, 1024], BF16)
    rh = sb("l2_rh", [1, 1024], BF16)
    rl = sb("l2_rl", [1, 1024], BF16)
    xb = [sb(f"l2_xb{i}", [128, 4096], BF16) for i in range(2)]

    for s_ in ("dS0", "dS1", "PE", "EV", "VE"):
        b.sem(s_)
    pe0 = b.cnt["PE"]; ev0 = b.cnt["EV"]; ve0 = b.cnt["VE"]
    sb0 = [b.cnt["dS0"], b.cnt["dS1"]]
    per_st = 8 if r1o_ap is not None else 4
    sv = lambda qi: sb0[qi % 2] + 16 * per_st * (qi // 2 + 1)

    def f_act(s):
        for qi in range(4):
            if qi >= 2:
                s.wait_ge(b.sems["PE"], pe0 + qi - 1)
            b.inc(s.activation(sq4[qi % 2][:, :], src[:, qi * 4096:(qi + 1) * 4096],
                               AF.Square), "EV", 1)
        s.wait_ge(b.sems["VE"], ve0 + 1)
        s.activation(t_row[:, :], s_row[:, :], AF.Ln)
        b.inc(s.activation(t_row[:, :], t_row[:, :], AF.Exp, scale=-0.5), "EV", 1)
        for which in range(2):
            s.wait_ge(b.sems["PE"], pe0 + 5 + which)
            dstb = m_bc if which == 0 else r_bc
            for t in range(8):
                ins = s.copy(dstb[:, t * 512:(t + 1) * 512], PS[4 + t % 2][:, :])
            b.inc(ins, "EV", 1)
    block.scalar(f_act)

    def f_pe(t_):
        for qi in range(4):
            t_.wait_ge(b.sems["EV"], ev0 + qi + 1)
            ins = None
            for st in range(8):
                c, half = st // 2, st % 2
                sl = slice(c * 1024 + half * 512, c * 1024 + (half + 1) * 512)
                t_.matmul(PS[half][0:1, :], io["s_ones_col"][:, :],
                          src[:, qi * 4096 + c * 1024 + half * 512:
                              qi * 4096 + c * 1024 + (half + 1) * 512],
                          start=(qi == 0 and c == 0), stop=(qi == 3 and c == 3))
                ins = t_.matmul(PS[2 + half][0:1, :], io["s_ones_col"][:, :],
                                sq4[qi % 2][:, c * 1024 + half * 512:
                                            c * 1024 + (half + 1) * 512],
                                start=(qi == 0 and c == 0), stop=(qi == 3 and c == 3))
            b.inc(ins, "PE", 1)
        t_.wait_ge(b.sems["VE"], ve0 + 2)
        for half in range(2):
            sl = slice(half * 512, (half + 1) * 512)
            t_.matmul(PS[4 + half][:, :], io["s_onesbf"][:, :], mh[:, sl], start=True, stop=False)
            ins = t_.matmul(PS[4 + half][:, :], io["s_onesbf"][:, :], ml[:, sl],
                            start=False, stop=True)
        b.inc(ins, "PE", 1)
        t_.wait_ge(b.sems["EV"], ev0 + 6)
        for half in range(2):
            sl = slice(half * 512, (half + 1) * 512)
            t_.matmul(PS[4 + half][:, :], io["s_onesbf"][:, :], rh[:, sl], start=True, stop=False)
            ins = t_.matmul(PS[4 + half][:, :], io["s_onesbf"][:, :], rl[:, sl],
                            start=False, stop=True)
        b.inc(ins, "PE", 1)
    block.tensor(f_pe)

    def f_dve(v):
        v.wait_ge(b.sems["PE"], pe0 + 4)
        for half in range(2):
            sl = slice(half * 512, (half + 1) * 512)
            v.tensor_scalar(m_row[:, sl], PS[half][0:1, :], 1.0 / D, None, ALU.mult)
            v.tensor_scalar(s_row[:, sl], PS[2 + half][0:1, :], 1.0 / D, None, ALU.mult)
        v.tensor_tensor(t_row[:, :], m_row[:, :], m_row[:, :], ALU.mult)
        v.tensor_tensor(s_row[:, :], s_row[:, :], t_row[:, :], ALU.subtract)
        b.inc(v.tensor_scalar(s_row[:, :], s_row[:, :], eps, None, ALU.add), "VE", 1)
        v.wait_ge(b.sems["EV"], ev0 + 5)
        v.tensor_copy(mh[:, :], m_row[:, :])
        v.tensor_tensor(w_row[:, :], m_row[:, :], mh[:, :], ALU.subtract)
        v.tensor_copy(ml[:, :], w_row[:, :])
        v.tensor_copy(rh[:, :], t_row[:, :])
        v.tensor_tensor(w_row[:, :], t_row[:, :], rh[:, :], ALU.subtract)
        b.inc(v.tensor_copy(rl[:, :], w_row[:, :]), "VE", 1)
        v.wait_ge(b.sems["EV"], ev0 + 7)
        for qi in range(4):
            if qi >= 2:
                v.wait_ge(b.sems["dS%d" % (qi % 2)], sv(qi - 2))
            v.tensor_tensor(sq4[qi % 2][:, :], src[:, qi * 4096:(qi + 1) * 4096],
                            m_bc[:, :], ALU.subtract)
            b.inc(v.tensor_tensor(xb[qi % 2][:, :], sq4[qi % 2][:, :], r_bc[:, :],
                                  ALU.mult), "VE", 1)
    block.vector(f_dve)

    def f_store(s):
        for qi in range(4):
            s.wait_ge(b.sems["VE"], ve0 + 3 + qi)
            for c in range(4):
                ch = qi * 4 + c
                b.dma(s, dst_ap[ch * 128:(ch + 1) * 128, :],
                      xb[qi % 2][:, c * 1024:(c + 1) * 1024], "dS%d" % (qi % 2))
                if r1o_ap is not None:
                    b.dma(s, r1o_ap[ch * 128:(ch + 1) * 128, :],
                          src[:, ch * 1024:(ch + 1) * 1024], "dS%d" % (qi % 2))
    block.sync(f_store)

    b.barrier()
    es.close()



def _emit_proj(b, io, K_own, V_own, Qr_sb, xn2_sb):
    nc, block = b.nc, b.block
    PS = io["PS"]
    es = ExitStack()
    u = b.uid()
    sb = lambda nm, shape, dt=BF16: es.enter_context(nc.sbuf_tensor(f"{nm}_{u}", shape, dt))
    wkb = [sb(f"wkb{i}", [128, 512]) for i in range(2)]
    wvb = [sb(f"wvb{i}", [128, 512]) for i in range(2)]
    wqb = [sb(f"wqb{i}", [128, 2048]) for i in range(2)]

    for s_ in ("dX", "dG0", "dG1", "PE", "EV"):
        b.sem(s_)
    pe0 = b.cnt["PE"]; ev0 = b.cnt["EV"]
    gb0 = [b.cnt["dG0"], b.cnt["dG1"]]
    gv = lambda li: gb0[li % 2] + 16 * (li // 2 + 1)

    def f_sync(q):
        for c in range(HC):
            b.dma(q, xn2_sb[:, c * 1024:(c + 1) * 1024], io["s_xn2"][c * 128:(c + 1) * 128, :], "dX")
    block.sync(f_sync)
    w_xn = b.cnt["dX"]

    jobs = [("k", 0), ("v", 0), ("q", 0), ("q", 1), ("q", 2), ("q", 3)]

    def f_gload(g):
        for jo, (kind, oc) in enumerate(jobs):
            for h in range(HC):
                li = jo * HC + h
                if li >= 2:
                    g.wait_ge(b.sems["PE"], pe0 + li - 1)
                if kind == "k":
                    b.dma(g, wkb[h % 2][:, :], io["wk"][h * 128:(h + 1) * 128, :], "dG%d" % (li % 2))
                elif kind == "v":
                    b.dma(g, wvb[h % 2][:, :], io["wv"][h * 128:(h + 1) * 128, :], "dG%d" % (li % 2))
                else:
                    b.dma(g, wqb[h % 2][:, :], io["wq"][h * 128:(h + 1) * 128, :], "dG%d" % (li % 2))
    block.gpsimd(f_gload)

    def f_pe(t):
        t.wait_ge(b.sems["dX"], w_xn)
        for jo, (kind, oc) in enumerate(jobs):
            if jo >= 1:
                t.wait_ge(b.sems["EV"], ev0 + 8 * jo)
            for h in range(HC):
                li = jo * HC + h
                t.wait_ge(b.sems["dG%d" % (li % 2)], gv(li))
                ins = None
                for g8 in range(8):
                    if kind == "k":
                        gg, sp = g8 // 2, g8 % 2
                        lhsT = wkb[h % 2][:, gg * 128:(gg + 1) * 128]
                        rhs = xn2_sb[:, h * 1024 + sp * 512: h * 1024 + (sp + 1) * 512]
                    elif kind == "v":
                        tc = g8
                        lhsT = xn2_sb[:, h * 1024 + tc * 128: h * 1024 + (tc + 1) * 128]
                        rhs = wvb[h % 2][:, :]
                    else:
                        dq, sp = (oc * 8 + g8) // 2, (oc * 8 + g8) % 2
                        lhsT = wqb[h % 2][:, dq * 128:(dq + 1) * 128]
                        rhs = xn2_sb[:, h * 1024 + sp * 512: h * 1024 + (sp + 1) * 512]
                    ins = t.matmul(PS[g8][:, :], lhsT, rhs, start=(h == 0), stop=(h == HC - 1))
                b.inc(ins, "PE", 1)
    block.tensor(f_pe)

    def f_act(s):
        for jo, (kind, oc) in enumerate(jobs):
            s.wait_ge(b.sems["PE"], pe0 + HC * (jo + 1))
            for g8 in range(8):
                if kind == "k":
                    gg, sp = g8 // 2, g8 % 2
                    dst = K_own[:, gg * 1024 + sp * 512: gg * 1024 + (sp + 1) * 512]
                elif kind == "v":
                    tc = g8
                    dst = V_own[:, tc * 512:(tc + 1) * 512]
                else:
                    dq, sp = (oc * 8 + g8) // 2, (oc * 8 + g8) % 2
                    dst = Qr_sb[:, dq * 1024 + sp * 512: dq * 1024 + (sp + 1) * 512]
                b.inc(s.copy(dst, PS[g8][:, :]), "EV", 1)
    block.scalar(f_act)

    b.barrier()
    es.close()



def _emit_rms_rope(b, io, K_own, Qr_sb, Qn_sb):
    nc, block = b.nc, b.block
    PS = io["PS"]
    es = ExitStack()
    u = b.uid()
    sbf = lambda nm, shape, dt=F32: es.enter_context(nc.sbuf_tensor(f"{nm}_{u}", shape, dt))
    sqf = sbf("rr_sq", [128, 4096])
    tmp1 = sbf("rr_t1", [128, 4096])
    t_row = sqf[0:1, :]
    w_row = tmp1[0:1, :]
    rh = sbf("rr_rh", [1, 4096], BF16)
    rl = sbf("rr_rl", [1, 4096], BF16)
    rbc = sbf("rr_rbc", [128, 4096])
    s_row = rbc[0:1, :]
    qb = sbf("rr_qb", [128, 4096], BF16)

    for s_ in ("PE", "EV", "VE"):
        b.sem(s_)
    pe0 = b.cnt["PE"]; ev0 = b.cnt["EV"]; ve0 = b.cnt["VE"]
    units = [("k", 0)] + [("q", bq) for bq in range(4)]
    evA = lambda i: ev0 + 4 * i + 1
    evB = lambda i: ev0 + 4 * i + 2
    evC1 = lambda i: ev0 + 4 * i + 3
    evC2 = lambda i: ev0 + 4 * i + 4
    peS1 = lambda i: pe0 + 6 * i + 1
    peS2 = lambda i: pe0 + 6 * i + 2
    peB1 = lambda i: pe0 + 6 * i + 3
    peB2 = lambda i: pe0 + 6 * i + 4
    peC1 = lambda i: pe0 + 6 * i + 5
    peC2 = lambda i: pe0 + 6 * i + 6
    veR1 = lambda i: ve0 + 6 * i + 1
    veA = lambda i: ve0 + 6 * i + 2
    veB = lambda i: ve0 + 6 * i + 3
    veC = lambda i: ve0 + 6 * i + 4
    veD1 = lambda i: ve0 + 6 * i + 5
    veD2 = lambda i: ve0 + 6 * i + 6

    def usrc(kind, idx):
        return K_own[:, :] if kind == "k" else Qr_sb[:, idx * 4096:(idx + 1) * 4096]

    def udst(kind, idx):
        return K_own[:, :] if kind == "k" else Qn_sb[:, idx * 4096:(idx + 1) * 4096]

    def f_act(s):
        for ui, (kind, idx) in enumerate(units):
            if ui >= 1:
                s.wait_ge(b.sems["PE"], peS2(ui - 1))
                s.wait_ge(b.sems["VE"], veB(ui - 1))
            b.inc(s.activation(sqf[:, :], usrc(kind, idx), AF.Square), "EV", 1)
            s.wait_ge(b.sems["VE"], veA(ui))
            s.activation(t_row[:, :], s_row[:, :], AF.Ln)
            b.inc(s.activation(t_row[:, :], t_row[:, :], AF.Exp, scale=-0.5), "EV", 1)
            for rnd in range(2):
                s.wait_ge(b.sems["PE"], peB1(ui) if rnd == 0 else peB2(ui))
                q4 = 4 * (ui % 2)
                for t in range(4):
                    ins = s.copy(rbc[:, (4 * rnd + t) * 512:(4 * rnd + t + 1) * 512],
                                 PS[q4 + t][:, :])
                b.inc(ins, "EV", 1)
    block.scalar(f_act)

    def f_pe(t_):
        for ui, (kind, idx) in enumerate(units):
            q4 = 4 * (ui % 2)
            t_.wait_ge(b.sems["EV"], evA(ui))
            if ui >= 2:
                t_.wait_ge(b.sems["VE"], veD2(ui - 2))
            for rnd in range(2):
                if rnd == 1:
                    t_.wait_ge(b.sems["VE"], veR1(ui))
                ins = None
                for t in range(4):
                    ins = t_.matmul(PS[q4 + t][0:1, :], io["s_ones_col"][:, :],
                                    sqf[:, (4 * rnd + t) * 512:(4 * rnd + t + 1) * 512],
                                    start=True, stop=True)
                b.inc(ins, "PE", 1)
            t_.wait_ge(b.sems["VE"], veB(ui))
            for rnd in range(2):
                if rnd == 1:
                    t_.wait_ge(b.sems["EV"], evC1(ui))
                for t in range(4):
                    sl = slice((4 * rnd + t) * 512, (4 * rnd + t + 1) * 512)
                    t_.matmul(PS[q4 + t][:, :], io["s_onesbf"][:, :], rh[:, sl],
                              start=True, stop=False)
                    ins = t_.matmul(PS[q4 + t][:, :], io["s_onesbf"][:, :], rl[:, sl],
                                    start=False, stop=True)
                b.inc(ins, "PE", 1)
            t_.wait_ge(b.sems["VE"], veC(ui))
            t_.wait_ge(b.sems["EV"], evC2(ui))
            for rnd in range(2):
                if rnd == 1:
                    t_.wait_ge(b.sems["VE"], veD1(ui))
                ins = None
                for t in range(4):
                    ins = t_.matmul(PS[q4 + t][:, :], io["s_sswap"][:, :],
                                    qb[:, (4 * rnd + t) * 512:(4 * rnd + t + 1) * 512],
                                    start=True, stop=True)
                b.inc(ins, "PE", 1)
    block.tensor(f_pe)

    def f_dve(v):
        for ui, (kind, idx) in enumerate(units):
            q4 = 4 * (ui % 2)
            for rnd in range(2):
                v.wait_ge(b.sems["PE"], peS1(ui) if rnd == 0 else peS2(ui))
                ins = None
                for t in range(4):
                    ins = v.tensor_copy(s_row[:, (4 * rnd + t) * 512:(4 * rnd + t + 1) * 512],
                                        PS[q4 + t][0:1, :])
                if rnd == 0:
                    b.inc(ins, "VE", 1)
            b.inc(v.tensor_scalar(s_row[:, :], s_row[:, :], 1.0 / DH, RMS_EPS,
                                  ALU.mult, ALU.add), "VE", 1)
            v.wait_ge(b.sems["EV"], evB(ui))
            if kind == "q":
                v.tensor_scalar(t_row[:, :], t_row[:, :], DH ** -0.5, None, ALU.mult)
            v.tensor_copy(rh[:, :], t_row[:, :])
            v.tensor_tensor(w_row[:, :], t_row[:, :], rh[:, :], ALU.subtract)
            b.inc(v.tensor_copy(rl[:, :], w_row[:, :]), "VE", 1)
            v.wait_ge(b.sems["EV"], evC2(ui))
            v.tensor_tensor(tmp1[:, :], usrc(kind, idx), rbc[:, :], ALU.mult)
            ins = None
            for sub in range(4):
                gcol = io["s_gkt"][:, sub:sub + 1] if kind == "k" else \
                    io["s_gqt"][:, idx * 4 + sub: idx * 4 + sub + 1]
                ins = v.tensor_scalar(qb[:, sub * 1024:(sub + 1) * 1024],
                                      tmp1[:, sub * 1024:(sub + 1) * 1024], gcol, None, ALU.mult)
            b.inc(ins, "VE", 1)
            dst = udst(kind, idx)
            for rnd in range(2):
                v.wait_ge(b.sems["PE"], peC1(ui) if rnd == 0 else peC2(ui))
                ins = None
                for t in range(4):
                    sl = slice((4 * rnd + t) * 512, (4 * rnd + t + 1) * 512)
                    tsl = slice(((4 * rnd + t) % 2) * 512, (((4 * rnd + t) % 2) + 1) * 512)
                    v.tensor_tensor(dst[:, sl], qb[:, sl], io["s_rcos"][:, tsl], ALU.mult)
                    v.tensor_tensor(tmp1[:, sl], PS[q4 + t][:, :], io["s_rsin"][:, tsl], ALU.mult)
                    ins = v.tensor_tensor(dst[:, sl], dst[:, sl], tmp1[:, sl], ALU.add)
                b.inc(ins, "VE", 1)
    block.vector(f_dve)

    es.close()
    return veD2(0)



def _emit_kv_exchange(b, io, K_own, V_own, K_sb, V_sb, k_done):
    nc, block = b.nc, b.block
    for s_ in ("dX", "dS0", "CK", "VE"):
        b.sem(s_)

    def f_sync(q):
        q.wait_ge(b.sems["VE"], k_done)
        for g in range(NG):
            b.dma(q, io["kv_loc"][g * 128:(g + 1) * 128, :],
                  K_own[:, g * 1024:(g + 1) * 1024], "dS0")
        for tc in range(8):
            b.dma(q, io["kv_loc"][512 + tc * 128: 512 + (tc + 1) * 128, 0:512],
                  V_own[:, tc * 512:(tc + 1) * 512], "dS0")
    block.sync(f_sync)
    st = b.cnt["dS0"]

    def f_pool(gq):
        gq.wait_ge(b.sems["dS0"], st)
        gq.collective_compute(
            "AllGather", ALU.bypass,
            replica_groups=[[0, 1], [2, 3], [4, 5], [6, 7]],
            ins=[io["kv_loc"].opt()], outs=[io["kv_glob"].opt()],
        ).then_inc(b.sems["CK"])
        b.cnt["CK"] = max(b.cnt["CK"], 1)
        gq.wait_ge(b.sems["CK"], 1)
        for g in range(NG):
            for s in range(16):
                m, ls = s % 2, s // 2
                b.dma(gq, K_sb[:, g * 2048 + s * 128: g * 2048 + (s + 1) * 128],
                      io["kv_glob"][m * 1536 + g * 128: m * 1536 + (g + 1) * 128,
                                    ls * 128:(ls + 1) * 128], "dX")
        for s in range(16):
            m, ls = s % 2, s // 2
            b.dma(gq, V_sb[:, s * 512:(s + 1) * 512],
                  io["kv_glob"][m * 1536 + 512 + ls * 128: m * 1536 + 512 + (ls + 1) * 128,
                                0:512], "dX")
    block.gpsimd(f_pool)

    b.barrier()



def _emit_attn_wo(b, io, K_sb, V_sb, Qn_sb, res2):
    nc, block = b.nc, b.block
    PS = io["PS"]
    PT6 = PS[6][:, :].bitcast(BF16)
    PT7 = PS[7][:, :].bitcast(BF16)
    es = ExitStack()
    u = b.uid()
    sbf = lambda nm, shape, dt=F32: es.enter_context(nc.sbuf_tensor(f"{nm}_{u}", shape, dt))
    sc_sb = [sbf(f"at_sc{i}", [128, 2048]) for i in range(2)]
    prb = [sbf(f"at_prb{i}", [128, 2048], BF16) for i in range(2)]
    prb2 = [sbf(f"at_prb2{i}", [128, 2048], BF16) for i in range(2)]
    pt_sb = [sbf(f"at_pt{i}", [128, 2048], BF16) for i in range(2)]
    nm_c = [sbf(f"at_nm{i}", [128, 1]) for i in range(2)]
    ssum = [sbf(f"at_ss{i}", [128, 1]) for i in range(2)]
    rs_c = [sbf(f"at_rs{i}", [128, 1]) for i in range(2)]
    wob = [sbf(f"at_wo{i}", [128, 2048], BF16) for i in range(2)]

    for s_ in ("dX", "dG0", "dG1", "PE", "EV", "VE", "TR"):
        b.sem(s_)
    pe0 = b.cnt["PE"]; ev0 = b.cnt["EV"]; ve0 = b.cnt["VE"]; tr0 = b.cnt["TR"]

    its = []
    for g in range(NG):
        for h4 in range(HPG):
            for j in range(8):
                its.append((g, g * HPG + h4, j))

    NT = len(its)
    pe_sc = [0] * NT; pe_pv = [0] * NT; tr_b = [0] * NT
    pec, trc = pe0, tr0
    for tau in range(NT + 1):
        if tau < NT:
            pec += 1; pe_sc[tau] = pec
        if tau >= 1:
            trc += 2; tr_b[tau - 1] = trc
            pec += 1; pe_pv[tau - 1] = pec
    ve_ev = [0] * NT; ve_nm = [0] * NT; ve_pc = [0] * NT
    vec = ve0
    for tau in range(NT + 1):
        if tau < NT:
            vec += 1; ve_ev[tau] = vec
        if tau >= 1:
            vec += 1; ve_nm[tau - 1] = vec
            vec += 1; ve_pc[tau - 1] = vec
    ev_ex = [0] * NT; ev_pc = [0] * NT; ev_oe = [0] * NT
    evc = ev0
    for tau in range(NT + 1):
        if tau < NT:
            evc += 1; ev_ex[tau] = evc
        if tau >= 1:
            evc += 1; ev_pc[tau - 1] = evc
            evc += 1; ev_oe[tau - 1] = evc

    def nblocks(j):
        return 2 * (j + 1)

    def kv_tc(j, i):
        return i

    def f_pe(t):
        for tau in range(NT + 1):
            if tau < NT:
                g, hh, j = its[tau]
                W = (j + 1) * 128
                if tau >= 1:
                    t.wait_ge(b.sems["VE"], ve_ev[tau - 1])
                qsl = Qn_sb[:, hh * 1024 + j * 128: hh * 1024 + (j + 1) * 128]
                segs = []
                ofs = 0
                while ofs < 2 * W:
                    wseg = min(512 - (ofs % 512), 2 * W - ofs)
                    segs.append((ofs, wseg, g * 2048 + ofs))
                    ofs += wseg
                ins = None
                for a, wseg, kc in segs:
                    ins = t.matmul(PS[a // 512][:, a % 512:(a % 512) + wseg], qsl,
                                   K_sb[:, kc:kc + wseg], start=True, stop=True)
                b.inc(ins, "PE", 1)
            if tau >= 1:
                g, hh, j = its[tau - 1]
                nb = nblocks(j)
                p = (tau - 1) % 2
                t.wait_ge(b.sems["VE"], ve_nm[tau - 1])
                if tau >= 2:
                    t.wait_ge(b.sems["EV"], ev_pc[tau - 2])
                for bi in range(nb):
                    pv = PT6 if bi < 8 else PT7
                    sl = (bi % 8) * 128
                    ins = t.transpose(pv[:, sl:sl + 128], prb2[p][:, bi * 128:(bi + 1) * 128],
                                      io["s_ident"][:, :])
                    if bi == min(nb, 8) - 1:
                        b.inc(ins, "TR", 1)
                if nb <= 8:
                    b.inc(t.nop(), "TR", 1)
                else:
                    b.inc(ins, "TR", 1)
                t.wait_ge(b.sems["VE"], ve_pc[tau - 1])
                t.wait_ge(b.sems["EV"], ev_pc[tau - 1])
                if tau >= 2:
                    t.wait_ge(b.sems["EV"], ev_oe[tau - 2])
                ins = None
                for bi in range(nb):
                    tc = kv_tc(j, bi)
                    ins = t.matmul(PS[5][:, 0:128],
                                   V_sb[:, tc * 512 + g * 128: tc * 512 + (g + 1) * 128],
                                   pt_sb[p][:, bi * 128:(bi + 1) * 128],
                                   start=(bi == 0), stop=(bi == nb - 1))
                b.inc(ins, "PE", 1)
    block.tensor(f_pe)

    def f_dve(v):
        for tau in range(NT + 1):
            if tau < NT:
                g, hh, j = its[tau]
                nb = nblocks(j)
                W = (j + 1) * 128
                p = tau % 2
                v.wait_ge(b.sems["PE"], pe_sc[tau])
                for bi in range(nb):
                    a = bi * 128
                    seg = PS[a // 512][:, a % 512:(a % 512) + 128]
                    if bi == nb - 2:
                        v.tensor_tensor(sc_sb[p][:, a:a + 128], seg, io["s_mask_a"][:, :], ALU.add)
                    elif bi == nb - 1:
                        v.tensor_tensor(sc_sb[p][:, a:a + 128], seg, io["s_mask_b"][:, :], ALU.add)
                    else:
                        v.tensor_copy(sc_sb[p][:, a:a + 128], seg)
                b.inc(v.reduce_max(nm_c[p][:, :], sc_sb[p][:, 0:2 * W], axis=AX, negate=True),
                      "VE", 1)
            if tau >= 1:
                g, hh, j = its[tau - 1]
                nb = nblocks(j)
                W = (j + 1) * 128
                p = (tau - 1) % 2
                v.wait_ge(b.sems["EV"], ev_ex[tau - 1])
                v.reciprocal(rs_c[p][:, :], ssum[p][:, :])
                b.inc(v.tensor_scalar(prb2[p][:, 0:2 * W], prb[p][:, 0:2 * W], rs_c[p][:, :],
                                      None, ALU.mult), "VE", 1)
                v.wait_ge(b.sems["TR"], tr_b[tau - 1])
                ins = None
                for bi in range(min(nb, 8)):
                    ins = v.tensor_copy(pt_sb[p][:, bi * 128:(bi + 1) * 128],
                                        PT6[:, (bi % 8) * 128:(bi % 8) * 128 + 128])
                b.inc(ins, "VE", 1)
    block.vector(f_dve)

    def f_act(s):
        for tau in range(NT + 1):
            if tau < NT:
                g, hh, j = its[tau]
                W = (j + 1) * 128
                p = tau % 2
                s.wait_ge(b.sems["VE"], ve_ev[tau])
                b.inc(s.activation(prb[p][:, 0:2 * W], sc_sb[p][:, 0:2 * W], AF.Exp,
                                   bias=nm_c[p][:, :], accum_out=ssum[p][:, :]), "EV", 1)
            if tau >= 1:
                g, hh, j = its[tau - 1]
                nb = nblocks(j)
                p = (tau - 1) % 2
                s.wait_ge(b.sems["TR"], tr_b[tau - 1])
                ins = s.nop()
                for bi in range(8, nb):
                    ins = s.copy(pt_sb[p][:, bi * 128:(bi + 1) * 128],
                                 PT7[:, (bi % 8) * 128:(bi % 8) * 128 + 128])
                b.inc(ins, "EV", 1)
                s.wait_ge(b.sems["PE"], pe_pv[tau - 1])
                b.inc(s.copy(Qn_sb[:, hh * 1024 + j * 128: hh * 1024 + (j + 1) * 128],
                             PS[5][:, 0:128]), "EV", 1)
    block.scalar(f_act)

    b.barrier()

    pe1 = b.cnt["PE"]; ve1 = b.cnt["VE"]
    gb1 = [b.cnt["dG0"], b.cnt["dG1"]]
    gv1 = lambda li: gb1[li % 2] + 16 * (li // 2 + 1)

    def f_sync(q):
        for c in range(HC):
            b.dma(q, res2[:, c * 1024:(c + 1) * 1024], io["s_r1o"][c * 128:(c + 1) * 128, :], "dX")
    block.sync(f_sync)
    w_r = b.cnt["dX"]

    def f_gload(g):
        for oc in range(4):
            for d in range(HC):
                li = oc * HC + d
                if li >= 2:
                    g.wait_ge(b.sems["PE"], pe1 + li - 1)
                b.dma(g, wob[d % 2][:, :], io["wo"][d * 128:(d + 1) * 128, :], "dG%d" % (li % 2))
    block.gpsimd(f_gload)

    def f_pe2(t):
        for oc in range(4):
            if oc >= 1:
                t.wait_ge(b.sems["VE"], ve1 + 8 * oc)
            for d in range(HC):
                li = oc * HC + d
                t.wait_ge(b.sems["dG%d" % (li % 2)], gv1(li))
                ins = None
                for g8 in range(8):
                    ho, sp = (oc * 8 + g8) // 2, (oc * 8 + g8) % 2
                    ins = t.matmul(PS[g8][:, :], wob[d % 2][:, ho * 128:(ho + 1) * 128],
                                   Qn_sb[:, d * 1024 + sp * 512: d * 1024 + (sp + 1) * 512],
                                   start=(d == 0), stop=(d == HC - 1))
                b.inc(ins, "PE", 1)
    block.tensor(f_pe2)

    def f_dve2(v):
        v.wait_ge(b.sems["dX"], w_r)
        for oc in range(4):
            v.wait_ge(b.sems["PE"], pe1 + HC * (oc + 1))
            for g8 in range(8):
                ho, sp = (oc * 8 + g8) // 2, (oc * 8 + g8) % 2
                col = ho * 1024 + sp * 512
                b.inc(v.tensor_tensor(res2[:, col:col + 512], res2[:, col:col + 512],
                                      PS[g8][:, :], ALU.add), "VE", 1)
    block.vector(f_dve2)

    b.barrier()
    es.close()



def _build(debug=None):
    nc = bass.Bass("TRN2", target_bir_lowering=False)
    io = {}
    dt_in = [
        ("x", [D, T], F32), ("wq", [D, D], F32), ("wk", [D, DKV], F32), ("wv", [D, DKV], F32),
        ("wo", [D, D], F32), ("gqt", [128, 16], F32), ("gkt", [128, 4], F32),
        ("gpT", [1024, 2048], I32), ("gsT", [64, 2048], F32),
        ("upT", [1024, 2048], I32), ("usT", [64, 2048], F32),
        ("dpT", [1024, D], I32), ("dsT", [64, D], F32),
        ("ident", [128, 128], BF16), ("ones_col", [128, 1], F32), ("ones_row", [1, 128], F32),
        ("ones_row_bf", [1, 128], BF16),
        ("emat", [8, 128], F32), ("emat_h", [8, 128], F32),
        ("mask_a", [128, 128], F32), ("mask_b", [128, 128], F32),
        ("rcos", [128, T], F32), ("rsin", [128, T], F32), ("sswap", [128, 128], BF16),
    ]
    for nm, shape, dt in dt_in:
        io[nm] = nc.dram_tensor(nm, shape, dt, kind="ExternalInput").ap()
    io["y"] = nc.dram_tensor("y", [D, TO], F32, kind="ExternalOutput").ap()
    io["dq_loc"] = nc.dram_tensor("dq_loc", [6144, 2048], BF16, kind="Internal").ap()
    for fam in ("g0", "g1", "u0", "u1", "d0", "d1"):
        io["dqg_" + fam] = nc.dram_tensor("dqg_" + fam, [8192, 2048], BF16, kind="Internal",
                                          addr_space="Shared").ap()

    def _sgap(i, h2):
        r0 = 1024 * (i // 16)
        c0 = (i % 16) * 128
        return io["dqg_g%d" % h2][r0:r0 + 1024, c0:c0 + 128].rearrange("(k p) c -> p k c", k=8)

    def _suap(i, h2):
        r0 = 1024 * (i // 16)
        c0 = (i % 16) * 128
        return io["dqg_u%d" % h2][r0:r0 + 1024, c0:c0 + 128].rearrange("(k p) c -> p k c", k=8)

    def _sdap(i, qq):
        c = i % 16
        r0 = 1024 * (i // 16) + (c % 8) * 128
        return io["dqg_d%d" % (c // 8)][r0:r0 + 128, qq * 512:(qq + 1) * 512]

    io["sgap"] = _sgap
    io["suap"] = _suap
    io["sdap"] = _sdap
    io["s_xn1"] = nc.dram_tensor("s_xn1", [D, T], BF16, kind="Internal").ap()
    io["s_xn2"] = nc.dram_tensor("s_xn2", [D, T], BF16, kind="Internal").ap()
    io["s_r1o"] = nc.dram_tensor("s_r1o", [D, TO], F32, kind="Internal").ap()
    io["s_xn3"] = nc.dram_tensor("s_xn3", [D, TO], BF16, kind="Internal").ap()
    io["kv_loc"] = nc.dram_tensor("kv_loc", [1536, 1024], BF16, kind="Internal").ap()
    io["kv_glob"] = nc.dram_tensor("kv_glob", [3072, 1024], BF16, kind="Internal").ap()
    if debug == "dequant":
        io["dbg_dq"] = nc.dram_tensor("dbg_dq", [49152, 2048], BF16, kind="ExternalOutput").ap()
    if debug == "ffn1":
        io["dbg_xn2"] = nc.dram_tensor("dbg_xn2", [D, T], BF16, kind="ExternalOutput").ap()
        io["dbg_r1o"] = nc.dram_tensor("dbg_r1o", [D, TO], F32, kind="ExternalOutput").ap()
    if debug == "attn":
        io["dbg_res2"] = nc.dram_tensor("dbg_res2", [D, TO], F32, kind="ExternalOutput").ap()
    if debug == "ln1":
        io["dbg_xn1"] = nc.dram_tensor("dbg_xn1", [D, T], BF16, kind="ExternalOutput").ap()
        io["dbg_mbc"] = nc.dram_tensor("dbg_mbc", [128, T], F32, kind="ExternalOutput").ap()
        io["dbg_rbc"] = nc.dram_tensor("dbg_rbc", [128, T], F32, kind="ExternalOutput").ap()

    es = ExitStack()
    block = es.enter_context(nc.Block())
    b = Build(nc, es, block)
    sbc = lambda nm, shape, dt=F32: es.enter_context(nc.sbuf_tensor(nm, shape, dt))
    consts = [("s_ident", "ident", [128, 128], BF16), ("s_ones_col", "ones_col", [128, 1], F32),
              ("s_ones_row", "ones_row", [1, 128], F32), ("s_onesbf", "ones_row_bf", [1, 128], BF16),
              ("s_emat", "emat", [8, 128], F32),
              ("s_emat_h", "emat_h", [8, 128], F32), ("s_gqt", "gqt", [128, 16], F32),
              ("s_gkt", "gkt", [128, 4], F32), ("s_mask_a", "mask_a", [128, 128], F32),
              ("s_mask_b", "mask_b", [128, 128], F32), ("s_rcos", "rcos", [128, T], F32),
              ("s_rsin", "rsin", [128, T], F32), ("s_sswap", "sswap", [128, 128], BF16)]
    for snm, inm, shape, dt in consts:
        io[snm] = sbc(snm, shape, dt)
    io["PS"] = [es.enter_context(nc.psum_tensor(f"PS{i}", [128, 512], F32)) for i in range(8)]

    b.sem("dX")

    def f_const(q):
        for snm, inm, shape, dt in consts:
            b.dma(q, io[snm][:, :], io[inm][:, :], "dX")
    block.sync(f_const)
    b.barrier()

    _emit_dequant(b, io)

    esL = ExitStack()
    xarena = esL.enter_context(nc.sbuf_tensor("x_ln1", [128, HC * 1024], F32))

    def f_ldx(q):
        for c in range(HC):
            b.dma(q, xarena[:, c * 1024:(c + 1) * 1024], io["x"][c * 128:(c + 1) * 128, :], "dX")
    block.sync(f_ldx)
    b.barrier()
    _emit_ln_sbuf(b, io, {}, xarena, io["s_xn1"], None, LN_EPS)
    esL.close()

    if debug == "dequant":
        def f_dbg(q):
            q.wait_ge(b.sems["CC"], 6)
            for fi, fam in enumerate(("d0", "d1")):
                for blk in range(4):
                    b.dma(q, io["dbg_dq"][(fi * 4 + blk) * 2048:(fi * 4 + blk + 1) * 2048, :],
                          io["dqg_" + fam][blk * 2048:(blk + 1) * 2048, :], "dX")
            q.wait_ge(b.sems["dX"], b.cnt["dX"])
        block.sync(f_dbg)
        es.close()
        return nc

    if debug == "ln1":
        def f_dbg(q):
            b.dma(q, io["dbg_xn1"][:, :], io["s_xn1"][:, :], "dX")
            q.wait_ge(b.sems["dX"], b.cnt["dX"])
        block.sync(f_dbg)
        es.close()
        return nc

    esF = ExitStack()
    sbF = lambda nm, shape, dt=F32: esF.enter_context(nc.sbuf_tensor(nm, shape, dt))
    acc_a = sbF("acc0", [128, HC * 1024], F32)
    esT = ExitStack()
    sbT = lambda nm, shape, dt=F32: esT.enter_context(nc.sbuf_tensor(nm, shape, dt))
    ar = {
        "xn": sbT("xn0", [128, HC * 1024], BF16),
        "acc": acc_a,
        "inter": sbT("int0", [128, 16 * 1024], BF16),
        "gbuf": [sbT(f"gb0{i}", [128, 2048], BF16) for i in range(2)],
        "ubuf": [sbT(f"ub0{i}", [128, 2048], BF16) for i in range(2)],
        "dbuf": [sbT(f"db0{i}", [128, 512], BF16) for i in range(8)],
        "gf": sbT("gf0", [128, 1024], F32),
        "uf": sbT("uf0", [128, 1024], F32),
        "At": sbT("At0", [128, 1024], F32),
        "Bt": sbT("Bt0", [128, 1024], F32),
        "Ct": sbT("Ct0", [128, 1024], F32),
    }
    _emit_ffn_tblock(b, io, ar, io["s_xn1"], io["x"])
    esT.close()
    _emit_ln_sbuf(b, io, {}, acc_a, io["s_xn2"], io["s_r1o"], LN_EPS)
    esF.close()

    if debug == "ffn1":
        def f_dbg(q):
            b.dma(q, io["dbg_xn2"][:, :], io["s_xn2"][:, :], "dX")
            b.dma(q, io["dbg_r1o"][:, :], io["s_r1o"][:, :], "dX")
            q.wait_ge(b.sems["dX"], b.cnt["dX"])
        block.sync(f_dbg)
        es.close()
        return nc

    esA = ExitStack()
    sbA = lambda nm, shape, dt=BF16: esA.enter_context(nc.sbuf_tensor(nm, shape, dt))
    K_sb = sbA("K_sb", [128, NG * 2048])
    V_sb = sbA("V_sb", [128, 16 * 512])
    Qn_sb = sbA("Qn_sb", [128, NHEAD * 1024])
    esP = ExitStack()
    Qr_sb = esP.enter_context(nc.sbuf_tensor("Qr_sb", [128, NHEAD * 1024], BF16))
    K_own = esP.enter_context(nc.sbuf_tensor("K_own", [128, NG * 1024], BF16))
    V_own = esP.enter_context(nc.sbuf_tensor("V_own", [128, 8 * 512], BF16))
    esP2 = ExitStack()
    xn2_sb = esP2.enter_context(nc.sbuf_tensor("xn2_sb", [128, HC * 1024], BF16))
    _emit_proj(b, io, K_own, V_own, Qr_sb, xn2_sb)
    esP2.close()
    k_done = _emit_rms_rope(b, io, K_own, Qr_sb, Qn_sb)
    _emit_kv_exchange(b, io, K_own, V_own, K_sb, V_sb, k_done)
    esP.close()
    esR2 = ExitStack()
    res2 = esR2.enter_context(nc.sbuf_tensor("res2", [128, HC * 1024], F32, side="right"))
    _emit_attn_wo(b, io, K_sb, V_sb, Qn_sb, res2)
    esA.close()

    if debug == "attn":
        def f_dbg(q):
            for c in range(HC):
                b.dma(q, io["dbg_res2"][c * 128:(c + 1) * 128, :], res2[:, c * 1024:(c + 1) * 1024], "dX")
            q.wait_ge(b.sems["dX"], b.cnt["dX"])
        block.sync(f_dbg)
        es.close()
        return nc

    _emit_ln_sbuf(b, io, {}, res2, io["s_xn3"], None, LN_EPS)

    esF3 = ExitStack()
    sb3 = lambda nm, shape, dt=F32: esF3.enter_context(nc.sbuf_tensor(nm, shape, dt))
    ar3 = {
        "xn": sb3("xn3", [128, HC * 1024], BF16),
        "acc": res2,
        "inter": sb3("int3", [128, 16 * 1024], BF16),
        "gbuf": [sb3(f"gb3{i}", [128, 2048], BF16) for i in range(2)],
        "ubuf": [sb3(f"ub3{i}", [128, 2048], BF16) for i in range(2)],
        "dbuf": [sb3(f"db3{i}", [128, 512], BF16) for i in range(8)],
        "gf": sb3("gf3", [128, 1024], F32),
        "uf": sb3("uf3", [128, 1024], F32),
        "At": sb3("At3", [128, 1024], F32),
        "Bt": sb3("Bt3", [128, 1024], F32),
        "Ct": sb3("Ct3", [128, 1024], F32),
    }
    _emit_ffn_tblock(b, io, ar3, io["s_xn3"], None)
    esF3.close()

    def f_out(q):
        for c in range(HC):
            b.dma(q, io["y"][c * 128:(c + 1) * 128, :], res2[:, c * 1024:(c + 1) * 1024], "dX")
        q.wait_ge(b.sems["dX"], b.cnt["dX"])
    block.sync(f_out)
    esR2.close()

    es.close()
    return nc


_CACHE = {}


def kernel(**inputs):
    maps = _host_inputs(inputs)
    if "nc" not in _CACHE:
        _CACHE["nc"] = _build()
    nc = _CACHE["nc"]
    res = run_bass_kernel_spmd(nc, maps, core_ids=list(range(8)))
    return _host_output(res.results)
